# version 10
# speedup vs baseline: 1.1514x; 1.0292x over previous
import contextlib
import numpy as np
import concourse.bass as bass
import concourse.mybir as mybir
from concourse.bass_utils import run_bass_kernel_spmd

F32 = mybir.dt.float32
BF16 = mybir.dt.bfloat16
AF = mybir.ActivationFunctionType
ALU = mybir.AluOpType
AX = mybir.AxisListType

NT = 16
L0A_PH = 3
DBG = False
L0A_SUB = 4
TOK = 2048
D = 1024
NDM = 8
FMW = 2064
FMO = 16


class Sem:
    def __init__(self, h):
        self.h = h
        self.v = 0

    def inc(self, ins, n=1):
        ins.then_inc(self.h, n)
        self.v += n
        return self.v


def build(stages=("l0a", "l0f", "l1a", "l1f"), race=True):
    nc = bass.Bass("TRN2", target_bir_lowering=False,
                   detect_race_conditions=race)

    x_in = nc.dram_tensor("x_in", [TOK, D], F32, kind="ExternalInput").ap()
    out = nc.dram_tensor("out", [TOK, D], F32, kind="ExternalOutput").ap()

    def win(name, shape):
        return nc.dram_tensor(name, list(shape), F32, kind="ExternalInput").ap()

    w_qkv = win("w_qkv", (D, 3 * D))
    b_qkv = win("b_qkv", (1, 3 * D))
    w_out = win("w_out", (D, D))
    b_out = win("b_out", (1, D))
    w_router_t = win("w_router_t", (D, 8))
    wg_t = win("wg_t", (8, D, 2816))
    wu_t = win("wu_t", (8, D, 2816))
    wd_t = win("wd_t", (8, 2816, D))
    ww_t = win("ww_t", (D, D))
    wr_t = win("wr_t", (D, D))
    decay = win("decay", (1, 1))
    log_alpha = win("log_alpha", (1, 1))
    enc_w = win("enc_w", (D, 2048))
    enc_b = win("enc_b", (1, 2048))
    s2c_w = win("s2c_w", (2048, 64))
    s2c_b = win("s2c_b", (1, 64))
    r_w1 = win("r_w1", (64, 64))
    r_b1 = win("r_b1", (1, 64))
    r_w2 = win("r_w2", (64, 8))
    r_b2 = win("r_b2", (1, 8))
    e_w1 = win("e_w1", (8, 64, D))
    e_b1 = win("e_b1", (8, D))
    e_w2 = win("e_w2", (8, D, 64))
    e_b2 = win("e_b2", (1, 8 * 64))
    c2s_w = win("c2s_w", (64, 2048))
    c2s_b = win("c2s_b", (1, 2048))
    dec_w = win("dec_w", (2048, D))
    dec_b = win("dec_b", (1, D))
    ln_g = win("ln_g", (1, D))
    ln_b = win("ln_b", (1, D))

    _sk = "ExternalOutput" if DBG else "Internal"
    if DBG:
        dbg_dram = nc.dram_tensor("dbg_dram", [128, 16384], BF16,
                                  kind="ExternalOutput").ap()
        dbg2_dram = nc.dram_tensor("dbg2_dram", [128, 512], F32,
                                   kind="ExternalOutput").ap()
    xbf_dram = nc.dram_tensor("xbf_dram", [TOK, D], BF16, kind=_sk).ap()
    aux_dram = nc.dram_tensor("aux_dram", [TOK, D], BF16, kind=_sk).ap()
    aux2_dram = nc.dram_tensor("aux2_dram", [TOK, D], BF16, kind=_sk).ap()

    with contextlib.ExitStack() as stack:
        ec = stack.enter_context
        x_tm = ec(nc.sbuf_tensor("x_tm", [128, NT * D], F32))
        x_fm = ec(nc.sbuf_tensor("x_fm", [128, NDM * FMW], BF16))
        rows = ec(nc.sbuf_tensor("rows", [128, 9216], BF16))
        mbf = ec(nc.sbuf_tensor("mbf", [128, 2432], BF16))
        mf32 = ec(nc.sbuf_tensor("mf32", [128, 256], F32))
        onesf = ec(nc.sbuf_tensor("onesf", [1, 128], F32))
        pb = ec(nc.psum_tensor("pb", [128, 8, 512], F32))
        sIN, sW, sTP, sPE, sVE, sSC, sGP, sST, sAX = (
            Sem(ec(nc.semaphore(n))) for n in
            ("sIN", "sW", "sTP", "sPE", "sVE", "sSC", "sGP", "sST", "sAX"))
        sD = [Sem(ec(nc.semaphore(f"sD{i}"))) for i in range(6)]
        sG = [Sem(ec(nc.semaphore(f"sG{i}"))) for i in range(6)]

        ident = mbf[:, 0:128]
        m_t_tile = mbf[:, 128:256]
        gam_bc = mbf[:, 256:384]
        lng_bc = mbf[:, 384:1408]
        lnb_bc = mbf[:, 1408:2432]
        ones_row = rows[0:1, 8960:9088]

        ln_sum = mf32[:, 64:80]
        ln_ssq = mf32[:, 80:96]
        ln_mean = mf32[:, 96:112]
        ln_var = mf32[:, 112:128]
        ln_ms = mf32[:, 128:144]
        sm_max = mf32[:, 144:160]
        sm_den = mf32[:, 160:176]
        c_e6 = mf32[:, 176:177]
        c_e5 = mf32[:, 177:178]
        c_t0 = mf32[:, 38:39]
        c_t1 = mf32[:, 39:40]
        c_lg = mf32[:, 40:41]
        c_gw = mf32[:, 42:43]
        c_gc = mf32[:, 43:44]
        c_al = mf32[:, 44:45]

        def xt(t):
            return x_tm[:, t * D:(t + 1) * D]

        def fm(c, t0, n):
            return x_fm[:, c * FMW + FMO + t0: c * FMW + FMO + t0 + n]

        def ln_scalar_pass(s, tiles, tmp1024, lnSC, lnVE, gated=False):
            s.memzero(ln_ssq[:, 0:len(tiles)])
            for j, (i, ap) in enumerate(tiles):
                s.activation(tmp1024, ap, AF.Square, accum_out=ln_ssq[:, j:j + 1])
            sSC.inc(s.drain())
            s.wait_ge(sVE.h, lnVE + 1)
            s.activation(ln_var[:, 0:len(tiles)], ln_var[:, 0:len(tiles)],
                         AF.Ln, bias=c_e5)
            s.activation(ln_var[:, 0:len(tiles)], ln_var[:, 0:len(tiles)],
                         AF.Exp, scale=-0.5)
            sSC.inc(s.drain())

        def ln_vector_pass(v, tiles, lnSC, lnVE, gated=False):
            n = len(tiles)
            for j, (i, ap) in enumerate(tiles):
                v.tensor_reduce(ln_sum[:, j:j + 1], ap, AX.X, ALU.add)
            v.drain()
            v.wait_ge(sSC.h, lnSC + 1)
            v.tensor_scalar(ln_mean[:, 0:n], ln_sum[:, 0:n], 1.0 / 1024.0, None, ALU.mult)
            v.tensor_scalar(ln_ssq[:, 0:n], ln_ssq[:, 0:n], 1.0 / 1024.0, None, ALU.mult)
            v.drain()
            v.tensor_mul(ln_var[:, 0:n], ln_mean[:, 0:n], ln_mean[:, 0:n])
            v.drain()
            v.tensor_sub(ln_var[:, 0:n], ln_ssq[:, 0:n], ln_var[:, 0:n])
            sVE.inc(v.drain())
            v.wait_ge(sSC.h, lnSC + 2)
            v.tensor_mul(ln_ms[:, 0:n], ln_mean[:, 0:n], ln_var[:, 0:n])
            v.drain()
            for j, (i, ap) in enumerate(tiles):
                v.tensor_scalar(ap, ap, ln_var[:, j:j + 1], ln_ms[:, j:j + 1],
                                ALU.mult, ALU.subtract)
                if gated:
                    v.tensor_tensor(ap, ap, lng_bc, ALU.mult)
                    v.tensor_tensor(ap, ap, lnb_bc, ALU.add)
            sVE.inc(v.drain())

        def xfm_rebuild(tag):
            with (
                nc.sbuf_tensor(f"xr_{tag}", [128, NT * D], BF16) as xr,
                nc.Block(f"xfm_{tag}") as blk,
            ):
                @blk.vector
                def _(v):
                    for t in range(NT):
                        ins = v.tensor_copy(xr[:, t * D:(t + 1) * D], xt(t))
                    sVE.inc(ins)

                @blk.sync
                def _(sy):
                    sy.wait_ge(sVE.h, sVE.v)
                    ins = sy.dma_start(
                        out=xbf_dram.rearrange("(t p) d -> p t d", p=128),
                        in_=xr[:].rearrange("p (t d) -> p t d", d=D))
                    sIN.inc(ins, 16)
                    sy.wait_ge(sIN.h, sIN.v)
                    for c in range(NDM):
                        ins = sy.dma_start_transpose(
                            x_fm[:, c * FMW + FMO:c * FMW + FMO + TOK],
                            xbf_dram[:, c * 128:(c + 1) * 128])
                        sTP.inc(ins, 16)
                    sy.wait_ge(sTP.h, sTP.v)

        with nc.Block("pro") as blk:
            bW, bGP = sW.v, sGP.v

            @blk.sync
            def _(sy):
                ins = sy.dma_start(
                    out=x_tm[:].rearrange("p (t d) -> p t d", d=D),
                    in_=x_in.rearrange("(t p) d -> p t d", p=128))
                sIN.inc(ins, 16)
                sy.wait_ge(sIN.h, sIN.v)

            @blk.gpsimd
            def _(g):
                ins = g.iota(mf32[:, 0:128], [[1, 128]], base=0,
                             channel_multiplier=-1,
                             allow_small_or_imprecise_dtypes=True)
                sGP.inc(ins)
                ins = g.dma_start(out=rows[0:1, 0:3072], in_=b_qkv)
                sW.inc(ins, 16)
                ins = g.dma_start(out=rows[0:1, 3072:4096], in_=b_out)
                sW.inc(ins, 16)
                g.wait_ge(sW.h, bW + 32)

            @blk.vector
            def _(v):
                v.memset(ones_row, 1.0)
                v.memset(onesf[:], 1.0)
                v.memset(c_e6, 1e-6)
                v.memset(c_e5, 1e-5)
                for c in range(NDM):
                    v.memset(x_fm[:, c * FMW:c * FMW + FMO], 0.0)
                v.wait_ge(sGP.h, bGP + 1)
                v.tensor_single_scalar(ident, mf32[:, 0:128], 0.0, ALU.is_equal)

        def emit_l0a():
            xfm_rebuild("l0a")
            NG = 4
            GT = NT // NG
            with (
                nc.sbuf_tensor("dist_bf", [128, NT * 256], BF16) as dist_bf,
                nc.sbuf_tensor("ksqm_bf", [128, NT * 256], BF16) as ksqm_bf,
                nc.sbuf_tensor("cos_bf", [128, NT * 256], BF16) as cos_bf,
                nc.sbuf_tensor("qsq_all", [128, 256], F32) as qsq_all,
                nc.sbuf_tensor("mq_bf", [128, 256], BF16) as mq_bf,
                nc.sbuf_tensor("tmp256", [128, 256], F32) as tmp256,
                nc.sbuf_tensor("stmp256", [128, 256], F32) as stmp256,
                nc.sbuf_tensor("qk256", [128, 256], F32) as qk256,
                nc.sbuf_tensor("pr_t", [128, 1024], BF16) as pr_t,
            ):
                with nc.sbuf_tensor("qkv_g", [128, GT * 3072], BF16) as qkv_g, \
                     nc.sbuf_tensor("wq_s", [128, 2 * 4096], BF16) as wq_s:
                    with nc.Block("l0a_g") as blk:
                        bSC, bPE, bW, bAX = sSC.v, sPE.v, sW.v, sAX.v
                        bVE = sVE.v
                        bD0, bD1 = sD[0].v, sD[1].v
                        bDs = (bD0, bD1)

                        @blk.gpsimd
                        def _(gp):
                            for i in range(4 * 6):
                                if i >= 2:
                                    pg, pp = (i - 2) // 6, (i - 2) % 6
                                    gp.wait_ge(sPE.h, bPE + pg * 24 + 4 * (pp + 1))
                                p = i % 6
                                src = w_qkv.rearrange("(c p) n -> p c n", p=128)[
                                    :, :, 512 * p:512 * (p + 1)]
                                dst = wq_s[:, (i % 2) * 4096:(i % 2) * 4096 + 4096]
                                ins = gp.dma_start(
                                    out=dst.rearrange("p (c n) -> p c n", n=512),
                                    in_=src)
                                sD[i % 2].inc(ins, 16)

                        @blk.tensor
                        def _(t_):
                            for u in range(4 * 24):
                                g, ul = u // 24, u % 24
                                p, t = ul // 4, ul % 4
                                i = g * 6 + p
                                t_.wait_ge(sD[i % 2].h, bDs[i % 2] + 16 * (i // 2 + 1))
                                if u >= 8:
                                    t_.wait_ge(sSC.h, bSC + u - 7)
                                bank = pb[:, u % 8, :]
                                wsl = wq_s[:, (i % 2) * 4096:(i % 2) * 4096 + 4096]
                                for dm in range(NDM):
                                    t_.matmul(bank,
                                              fm(dm, (g * GT + t) * 128, 128),
                                              wsl[:, dm * 512:(dm + 1) * 512],
                                              start=(dm == 0), stop=False)
                                ins = t_.matmul(bank, ones_row,
                                                rows[0:1, 512 * p:512 * (p + 1)],
                                                start=False, stop=True)
                                sPE.inc(ins)

                        @blk.scalar
                        def _(s):
                            for u in range(4 * 24):
                                g, ul = u // 24, u % 24
                                p, t = ul // 4, ul % 4
                                s.wait_ge(sPE.h, bPE + u + 1)
                                if g >= 1:
                                    s.wait_ge(sVE.h, bVE + (g - 1) * 4 + t + 1)
                                    s.wait_ge(sAX.h, bAX + 16 * ((g - 1) * 4 + t + 1))
                                ins = s.copy(
                                    qkv_g[:, t * 3072 + 512 * p:t * 3072 + 512 * (p + 1)],
                                    pb[:, u % 8, :])
                                sSC.inc(ins)

                        @blk.sync
                        def _(sy):
                            for tg in range(16):
                                g, t = tg // 4, tg % 4
                                sy.wait_ge(sSC.h, bSC + g * 24 + 21 + t)
                                ins = sy.dma_start(
                                    out=aux_dram[tg * 128:(tg + 1) * 128, :],
                                    in_=qkv_g[:, t * 3072 + 2048:t * 3072 + 3072])
                                sAX.inc(ins, 16)

                        @blk.vector
                        def _(v):
                            for tg in range(16):
                                g, t = tg // 4, tg % 4
                                tile = tg
                                v.wait_ge(sSC.h, bSC + g * 24 + 13 + t)
                                qb = qkv_g[:, t * 3072:t * 3072 + 1024]
                                kb = qkv_g[:, t * 3072 + 1024:t * 3072 + 2048]
                                v.tensor_mul(pr_t[:], qb, qb)
                                v.tensor_reduce(
                                    qsq_all[:, tile * 16:tile * 16 + 16],
                                    pr_t[:].rearrange("p (h d) -> p h d", d=64),
                                    AX.X, ALU.add)
                                v.tensor_mul(pr_t[:], kb, kb)
                                v.tensor_reduce(
                                    mf32[:, 0:16],
                                    pr_t[:].rearrange("p (h d) -> p h d", d=64),
                                    AX.X, ALU.add)
                                v.drain()
                                ksq = ksqm_bf[:, tile * 256:(tile + 1) * 256]
                                for r in range(16):
                                    v.tensor_copy(ksq[:, r * 16:r * 16 + 16 - r],
                                                  mf32[:, r:16])
                                    if r > 0:
                                        v.tensor_copy(ksq[:, r * 16 + 16 - r:(r + 1) * 16],
                                                      mf32[:, 0:r])
                                for r in range(16):
                                    seg = (16 - r) * 64
                                    v.tensor_mul(pr_t[:, 0:seg], qb[:, 0:seg],
                                                 kb[:, r * 64:1024])
                                    if r > 0:
                                        v.tensor_mul(pr_t[:, seg:1024],
                                                     qb[:, seg:1024],
                                                     kb[:, 0:r * 64])
                                    v.tensor_reduce(
                                        qk256[:, r * 16:(r + 1) * 16],
                                        pr_t[:].rearrange("p (h d) -> p h d", d=64),
                                        AX.X, ALU.add)
                                qsq_bc = qsq_all[:, tile * 16:tile * 16 + 16] \
                                    .unsqueeze(1).broadcast_to([128, 16, 16])
                                v.tensor_copy(tmp256[:], ksq)
                                v.tensor_tensor(
                                    tmp256[:].rearrange("p (r h) -> p r h", h=16),
                                    tmp256[:].rearrange("p (r h) -> p r h", h=16),
                                    qsq_bc, ALU.add)
                                v.scalar_tensor_tensor(tmp256[:], qk256[:], -2.0,
                                                       tmp256[:], ALU.mult, ALU.add)
                                v.tensor_scalar(tmp256[:], tmp256[:], 0.0, None,
                                                ALU.max)
                                ins = v.tensor_copy(
                                    dist_bf[:, tile * 256:(tile + 1) * 256], tmp256[:])
                                sVE.inc(ins)

                if DBG:
                    with nc.Block("l0a_dbg0") as blk:
                        @blk.sync
                        def _(sy):
                            ins = sy.dma_start(out=dbg_dram[:, 0:4096], in_=dist_bf[:])
                            sAX.inc(ins, 16)
                            ins = sy.dma_start(out=dbg2_dram[:, 256:512], in_=qk256[:])
                            sAX.inc(ins, 16)
                            ins = sy.dma_start(out=dbg_dram[:, 12544:16384],
                                               in_=ksqm_bf[:, 0:3840])
                            sAX.inc(ins, 16)
                            sy.wait_ge(sAX.h, sAX.v)

                if L0A_PH < 2:
                    return
                with nc.sbuf_tensor("vb_t", [128, 2 * 1024], BF16) as vb_t, \
                     nc.sbuf_tensor("ao_t", [128, 2 * 1024], BF16) as ao_t, \
                     nc.Block("l0a_t") as blk:
                    bSC, bVE, bAX, bIN = sSC.v, sVE.v, sAX.v, sIN.v
                    bD2, bD3, bD4, bD5 = sD[2].v, sD[3].v, sD[4].v, sD[5].v
                    bDp = (bD2, bD3)
                    bDa = (bD4, bD5)

                    @blk.scalar
                    def _(s):
                        for t in range(NT):
                            qs = qsq_all[:, t * 16:t * 16 + 16]
                            s.activation(mf32[:, 16:32], qs, AF.Ln)
                            s.activation(mq_bf[:, t * 16:t * 16 + 16],
                                         mf32[:, 16:32], AF.Exp, scale=0.5)
                            ks = ksqm_bf[:, t * 256:(t + 1) * 256]
                            s.activation(stmp256[:], ks, AF.Ln)
                            s.activation(ks, stmp256[:], AF.Exp, scale=0.5)
                            db = dist_bf[:, t * 256:(t + 1) * 256]
                            s.activation(stmp256[:], db, AF.Ln, bias=c_e6)
                            ins = s.activation(cos_bf[:, t * 256:(t + 1) * 256],
                                               stmp256[:], AF.Exp, scale=0.5)
                            sSC.inc(ins)
                        for t in range(NT):
                            cb = cos_bf[:, t * 256:(t + 1) * 256]
                            ins = s.activation(cb, cb, AF.Sin, scale=1.0 / 32.0)
                            sSC.inc(ins)
                        if L0A_SUB < 2:
                            return
                        for t in range(NT):
                            s.wait_ge(sVE.h, bVE + t + 1)
                            cb = cos_bf[:, t * 256:(t + 1) * 256]
                            ins = s.activation(cb, cb, AF.Exp)
                            sSC.inc(ins)

                    @blk.vector
                    def _(v):
                        for t in range(NT):
                            v.wait_ge(sSC.h, bSC + NT + t + 1)
                            cb = cos_bf[:, t * 256:(t + 1) * 256]
                            db = dist_bf[:, t * 256:(t + 1) * 256]
                            mk = ksqm_bf[:, t * 256:(t + 1) * 256]
                            v.tensor_mul(tmp256[:], cb, cb)
                            v.tensor_scalar(tmp256[:], tmp256[:], -2.0, 1.0,
                                            ALU.mult, ALU.add)
                            for _i in range(4):
                                v.tensor_mul(qk256[:], tmp256[:], tmp256[:])
                                v.tensor_scalar(tmp256[:], qk256[:], 2.0, -1.0,
                                                ALU.mult, ALU.add)
                            v.tensor_scalar(tmp256[:], tmp256[:], 0.15, 1.0,
                                            ALU.mult, ALU.add)
                            v.tensor_copy(qk256[:], db)
                            v.tensor_mul(qk256[:], qk256[:], tmp256[:])
                            v.tensor_scalar(qk256[:], qk256[:], 1e-6, None, ALU.add)
                            v.reciprocal(tmp256[:], qk256[:])
                            mq_bc = mq_bf[:, t * 16:t * 16 + 16] \
                                .unsqueeze(1).broadcast_to([128, 16, 16])
                            v.tensor_tensor(
                                tmp256[:].rearrange("p (r h) -> p r h", h=16),
                                tmp256[:].rearrange("p (r h) -> p r h", h=16),
                                mq_bc, ALU.mult)
                            v.tensor_copy(qk256[:], mk)
                            v.tensor_mul(tmp256[:], tmp256[:], qk256[:])
                            fv = tmp256[:].rearrange("p (r h) -> p h r", h=16)
                            v.tensor_reduce(sm_max[:], fv, AX.X, ALU.max)
                            v.drain()
                            mx_bc = sm_max[:].unsqueeze(1).broadcast_to([128, 16, 16])
                            v.tensor_tensor(
                                tmp256[:].rearrange("p (r h) -> p r h", h=16),
                                tmp256[:].rearrange("p (r h) -> p r h", h=16),
                                mx_bc, ALU.subtract)
                            ins = v.tensor_copy(cos_bf[:, t * 256:(t + 1) * 256],
                                                tmp256[:])
                            sVE.inc(ins)
                        if L0A_SUB < 2:
                            return
                        for t in range(NT):
                            v.wait_ge(sSC.h, bSC + 2 * NT + t + 1)
                            v.wait_ge(sD[2 + t % 2].h, bDp[t % 2] + 16 * (t // 2 + 1))
                            cb = cos_bf[:, t * 256:(t + 1) * 256]
                            nv = cb.rearrange("p (r h) -> p h r", h=16)
                            v.tensor_reduce(sm_den[:], nv, AX.X, ALU.add)
                            v.drain()
                            v.reciprocal(sm_den[:], sm_den[:])
                            v.drain()
                            dn_bc = sm_den[:].unsqueeze(1).broadcast_to([128, 16, 16])
                            if t >= 2:
                                v.wait_ge(sD[4 + t % 2].h, bDa[t % 2] + 16 * (t // 2))
                            at = ksqm_bf[:, t * 256:(t + 1) * 256]
                            v.tensor_copy(tmp256[:], cb)
                            v.tensor_tensor(
                                tmp256[:].rearrange("p (r h) -> p r h", h=16),
                                tmp256[:].rearrange("p (r h) -> p r h", h=16),
                                dn_bc, ALU.mult)
                            v.tensor_copy(at, tmp256[:])
                            vb = vb_t[:, (t % 2) * 1024:(t % 2) * 1024 + 1024]
                            ao = ao_t[:, (t % 2) * 1024:(t % 2) * 1024 + 1024]
                            for r in range(16):
                                seg = (16 - r) * 64
                                dst = ao if r == 0 else pr_t
                                a1 = at[:, r * 16:r * 16 + (16 - r)] \
                                    .unsqueeze(2).broadcast_to([128, 16 - r, 64])
                                v.tensor_tensor(
                                    dst[:, 0:seg].rearrange("p (h d) -> p h d", d=64),
                                    vb[:, r * 64:1024].rearrange("p (h d) -> p h d", d=64),
                                    a1, ALU.mult)
                                if r > 0:
                                    a2 = at[:, r * 16 + 16 - r:(r + 1) * 16] \
                                        .unsqueeze(2).broadcast_to([128, r, 64])
                                    v.tensor_tensor(
                                        dst[:, seg:1024].rearrange("p (h d) -> p h d", d=64),
                                        vb[:, 0:r * 64].rearrange("p (h d) -> p h d", d=64),
                                        a2, ALU.mult)
                                    ins = v.tensor_add(ao, ao, pr_t[:])
                            sVE.inc(ins)

                    @blk.sync
                    def _(sy):
                        if L0A_SUB < 2:
                            return
                        sy.wait_ge(sAX.h, bAX)
                        for t in range(NT + 2):
                            if t >= 2:
                                sy.wait_ge(sVE.h, bVE + NT + (t - 1))
                                ins = sy.dma_start(
                                    out=aux2_dram[(t - 2) * 128:(t - 1) * 128, :],
                                    in_=ao_t[:, (t % 2) * 1024:(t % 2) * 1024 + 1024])
                                sD[4 + t % 2].inc(ins, 16)
                            if t < NT:
                                ins = sy.dma_start(
                                    out=vb_t[:, (t % 2) * 1024:(t % 2) * 1024 + 1024],
                                    in_=aux_dram[t * 128:(t + 1) * 128, :])
                                sD[2 + t % 2].inc(ins, 16)
                        sy.wait_ge(sD[4].h, bD4 + 16 * 8)
                        sy.wait_ge(sD[5].h, bD5 + 16 * 8)

                if DBG:
                    with nc.Block("l0a_dbg") as blk:
                        @blk.sync
                        def _(sy):
                            ins = sy.dma_start(out=dbg_dram[:, 4096:8192], in_=ksqm_bf[:])
                            sAX.inc(ins, 16)
                            ins = sy.dma_start(out=dbg_dram[:, 8192:12288], in_=cos_bf[:])
                            sAX.inc(ins, 16)
                            ins = sy.dma_start(out=dbg_dram[:, 12288:12544], in_=mq_bf[:])
                            sAX.inc(ins, 16)
                            ins = sy.dma_start(out=dbg2_dram[:, 0:256], in_=qsq_all[:])
                            sAX.inc(ins, 16)
                            sy.wait_ge(sAX.h, sAX.v)

                if L0A_PH < 3:
                    return
                with nc.sbuf_tensor("afm", [128, 8 * 1024], BF16) as afm, \
                     nc.sbuf_tensor("wo_bf", [128, 8 * 1024], BF16) as wo_bf, \
                     nc.sbuf_tensor("lntmp", [128, 1024], F32) as lntmp, \
                     nc.Block("l0a_e") as blk:
                    bSC, bVE, bPE, bTP, bW = sSC.v, sVE.v, sPE.v, sTP.v, sW.v

                    @blk.sync
                    def _(sy):
                        for h in range(2):
                            if h == 1:
                                sy.wait_ge(sPE.h, bPE + 16)
                            for c in range(NDM):
                                ins = sy.dma_start_transpose(
                                    afm[:, c * 1024:(c + 1) * 1024],
                                    aux2_dram[h * 1024:(h + 1) * 1024,
                                              c * 128:(c + 1) * 128])
                                sTP.inc(ins, 16)

                    @blk.gpsimd
                    def _(gp):
                        ins = gp.dma_start(
                            out=wo_bf[:].rearrange("p (c n) -> p c n", n=1024),
                            in_=w_out.rearrange("(c p) n -> p c n", p=128))
                        sW.inc(ins, 16)

                    @blk.tensor
                    def _(t_):
                        t_.wait_ge(sW.h, bW + 16)
                        for u in range(2 * NT):
                            t, p = u // 2, u % 2
                            h, tl = t // 8, t % 8
                            t_.wait_ge(sTP.h, bTP + 128 * (h + 1))
                            if u >= 8:
                                t_.wait_ge(sVE.h, bVE + u - 7)
                            bank = pb[:, u % 8, :]
                            for c in range(NDM):
                                t_.matmul(bank,
                                          afm[:, c * 1024 + tl * 128:c * 1024 + (tl + 1) * 128],
                                          wo_bf[:, c * 1024 + p * 512:c * 1024 + (p + 1) * 512],
                                          start=(c == 0), stop=False)
                            ins = t_.matmul(bank, ones_row,
                                            rows[0:1, 3072 + 512 * p:3072 + 512 * (p + 1)],
                                            start=False, stop=True)
                            sPE.inc(ins)

                    @blk.vector
                    def _(v):
                        for u in range(2 * NT):
                            t, p = u // 2, u % 2
                            v.wait_ge(sPE.h, bPE + u + 1)
                            sl = x_tm[:, t * D + p * 512:t * D + (p + 1) * 512]
                            ins = v.tensor_add(sl, sl, pb[:, u % 8, :])
                            sVE.inc(ins)
                        if L0A_SUB >= 4:
                            ln_vector_pass(v, [(t, xt(t)) for t in range(NT)],
                                           bSC, bVE + 2 * NT)

                    @blk.scalar
                    def _(s):
                        if L0A_SUB >= 4:
                            s.wait_ge(sVE.h, bVE + 2 * NT)
                            ln_scalar_pass(s, [(t, xt(t)) for t in range(NT)], lntmp[:],
                                           bSC, bVE + 2 * NT)

        def emit_l0f():
            xfm_rebuild("l0f")
            NI = 22
            with (
                nc.sbuf_tensor("h_fm", [128, NI * 1024], BF16) as h_fm,
                nc.sbuf_tensor("wstr", [128, 14336], BF16) as wstr,
                nc.sbuf_tensor("gate_s", [128, 2 * 1024], BF16) as gate_s,
                nc.sbuf_tensor("comb_t", [128, 128], F32) as comb_t,
                nc.sbuf_tensor("wrt_bf", [128, 64], BF16) as wrt_bf,
                nc.sbuf_tensor("sc_t", [128, 16], F32) as sc_t,
            ):
                with nc.Block("l0f_r") as blk:
                    bW, bPE, bVE, bSC = sW.v, sPE.v, sVE.v, sSC.v

                    @blk.gpsimd
                    def _(gp):
                        ins = gp.dma_start(
                            out=wrt_bf[:].rearrange("p (c e) -> p c e", e=8),
                            in_=w_router_t.rearrange("(c p) e -> p c e", p=128))
                        sW.inc(ins, 16)

                    @blk.tensor
                    def _(t_):
                        t_.wait_ge(sW.h, bW + 16)
                        for t in range(NT):
                            if t >= 8:
                                t_.wait_ge(sVE.h, bVE + t - 7)
                            bank = pb[:, t % 8, 0:8]
                            for dm in range(NDM):
                                ins = t_.matmul(bank, fm(dm, t * 128, 128),
                                                wrt_bf[:, dm * 8:(dm + 1) * 8],
                                                start=(dm == 0), stop=(dm == NDM - 1))
                            sPE.inc(ins)

                    @blk.vector
                    def _(v):
                        for t in range(NT):
                            v.wait_ge(sPE.h, bPE + t + 1)
                            sc = comb_t[:, t * 8:(t + 1) * 8]
                            v.tensor_reduce(sc_t[:, 0:1], pb[:, t % 8, 0:8], AX.X, ALU.max)
                            v.drain()
                            v.tensor_scalar(sc, pb[:, t % 8, 0:8], sc_t[:, 0:1],
                                            None, ALU.subtract)
                            sVE.inc(v.drain())
                            v.wait_ge(sSC.h, bSC + t + 1)
                            v.tensor_reduce(sc_t[:, 1:2], sc, AX.X, ALU.add)
                            v.drain()
                            v.reciprocal(sc_t[:, 1:2], sc_t[:, 1:2])
                            v.drain()
                            v.tensor_scalar(sc, sc, sc_t[:, 1:2], None, ALU.mult)
                            v.drain()
                            v.tensor_reduce(sc_t[:, 2:3], sc, AX.X, ALU.max)
                            v.drain()
                            v.tensor_scalar(sc_t[:, 4:12], sc, sc_t[:, 2:3], None,
                                            ALU.is_ge)
                            v.drain()
                            v.scalar_tensor_tensor(sc_t[:, 4:12], sc_t[:, 4:12], -1e30,
                                                   sc, ALU.mult, ALU.add)
                            v.drain()
                            v.tensor_reduce(sc_t[:, 3:4], sc_t[:, 4:12], AX.X, ALU.max)
                            v.drain()
                            v.tensor_scalar(sc_t[:, 4:12], sc, sc_t[:, 3:4], None,
                                            ALU.is_ge)
                            v.tensor_add(sc_t[:, 2:3], sc_t[:, 2:3], sc_t[:, 3:4])
                            v.drain()
                            v.tensor_scalar(sc_t[:, 2:3], sc_t[:, 2:3], 1e-20, None,
                                            ALU.add)
                            v.drain()
                            v.reciprocal(sc_t[:, 2:3], sc_t[:, 2:3])
                            v.drain()
                            v.tensor_mul(sc, sc, sc_t[:, 4:12])
                            v.drain()
                            v.tensor_scalar(sc, sc, sc_t[:, 2:3], None, ALU.mult)
                            v.drain()

                    @blk.scalar
                    def _(s):
                        for t in range(NT):
                            s.wait_ge(sVE.h, bVE + t + 1)
                            sc = comb_t[:, t * 8:(t + 1) * 8]
                            s.activation(sc, sc, AF.Exp)
                            sSC.inc(s.drain())

                with nc.Block("l0f_e") as blk:
                    bPE, bVE, bSC = sPE.v, sVE.v, sSC.v
                    bD = [s_.v for s_ in sG]

                    def iPE(k):
                        return bPE + 66 * k

                    def iSC(k):
                        return bSC + 22 * k

                    def iVE(k):
                        return bVE + 38 * k

                    wg_ord = {0: 0, 1: 0}
                    wu_ord = {0: 0, 1: 0}
                    wd_ord = {}
                    wg_hist = []
                    wu_hist = []
                    wd_hist = []
                    for k in range(16):
                        for j in range(11):
                            s_ = j % 2
                            wg_ord[s_] += 1
                            wg_hist.append((k, j, s_, wg_ord[s_]))
                            wu_ord[s_] += 1
                            wu_hist.append((k, j, s_, wu_ord[s_]))
                        for g in range(22):
                            s_ = (k * 22 + g) % 3
                            wd_ord[s_] = wd_ord.get(s_, 0) + 1
                            wd_hist.append((k, g, s_, wd_ord[s_]))

                    def wg_need(k, i):
                        j = i // 2
                        for (kk, jj, s_, o_) in wg_hist:
                            if kk == k and jj == j:
                                return s_, o_
                    def wd_need(k, g2):
                        for (kk, gg, s_, o_) in wd_hist:
                            if kk == k and gg == g2:
                                return s_, o_

                    @blk.gpsimd
                    def _(gp):
                        nwg = {0: 0, 1: 0}
                        nwd = {0: 0, 1: 0}
                        lastg = {}
                        lastd = {}
                        for k in range(16):
                            e, half = k // 2, k % 2
                            for j in range(11):
                                s_ = j % 2
                                if (0, s_) in lastg:
                                    pk, pj = lastg[(0, s_)]
                                    gp.wait_ge(sPE.h, iPE(pk) + 2 * pj + 2)
                                lastg[(0, s_)] = (k, j)
                                src_g = wg_t[e].rearrange("(c p) i -> p c i", p=128)[
                                    :, :, j * 256:(j + 1) * 256]
                                dst = wstr[:, s_ * 2048:s_ * 2048 + 2048]
                                ins = gp.dma_start(
                                    out=dst.rearrange("p (c i) -> p c i", i=256),
                                    in_=src_g)
                                sG[s_].inc(ins, 16)
                                if (1, s_) in lastg:
                                    pk, pj = lastg[(1, s_)]
                                    gp.wait_ge(sPE.h, iPE(pk) + 2 * pj + 2)
                                lastg[(1, s_)] = (k, j)
                                src_u = wu_t[e].rearrange("(c p) i -> p c i", p=128)[
                                    :, :, j * 256:(j + 1) * 256]
                                dst = wstr[:, 4096 + s_ * 2048:4096 + s_ * 2048 + 2048]
                                ins = gp.dma_start(
                                    out=dst.rearrange("p (c i) -> p c i", i=256),
                                    in_=src_u)
                                sG[2 + s_].inc(ins, 16)
                            for g2 in range(22):
                                s_ = (k * 22 + g2) % 3
                                if s_ in lastd:
                                    pk, pg = lastd[s_]
                                    pp, pj = pg // 11, pg % 11
                                    gp.wait_ge(sPE.h, iPE(pk) + 22 + pp * NI + 2 * pj + 2)
                                lastd[s_] = (k, g2)
                                pp, pj = g2 // 11, g2 % 11
                                src_d = wd_t[e][pj * 256:(pj + 1) * 256, :] \
                                    .rearrange("(b p) n -> p b n", p=128)
                                dst = wstr[:, 8192 + s_ * 2048:8192 + s_ * 2048 + 2048]
                                ins = gp.dma_start(
                                    out=dst.rearrange("p (b n) -> p b n", n=1024),
                                    in_=src_d)
                                sG[3 + s_].inc(ins, 16)

                    @blk.tensor
                    def _(t_):
                        for k in range(16):
                            e, half = k // 2, k % 2
                            for i in range(NI):
                                s_, o_ = wg_need(k, i)
                                t_.wait_ge(sG[s_].h, bD[s_] + 16 * o_)
                                t_.wait_ge(sG[2 + s_].h, bD[2 + s_] + 16 * o_)
                                if i >= 2:
                                    t_.wait_ge(sSC.h, iSC(k) + i - 1)
                                    t_.wait_ge(sVE.h, iVE(k) + i - 1)
                                elif k >= 1:
                                    t_.wait_ge(sVE.h, iVE(k) - 6 + 2 * i)
                                gb = (i % 2) * 2
                                ub = 4 + (i % 2) * 2
                                sub = i % 2
                                wg_sl = wstr[:, s_ * 2048:s_ * 2048 + 2048]
                                wu_sl = wstr[:, 4096 + s_ * 2048:4096 + s_ * 2048 + 2048]
                                for b in range(2):
                                    for dm in range(NDM):
                                        t_.matmul(pb[:, gb + b, :],
                                                  wg_sl[:, dm * 256 + sub * 128:dm * 256 + sub * 128 + 128],
                                                  fm(dm, half * 1024 + b * 512, 512),
                                                  start=(dm == 0), stop=(dm == NDM - 1))
                                if i < 2 and k >= 1:
                                    t_.wait_ge(sVE.h, iVE(k) - 2 + 2 * i)
                                for b in range(2):
                                    for dm in range(NDM):
                                        ins = t_.matmul(pb[:, ub + b, :],
                                                        wu_sl[:, dm * 256 + sub * 128:dm * 256 + sub * 128 + 128],
                                                        fm(dm, half * 1024 + b * 512, 512),
                                                        start=(dm == 0), stop=(dm == NDM - 1))
                                sPE.inc(ins)
                            t_.wait_ge(sVE.h, iVE(k) + NI)
                            t_.wait_ge(sSC.h, iSC(k) + NI)
                            for g in range(2 * NI):
                                p_, i_ = g // NI, g % NI
                                s_, o_ = wd_need(k, (g // NI) * 11 + (g % NI) // 2)
                                t_.wait_ge(sG[3 + s_].h, bD[3 + s_] + 16 * o_)
                                sub = i_ % 2
                                wd_sl = wstr[:, 8192 + s_ * 2048:8192 + s_ * 2048 + 2048]
                                for tl in range(8):
                                    if p_ == 1 and i_ == 0:
                                        t_.wait_ge(sVE.h, iVE(k) + NI + tl + 1)
                                    ins = t_.matmul(pb[:, tl, :],
                                                    h_fm[:, i_ * 1024 + tl * 128:i_ * 1024 + (tl + 1) * 128],
                                                    wd_sl[:, sub * 1024 + p_ * 512:sub * 1024 + (p_ + 1) * 512],
                                                    start=(i_ == 0), stop=(i_ == NI - 1),
                                                    skip_group_check=True)
                                sPE.inc(ins)

                    @blk.scalar
                    def _(s):
                        for k in range(16):
                            for i in range(NI):
                                s.wait_ge(sPE.h, iPE(k) + i + 1)
                                if i >= 2:
                                    s.wait_ge(sVE.h, iVE(k) + i - 1)
                                gb = (i % 2) * 2
                                gs = gate_s[:, (i % 2) * 1024:(i % 2) * 1024 + 1024]
                                s.activation(gs[:, 0:512], pb[:, gb, :], AF.Silu)
                                ins = s.activation(gs[:, 512:1024], pb[:, gb + 1, :],
                                                   AF.Silu)
                                sSC.inc(ins)

                    @blk.vector
                    def _(v):
                        for k in range(16):
                            e, half = k // 2, k % 2
                            for i in range(NI):
                                v.wait_ge(sSC.h, iSC(k) + i + 1)
                                ub = 4 + (i % 2) * 2
                                gs = gate_s[:, (i % 2) * 1024:(i % 2) * 1024 + 1024]
                                v.tensor_mul(h_fm[:, i * 1024:i * 1024 + 512],
                                             gs[:, 0:512], pb[:, ub, :])
                                ins = v.tensor_mul(h_fm[:, i * 1024 + 512:(i + 1) * 1024],
                                                   gs[:, 512:1024], pb[:, ub + 1, :])
                                sVE.inc(ins)
                            for g in range(2 * NI):
                                p_, i_ = g // NI, g % NI
                                if i_ == NI - 1:
                                    for tl in range(8):
                                        v.wait_ge(sPE.h, iPE(k) + 22 + p_ * NI + NI)
                                        t = half * 8 + tl
                                        cc = comb_t[:, t * 8 + e:t * 8 + e + 1]
                                        sl = x_tm[:, t * D + p_ * 512:t * D + (p_ + 1) * 512]
                                        ins = v.scalar_tensor_tensor(
                                            sl, pb[:, tl, :], cc, sl, ALU.mult, ALU.add)
                                        sVE.inc(ins)

            with nc.sbuf_tensor("lntmp2", [128, 1024], F32) as lntmp2, \
                 nc.Block("l0f_ln") as blk:
                bSC, bVE = sSC.v, sVE.v

                @blk.vector
                def _(v):
                    ln_vector_pass(v, [(t, xt(t)) for t in range(NT)], bSC, bVE)

                @blk.scalar
                def _(s):
                    ln_scalar_pass(s, [(t, xt(t)) for t in range(NT)], lntmp2[:],
                                   bSC, bVE)

        def emit_l1a():
            xfm_rebuild("l1a")
            NC = 16
            with (
                nc.sbuf_tensor("WTbf", [128, 8 * 1024], BF16) as WTbf,
                nc.sbuf_tensor("stf", [128, 384], F32) as stf,
                nc.sbuf_tensor("gci", [128, 128], BF16) as gci,
                nc.sbuf_tensor("wrt2", [128, 8 * 1024], BF16) as wrt2,
                nc.sbuf_tensor("v_bf", [128, 1024], BF16) as v_bf,
                nc.sbuf_tensor("vt_bf", [128, 1024], BF16) as vt_bf,
                nc.sbuf_tensor("rk_s", [128, 1024], BF16) as rk_s,
                nc.sbuf_tensor("smt", [128, 128], BF16) as smt,
                nc.sbuf_tensor("rd_bf", [128, 1024], BF16) as rd_bf,
                nc.sbuf_tensor("rd_fm", [128, 2048], BF16) as rd_fm,
                nc.sbuf_tensor("sc11", [128, 2], F32) as sc11,
                nc.sbuf_tensor("wl1a", [128, 10240], BF16) as wl1a,
            ):
                st_t = stf[:, 0:128]
                st2 = stf[:, 128:256]
                st3 = stf[:, 256:384]
                wwt_s = wl1a[:, 0:8192]
                wk_tm = wl1a[:, 8192:10240]
                grow_bf = rows[0:1, 9088:9216]
                with nc.Block("l1a_s") as blk:
                    bIN, bW, bSC, bVE, bPE, bGP = (sIN.v, sW.v, sSC.v, sVE.v,
                                                   sPE.v, sGP.v)

                    @blk.sync
                    def _(sy):
                        ins = sy.dma_start(out=sc11[0:1, 0:1], in_=decay)
                        sIN.inc(ins, 16)
                        ins = sy.dma_start(out=sc11[0:1, 1:2], in_=log_alpha)
                        sIN.inc(ins, 16)

                    @blk.gpsimd
                    def _(gp):
                        ins = gp.dma_start(
                            out=wrt2[:].rearrange("p (c n) -> p c n", n=1024),
                            in_=wr_t.rearrange("(c p) n -> p c n", p=128))
                        sW.inc(ins, 16)
                        ins = gp.dma_start(
                            out=wwt_s[:].rearrange("p (c n) -> p c n", n=1024),
                            in_=ww_t.rearrange("(c p) n -> p c n", p=128))
                        sW.inc(ins, 16)
                        gp.iota(mf32[:, 20:21], [[1, 1]], base=0, channel_multiplier=1,
                                allow_small_or_imprecise_dtypes=True)
                        gp.iota(st2[0:1, 0:128], [[1, 128]], base=0,
                                channel_multiplier=0,
                                allow_small_or_imprecise_dtypes=True)
                        ins = gp.iota(st_t[:], [[1, 128]], base=-1,
                                      channel_multiplier=-1,
                                      allow_small_or_imprecise_dtypes=True)
                        sGP.inc(ins)

                    @blk.scalar
                    def _(s):
                        s.wait_ge(sIN.h, bIN + 32)
                        s.activation(sc11[0:1, 0:1], sc11[0:1, 0:1], AF.Sigmoid)
                        s.activation(sc11[0:1, 0:1], sc11[0:1, 0:1], AF.Ln)
                        sSC.inc(s.drain())
                        s.wait_ge(sVE.h, bVE + 1)
                        s.activation(c_gw, c_t1, AF.Exp)
                        s.activation(c_gc, c_t0, AF.Exp)
                        s.activation(c_al, c_al, AF.Exp)
                        s.activation(st3[:], st3[:], AF.Exp)
                        s.activation(st2[0:1, 0:128], st2[0:1, 0:128], AF.Exp)
                        sSC.inc(s.drain())

                    @blk.tensor
                    def _(t_):
                        t_.wait_ge(sSC.h, bSC + 1)
                        ins = t_.matmul(pb[:, 0, 0:2], onesf[:], sc11[0:1, 0:2],
                                        start=True, stop=True)
                        sPE.inc(ins)
                        t_.wait_ge(sVE.h, bVE + 2)
                        ins = t_.matmul(pb[:, 1, 0:128], ones_row, grow_bf,
                                        start=True, stop=True)
                        sPE.inc(ins)

                    @blk.vector
                    def _(v):
                        v.wait_ge(sPE.h, bPE + 1)
                        v.wait_ge(sGP.h, bGP + 1)
                        v.tensor_copy(c_lg, pb[:, 0, 0:1])
                        v.tensor_copy(c_al, pb[:, 0, 1:2])
                        v.drain()
                        v.tensor_mul(c_t0, c_lg, c_lg)
                        v.tensor_scalar(c_t1, mf32[:, 20:21], -1.0, 127.0,
                                        ALU.mult, ALU.add)
                        v.drain()
                        v.tensor_mul(c_t1, c_t1, c_lg)
                        v.tensor_scalar(c_t0, c_lg, 128.0, None, ALU.mult)
                        v.tensor_scalar(st3[:], st_t[:], c_lg, None, ALU.mult)
                        v.drain()
                        v.tensor_single_scalar(st_t[:], st_t[:], 0.0, ALU.is_ge)
                        v.tensor_scalar(st2[0:1, 0:128], st2[0:1, 0:128],
                                        c_lg[0:1, :], None, ALU.mult)
                        sVE.inc(v.drain())
                        v.wait_ge(sSC.h, bSC + 2)
                        v.tensor_mul(st_t[:], st3[:], st_t[:])
                        v.drain()
                        v.tensor_copy(m_t_tile, st_t[:])
                        v.tensor_copy(grow_bf, st2[0:1, 0:128])
                        sVE.inc(v.drain())
                        v.wait_ge(sPE.h, bPE + 2)
                        v.tensor_copy(gam_bc, pb[:, 1, 0:128])
                        v.drain()
                        v.tensor_scalar(gci[:], ident, c_gc, None, ALU.mult)
                        for ec in range(8):
                            v.memset(WTbf[:, ec * 1024:(ec + 1) * 1024], 0.0)
                        sVE.inc(v.drain())

                with nc.Block("l1a_c") as blk:
                    bPE, bVE, bSC = sPE.v, sVE.v, sSC.v
                    bWl = sW.v - 32
                    bD0, bD1, bD2 = sD[0].v, sD[1].v, sD[2].v
                    bDw = (bD0, bD1)
                    PEC, VEC, SCC = 19, 9, 4

                    def cPE(ch):
                        return bPE + PEC * ch

                    def cVE(ch):
                        return bVE + VEC * ch

                    def cSC(ch):
                        return bSC + SCC * ch

                    @blk.sync
                    def _(sy):
                        for ch in range(NC):
                            if ch >= 2:
                                sy.wait_ge(sPE.h, cPE(ch - 2) + 18)
                            dstw = wk_tm[:, (ch % 2) * 1024:(ch % 2) * 1024 + 1024]
                            if ch == 0:
                                ins = sy.dma_start(out=dstw[1:128, :],
                                                   in_=xbf_dram[0:127, :])
                            else:
                                ins = sy.dma_start(out=dstw,
                                                   in_=xbf_dram[ch * 128 - 1:
                                                                (ch + 1) * 128 - 1, :])
                            sD[ch % 2].inc(ins, 16)
                            sy.wait_ge(sVE.h, cVE(ch) + 4)
                            if ch >= 2:
                                sy.wait_ge(sPE.h, cPE(ch - 2) + 19)
                            for dc in range(NDM):
                                ins = sy.dma_start_transpose(
                                    rd_fm[:, (ch % 2) * 1024 + dc * 128:
                                          (ch % 2) * 1024 + (dc + 1) * 128],
                                    rd_bf[:, dc * 128:(dc + 1) * 128])
                                sD[2].inc(ins, 16)

                    @blk.vector
                    def _(v):
                        v.memset(wk_tm[0:1, 0:1024], 0.0)
                        v.drain()
                        for ch in range(NC):
                            v.wait_ge(sPE.h, cPE(ch) + 8)
                            v.tensor_copy(v_bf[:, 0:512], pb[:, 0, :])
                            v.tensor_copy(v_bf[:, 512:1024], pb[:, 1, :])
                            v.tensor_scalar(vt_bf[:, 0:512], pb[:, 0, :], c_gw, None,
                                            ALU.mult)
                            ins = v.tensor_scalar(vt_bf[:, 512:1024], pb[:, 1, :],
                                                  c_gw, None, ALU.mult)
                            sVE.inc(ins)
                            for ec in range(NDM):
                                ins = v.tensor_mul(rk_s[:, ec * 128:(ec + 1) * 128],
                                                   fm(ec, ch * 128, 128), gam_bc)
                            sVE.inc(ins)
                            v.wait_ge(sPE.h, cPE(ch) + 9)
                            ins = v.tensor_mul(smt[:], pb[:, 4, 0:128], m_t_tile)
                            sVE.inc(ins)
                            v.wait_ge(sPE.h, cPE(ch) + 10)
                            if ch >= 1:
                                v.wait_ge(sD[2].h, bD2 + 16 * 8 * ch)
                            v.tensor_copy(rd_bf[:, 0:512], pb[:, 2, :])
                            ins = v.tensor_copy(rd_bf[:, 512:1024], pb[:, 3, :])
                            sVE.inc(ins)
                            for ec in (0, 2, 4, 6):
                                v.wait_ge(sPE.h, cPE(ch) + 11 + ec)
                                wsl = WTbf[:, ec * 1024:(ec + 1) * 1024]
                                v.tensor_copy(wsl[:, 0:512], pb[:, 0, :])
                                ins = v.tensor_copy(wsl[:, 512:1024], pb[:, 1, :])
                                sVE.inc(ins)
                            v.wait_ge(sPE.h, cPE(ch) + 19)
                            xsl = xt(ch)
                            v.tensor_scalar(xsl, xsl, 2.0, None, ALU.mult)
                            v.scalar_tensor_tensor(xsl[:, 0:512], pb[:, 5, :], c_al,
                                                   xsl[:, 0:512], ALU.mult, ALU.add)
                            ins = v.scalar_tensor_tensor(xsl[:, 512:1024], pb[:, 6, :],
                                                         c_al, xsl[:, 512:1024],
                                                         ALU.mult, ALU.add)
                            sVE.inc(ins)

                    @blk.scalar
                    def _(s):
                        for ch in range(NC):
                            for ec in (1, 3, 5, 7):
                                s.wait_ge(sPE.h, cPE(ch) + 11 + ec)
                                wsl = WTbf[:, ec * 1024:(ec + 1) * 1024]
                                s.copy(wsl[:, 0:512], pb[:, 2, :])
                                ins = s.copy(wsl[:, 512:1024], pb[:, 3, :])
                                sSC.inc(ins)

                    @blk.tensor
                    def _(t_):
                        for ch in range(NC):
                            c0 = ch * 128
                            if ch >= 1:
                                t_.wait_ge(sVE.h, cVE(ch) - 1)
                                t_.wait_ge(sSC.h, cSC(ch))
                            if ch == 0:
                                t_.wait_ge(sW.h, bWl + 16 * 2)
                            for dm in range(NDM):
                                for p in range(2):
                                    ins = t_.matmul(
                                        pb[:, p, :], fm(dm, c0, 128),
                                        wwt_s[:, dm * 1024 + p * 512:
                                              dm * 1024 + (p + 1) * 512],
                                        start=(dm == 0), stop=(dm == NDM - 1),
                                        skip_group_check=True)
                                sPE.inc(ins)
                            for ec in range(NDM):
                                ins = t_.matmul(pb[:, 4, 0:128], fm(ec, c0 - 1, 128),
                                                fm(ec, c0, 128), start=(ec == 0),
                                                stop=(ec == NDM - 1))
                            sPE.inc(ins)
                            t_.wait_ge(sVE.h, cVE(ch) + 2)
                            for ec in range(NDM):
                                for p in range(2):
                                    t_.matmul(pb[:, 2 + p, :],
                                              rk_s[:, ec * 128:(ec + 1) * 128],
                                              WTbf[:, ec * 1024 + p * 512:
                                                   ec * 1024 + (p + 1) * 512],
                                              start=(ec == 0), stop=False,
                                              skip_group_check=True)
                            t_.wait_ge(sVE.h, cVE(ch) + 3)
                            for p in range(2):
                                ins = t_.matmul(pb[:, 2 + p, :], smt[:],
                                                v_bf[:, p * 512:(p + 1) * 512],
                                                start=False, stop=True,
                                                skip_group_check=True)
                            sPE.inc(ins)
                            t_.wait_ge(sD[ch % 2].h, bDw[ch % 2] + 16 * (ch // 2 + 1))
                            t_.wait_ge(sVE.h, cVE(ch) + 4)
                            for ec in range(NDM):
                                bk = (ec % 2) * 2
                                if ec >= 2:
                                    if ec % 2 == 0:
                                        t_.wait_ge(sVE.h, cVE(ch) + 4 + ec // 2)
                                    else:
                                        t_.wait_ge(sSC.h, cSC(ch) + (ec - 1) // 2)
                                for p in range(2):
                                    t_.matmul(pb[:, bk + p, :], gci[:],
                                              WTbf[:, ec * 1024 + p * 512:
                                                   ec * 1024 + (p + 1) * 512],
                                              start=True, stop=False,
                                              skip_group_check=True)
                                for p in range(2):
                                    ins = t_.matmul(
                                        pb[:, bk + p, :],
                                        wk_tm[:, (ch % 2) * 1024 + ec * 128:
                                              (ch % 2) * 1024 + (ec + 1) * 128],
                                        vt_bf[:, p * 512:(p + 1) * 512],
                                        start=False, stop=True, skip_group_check=True)
                                sPE.inc(ins)
                            t_.wait_ge(sVE.h, cVE(ch))
                            t_.wait_ge(sD[2].h, bD2 + 16 * 8 * (ch + 1))
                            for dc in range(NDM):
                                for p in range(2):
                                    ins = t_.matmul(pb[:, 5 + p, :],
                                                    rd_fm[:, (ch % 2) * 1024 + dc * 128:
                                                         (ch % 2) * 1024 + (dc + 1) * 128],
                                                    wrt2[:, dc * 1024 + p * 512:
                                                         dc * 1024 + (p + 1) * 512],
                                                    start=(dc == 0),
                                                    stop=(dc == NDM - 1),
                                                    skip_group_check=True)
                            sPE.inc(ins)

                with nc.Block("l1a_ln") as blk:
                    bSC, bVE = sSC.v, sVE.v

                    @blk.vector
                    def _(v):
                        ln_vector_pass(v, [(t, xt(t)) for t in range(NT)], bSC, bVE)

                    @blk.scalar
                    def _(s):
                        ln_scalar_pass(s, [(t, xt(t)) for t in range(NT)],
                                       v_bf[:], bSC, bVE)

        def emit_l1f():
            xfm_rebuild("l1f")
            NQ, QT = 4, 512
            NI = 16
            with (
                nc.sbuf_tensor("spk", [128, NI * QT], BF16) as spk,
                nc.sbuf_tensor("ehf", [128, 8 * QT], BF16) as ehf,
                nc.sbuf_tensor("ew2", [128, 8 * 8 * 64], BF16) as ew2,
                nc.sbuf_tensor("s2cb_w", [128, NI * 64], BF16) as s2cb_w,
                nc.sbuf_tensor("c2s_bf", [128, 2048], BF16) as c2s_bf,
                nc.sbuf_tensor("rw1_bf", [128, 64], BF16) as rw1_bf,
                nc.sbuf_tensor("rw2_bf", [128, 8], BF16) as rw2_bf,
                nc.sbuf_tensor("cont_bf", [128, QT], BF16) as cont_bf,
                nc.sbuf_tensor("rh_bf", [128, QT], BF16) as rh_bf,
                nc.sbuf_tensor("comb2", [128, 32], F32) as comb2,
                nc.sbuf_tensor("moe_tm", [128, 4 * 64], F32) as moe_tm,
                nc.sbuf_tensor("moe_bf", [128, 512], BF16) as moe_bf,
                nc.sbuf_tensor("mfm", [128, 512], BF16) as mfm,
                nc.sbuf_tensor("dec_s", [128, 4 * 1024], F32) as dec_s,
                nc.sbuf_tensor("ones5", [1, 512], BF16) as ones5,
                nc.sbuf_tensor("sc_q", [128, 16], F32) as sc_q,
                nc.sbuf_tensor("disc", [128, 1024], F32) as disc,
                nc.sbuf_tensor("wl1f", [128, 6144], BF16) as wl1f,
            ):
                rts = spk
                r_encb = rows[0:1, 0:2048]
                r_c2sb = rows[0:1, 2048:4096]
                r_decb = rows[0:1, 4096:5120]
                r_s2cb = rows[0:1, 5120:5184]
                r_rb1 = rows[0:1, 5184:5248]
                r_rb2 = rows[0:1, 5248:5256]
                r_eb1 = rows[0:1, 5256:6280]
                r_eb2 = rows[0:1, 6280:6792]
                r_lng = rows[0:1, 6792:7816]
                r_lnb = rows[0:1, 7816:8840]

                with nc.Block("l1f_s") as blk:
                    bW, bPE, bVE = sW.v, sPE.v, sVE.v

                    @blk.gpsimd
                    def _(gp):
                        for dst, srcd in ((r_encb, enc_b), (r_c2sb, c2s_b),
                                          (r_decb, dec_b), (r_s2cb, s2c_b),
                                          (r_rb1, r_b1), (r_rb2, r_b2),
                                          (r_eb2, e_b2), (r_lng, ln_g),
                                          (r_lnb, ln_b)):
                            ins = gp.dma_start(out=dst, in_=srcd)
                            sW.inc(ins, 16)
                        ins = gp.dma_start(
                            out=s2cb_w[:].rearrange("p (c m) -> p c m", m=64),
                            in_=s2c_w.rearrange("(c p) m -> p c m", p=128))
                        sW.inc(ins, 16)
                        ins = gp.dma_start(
                            out=c2s_bf[0:64, :], in_=c2s_w)
                        sW.inc(ins, 16)
                        ins = gp.dma_start(out=rw1_bf[0:64, :], in_=r_w1)
                        sW.inc(ins, 16)
                        ins = gp.dma_start(out=rw2_bf[0:64, :], in_=r_w2)
                        sW.inc(ins, 16)
                        ins = gp.dma_start(
                            out=ew2[:].rearrange("p (e c d) -> p e c d", e=8, d=64),
                            in_=e_w2.rearrange("e (c p) d -> p e c d", p=128))
                        sW.inc(ins, 16)

                    @blk.vector
                    def _(v):
                        v.memset(ones5[:], 1.0)
                        sVE.inc(v.drain())

                    @blk.tensor
                    def _(t_):
                        t_.wait_ge(sW.h, bW + 16 * 14)
                        t_.wait_ge(sVE.h, bVE + 1)
                        t_.matmul(pb[:, 0, 0:512], ones_row, r_lng[0:1, 0:512],
                                  start=True, stop=True)
                        t_.matmul(pb[:, 1, 0:512], ones_row, r_lng[0:1, 512:1024],
                                  start=True, stop=True)
                        t_.matmul(pb[:, 2, 0:512], ones_row, r_lnb[0:1, 0:512],
                                  start=True, stop=True)
                        ins = t_.matmul(pb[:, 3, 0:512], ones_row, r_lnb[0:1, 512:1024],
                                        start=True, stop=True)
                        sPE.inc(ins)

                    @blk.vector
                    def _(v2):
                        v2.wait_ge(sPE.h, bPE + 1)
                        v2.tensor_copy(lng_bc[:, 0:512], pb[:, 0, :])
                        v2.tensor_copy(lng_bc[:, 512:1024], pb[:, 1, :])
                        v2.tensor_copy(lnb_bc[:, 0:512], pb[:, 2, :])
                        v2.tensor_copy(lnb_bc[:, 512:1024], pb[:, 3, :])
                        v2.drain()

                for q in range(NQ):
                    q0 = q * QT
                    with nc.Block(f"l1f_a{q}") as blk:
                        bPE, bVE, bSC = sPE.v, sVE.v, sSC.v
                        bG = [s_.v for s_ in sG]

                        @blk.gpsimd
                        def _(gp):
                            for i in range(NI):
                                if i >= 2:
                                    gp.wait_ge(sPE.h, bPE + i - 1)
                                src = enc_w.rearrange("(c p) n -> p c n", p=128)[
                                    :, :, i * 128:(i + 1) * 128]
                                dst = wl1f[:, (i % 2) * 1024:(i % 2) * 1024 + 1024]
                                ins = gp.dma_start(
                                    out=dst.rearrange("p (c n) -> p c n", n=128),
                                    in_=src)
                                sG[i % 2].inc(ins, 16)

                        @blk.tensor
                        def _(t_):
                            for i in range(NI):
                                t_.wait_ge(sG[i % 2].h, bG[i % 2] + 16 * (i // 2 + 1))
                                if i >= 2:
                                    t_.wait_ge(sSC.h, bSC + i - 1)
                                bank = pb[:, i % 2, :]
                                wsl = wl1f[:, (i % 2) * 1024:(i % 2) * 1024 + 1024]
                                for dm in range(NDM):
                                    t_.matmul(bank, wsl[:, dm * 128:(dm + 1) * 128],
                                              fm(dm, q0, QT),
                                              start=(dm == 0), stop=False)
                                ins = t_.matmul(bank, r_encb[0:1, i * 128:(i + 1) * 128],
                                                ones5[:], start=False, stop=True)
                                sPE.inc(ins)
                            for i in range(NI):
                                t_.wait_ge(sSC.h, bSC + i + 1)
                                t_.matmul(pb[0:64, 2, 0:512], s2cb_w[:, i * 64:(i + 1) * 64],
                                          spk[:, i * QT:(i + 1) * QT],
                                          start=(i == 0), stop=False,
                                          skip_group_check=True)
                            ins = t_.matmul(pb[0:64, 2, 0:512], r_s2cb, ones5[:],
                                            start=False, stop=True,
                                            skip_group_check=True)
                            sPE.inc(ins)
                            t_.wait_ge(sVE.h, bVE + 1)
                            t_.matmul(pb[0:64, 3, 0:512], rw1_bf[0:64, :],
                                      cont_bf[0:64, :], start=True, stop=False)
                            ins = t_.matmul(pb[0:64, 3, 0:512], r_rb1, ones5[:],
                                            start=False, stop=True)
                            sPE.inc(ins)
                            t_.wait_ge(sSC.h, bSC + NI + 1)
                            for tl in range(4):
                                t_.matmul(pb[:, 4 + tl, 0:8], rh_bf[0:64, tl * 128:(tl + 1) * 128],
                                          rw2_bf[0:64, :], start=True, stop=False)
                                ins = t_.matmul(pb[:, 4 + tl, 0:8], ones_row, r_rb2,
                                                start=False, stop=True)
                                sPE.inc(ins)

                        @blk.scalar
                        def _(s):
                            for i in range(NI):
                                s.wait_ge(sPE.h, bPE + i + 1)
                                ins = s.activation(spk[:, i * QT:(i + 1) * QT],
                                                   pb[:, i % 2, :], AF.Sigmoid)
                                sSC.inc(ins)
                            s.wait_ge(sPE.h, bPE + 18)
                            ins = s.activation(rh_bf[0:64, :], pb[0:64, 3, 0:512],
                                               AF.Relu)
                            sSC.inc(s.drain())
                            for tl in range(4):
                                s.wait_ge(sVE.h, bVE + 1 + tl + 1)
                                sc = comb2[:, tl * 8:(tl + 1) * 8]
                                s.activation(sc, sc, AF.Exp)
                                sSC.inc(s.drain())

                        @blk.vector
                        def _(v):
                            v.memset(moe_tm[:], 0.0)
                            v.memset(moe_bf[:], 0.0)
                            v.wait_ge(sPE.h, bPE + 17)
                            ins = v.tensor_copy(cont_bf[0:64, :], pb[0:64, 2, 0:512])
                            sVE.inc(v.drain())
                            for tl in range(4):
                                v.wait_ge(sPE.h, bPE + 18 + tl + 1)
                                sc = comb2[:, tl * 8:(tl + 1) * 8]
                                v.tensor_reduce(sc_q[:, 0:1], pb[:, 4 + tl, 0:8], AX.X,
                                                ALU.max)
                                v.drain()
                                v.tensor_scalar(sc, pb[:, 4 + tl, 0:8], sc_q[:, 0:1],
                                                None, ALU.subtract)
                                sVE.inc(v.drain())
                                v.wait_ge(sSC.h, bSC + 17 + tl + 1)
                                v.tensor_reduce(sc_q[:, 1:2], sc, AX.X, ALU.add)
                                v.drain()
                                v.reciprocal(sc_q[:, 1:2], sc_q[:, 1:2])
                                v.drain()
                                v.tensor_scalar(sc, sc, sc_q[:, 1:2], None, ALU.mult)
                                v.drain()
                                v.tensor_reduce(sc_q[:, 2:3], sc, AX.X, ALU.max)
                                v.drain()
                                v.tensor_scalar(sc_q[:, 4:12], sc, sc_q[:, 2:3], None,
                                                ALU.is_ge)
                                v.drain()
                                v.scalar_tensor_tensor(sc_q[:, 4:12], sc_q[:, 4:12],
                                                       -1e30, sc, ALU.mult, ALU.add)
                                v.drain()
                                v.tensor_reduce(sc_q[:, 3:4], sc_q[:, 4:12], AX.X, ALU.max)
                                v.drain()
                                v.tensor_scalar(sc_q[:, 4:12], sc, sc_q[:, 3:4], None,
                                                ALU.is_ge)
                                v.tensor_add(sc_q[:, 2:3], sc_q[:, 2:3], sc_q[:, 3:4])
                                v.drain()
                                v.reciprocal(sc_q[:, 2:3], sc_q[:, 2:3])
                                v.drain()
                                v.tensor_mul(sc, sc, sc_q[:, 4:12])
                                v.drain()
                                v.tensor_scalar(sc, sc, sc_q[:, 2:3], None, ALU.mult)
                                v.drain()

                    with nc.Block(f"l1f_b{q}") as blk:
                        bPE, bVE, bSC, bW2 = sPE.v, sVE.v, sSC.v, sW.v
                        bG4, bG5 = sG[4].v, sG[5].v
                        bGe = (bG4, bG5)

                        @blk.gpsimd
                        def _(gp):
                            for e in range(8):
                                if e >= 2:
                                    gp.wait_ge(sPE.h, bPE + 12 * (e - 1))
                                src = e_w1[e]
                                dst = wl1f[:, 2048 + (e % 2) * 1024:
                                           2048 + (e % 2) * 1024 + 1024]
                                ins = gp.dma_start(out=dst[0:64, :], in_=src)
                                sG[4 + e % 2].inc(ins, 16)
                                if e >= 1:
                                    gp.wait_ge(sPE.h, bPE + 12 * (e - 1) + 8)
                                ins = gp.dma_start(out=r_eb1, in_=e_b1[e:e + 1, :])
                                sW.inc(ins, 16)

                        @blk.tensor
                        def _(t_):
                            for e in range(8):
                                t_.wait_ge(sG[4 + e % 2].h,
                                           bGe[e % 2] + 16 * (e // 2 + 1))
                                t_.wait_ge(sW.h, bW2 + 16 * (e + 1))
                                w1 = wl1f[:, 2048 + (e % 2) * 1024:
                                          2048 + (e % 2) * 1024 + 1024]
                                for hc in range(8):
                                    if e * 8 + hc >= 2:
                                        t_.wait_ge(sSC.h, bSC + 8 * e + hc - 1)
                                    bank = pb[:, hc % 2, :]
                                    t_.matmul(bank, w1[0:64, hc * 128:(hc + 1) * 128],
                                              cont_bf[0:64, :], start=True, stop=False)
                                    ins = t_.matmul(bank,
                                                    r_eb1[0:1, hc * 128:(hc + 1) * 128],
                                                    ones5[:], start=False, stop=True)
                                    sPE.inc(ins)
                                t_.wait_ge(sSC.h, bSC + 8 * e + 8)
                                for tl in range(4):
                                    if e >= 1:
                                        t_.wait_ge(sVE.h, bVE + 4 * (e - 1) + tl + 1)
                                    bank = pb[:, 2 + tl, 0:64]
                                    for hc in range(8):
                                        t_.matmul(bank,
                                                  ehf[:, hc * QT + tl * 128:
                                                      hc * QT + (tl + 1) * 128],
                                                  ew2[:, (e * 8 + hc) * 64:
                                                      (e * 8 + hc + 1) * 64],
                                                  start=(hc == 0), stop=False,
                                                  skip_group_check=True)
                                    ins = t_.matmul(bank, ones_row,
                                                    r_eb2[0:1, e * 64:(e + 1) * 64],
                                                    start=False, stop=True,
                                                    skip_group_check=True)
                                    sPE.inc(ins)

                        @blk.scalar
                        def _(s):
                            for e in range(8):
                                for hc in range(8):
                                    s.wait_ge(sPE.h, bPE + 12 * e + hc + 1)
                                    ins = s.activation(
                                        ehf[:, hc * QT:(hc + 1) * QT],
                                        pb[:, hc % 2, :], AF.Relu)
                                    sSC.inc(ins)

                        @blk.vector
                        def _(v):
                            for e in range(8):
                                for tl in range(4):
                                    v.wait_ge(sPE.h, bPE + 12 * e + 8 + tl + 1)
                                    cc = comb2[:, tl * 8 + e:tl * 8 + e + 1]
                                    msl = moe_tm[:, tl * 64:(tl + 1) * 64]
                                    v.scalar_tensor_tensor(msl, pb[:, 2 + tl, 0:64],
                                                           cc, msl, ALU.mult, ALU.add)
                                    sVE.inc(v.drain())

                    with nc.Block(f"l1f_c{q}") as blk:
                        bPE, bVE, bSC, bD3 = sPE.v, sVE.v, sSC.v, sD[3].v
                        bG4, bG5 = sG[4].v, sG[5].v
                        bGe = (bG4, bG5)

                        @blk.vector
                        def _(v):
                            for tl in range(4):
                                v.tensor_copy(moe_bf[:, tl * 128:tl * 128 + 64],
                                              moe_tm[:, tl * 64:(tl + 1) * 64])
                            sVE.inc(v.drain())

                        @blk.sync
                        def _(sy):
                            sy.wait_ge(sVE.h, bVE + 1)
                            for tl in range(4):
                                ins = sy.dma_start_transpose(
                                    mfm[:, tl * 128:(tl + 1) * 128],
                                    moe_bf[:, tl * 128:(tl + 1) * 128])
                                sD[3].inc(ins, 16)

                        @blk.gpsimd
                        def _(gp):
                            for i in range(NI):
                                if i >= 2:
                                    gp.wait_ge(sPE.h, bPE + 16 + i - 1)
                                src = dec_w[i * 128:(i + 1) * 128, :]
                                dst = wl1f[:, 4096 + (i % 2) * 1024:
                                           4096 + (i % 2) * 1024 + 1024]
                                ins = gp.dma_start(out=dst, in_=src)
                                sG[4 + i % 2].inc(ins, 16)

                        @blk.tensor
                        def _(t_):
                            t_.wait_ge(sD[3].h, bD3 + 64)
                            for i in range(NI):
                                if i >= 2:
                                    t_.wait_ge(sSC.h, bSC + i - 1)
                                bank = pb[:, i % 2, :]
                                t_.matmul(bank, c2s_bf[0:64, i * 128:(i + 1) * 128],
                                          mfm[0:64, :], start=True, stop=False)
                                ins = t_.matmul(bank,
                                                r_c2sb[0:1, i * 128:(i + 1) * 128],
                                                ones5[:], start=False, stop=True)
                                sPE.inc(ins)
                            t_.wait_ge(sSC.h, bSC + NI)
                            for i in range(NI):
                                gq = q * NI + i
                                t_.wait_ge(sG[4 + i % 2].h,
                                           bGe[i % 2] + 16 * (i // 2 + 1))
                                dsl = wl1f[:, 4096 + (i % 2) * 1024:
                                           4096 + (i % 2) * 1024 + 1024]
                                for tl in range(4):
                                    for p_ in range(2):
                                        ins = t_.matmul(pb[:, tl * 2 + p_, :],
                                                        rts[:, i * QT + tl * 128:
                                                            i * QT + (tl + 1) * 128],
                                                        dsl[:, p_ * 512:(p_ + 1) * 512],
                                                        start=(i == 0), stop=False,
                                                        skip_group_check=True)
                                sPE.inc(ins)
                            for tl in range(4):
                                for p_ in range(2):
                                    ins = t_.matmul(pb[:, tl * 2 + p_, :], ones_row,
                                                    r_decb[0:1, p_ * 512:(p_ + 1) * 512],
                                                    start=False, stop=True,
                                                    skip_group_check=True)
                                sPE.inc(ins)

                        @blk.scalar
                        def _(s):
                            for i in range(NI):
                                s.wait_ge(sPE.h, bPE + i + 1)
                                ins = s.activation(rts[:, i * QT:(i + 1) * QT],
                                                   pb[:, i % 2, :], AF.Sigmoid)
                                sSC.inc(ins)
                            for tl in range(4):
                                s.wait_ge(sPE.h, bPE + 32 + tl + 1)
                                s.activation(dec_s[:, tl * 1024:tl * 1024 + 512],
                                             pb[:, tl * 2, :], AF.Sigmoid)
                                ins = s.activation(dec_s[:, tl * 1024 + 512:
                                                         (tl + 1) * 1024],
                                                   pb[:, tl * 2 + 1, :], AF.Sigmoid)
                                sSC.inc(ins)
                            tiles = [(j, dec_s[:, j * 1024:(j + 1) * 1024])
                                     for j in range(4)]
                            ln_scalar_pass(s, tiles, disc[:], bSC + 20, bVE + 1)

                        @blk.vector
                        def _(v2):
                            tiles = [(j, dec_s[:, j * 1024:(j + 1) * 1024])
                                     for j in range(4)]
                            v2.wait_ge(sSC.h, bSC + 20)
                            ln_vector_pass(v2, tiles, bSC + 20, bVE + 1, gated=True)
                            for j in range(4):
                                t = q * 4 + j
                                v2.tensor_add(xt(t), xt(t),
                                              dec_s[:, j * 1024:(j + 1) * 1024])
                            v2.drain()

                with nc.Block("l1f_ln") as blk:
                    bSC, bVE = sSC.v, sVE.v

                    @blk.vector
                    def _(v):
                        ln_vector_pass(v, [(t, xt(t)) for t in range(NT)], bSC, bVE)

                    @blk.scalar
                    def _(s):
                        ln_scalar_pass(s, [(t, xt(t)) for t in range(NT)],
                                       disc[:], bSC, bVE)

        if "l0a" in stages:
            emit_l0a()
        if "l0f" in stages:
            emit_l0f()
        if "l1a" in stages:
            emit_l1a()
        if "l1f" in stages:
            emit_l1f()

        with nc.Block("epi") as blk:
            @blk.sync
            def _(sy):
                ins = sy.dma_start(
                    out=out.rearrange("(t p) d -> p t d", p=128),
                    in_=x_tm[:].rearrange("p (t d) -> p t d", d=D))
                sST.inc(ins, 16)
                sy.wait_ge(sST.h, sST.v)
    return nc


def _prep_inputs(inputs):
    f = np.float32
    g = {}
    g["w_qkv"] = np.ascontiguousarray(inputs["L0_attn_W_qkv"], f)
    g["b_qkv"] = np.ascontiguousarray(inputs["L0_attn_b_qkv"], f).reshape(1, -1)
    g["w_out"] = np.ascontiguousarray(inputs["L0_attn_W_out"], f)
    g["b_out"] = np.ascontiguousarray(inputs["L0_attn_b_out"], f).reshape(1, -1)
    g["w_router_t"] = np.ascontiguousarray(inputs["L0_ffn_w_router"].T, f)
    g["wg_t"] = np.ascontiguousarray(inputs["L0_ffn_w_gate"].transpose(0, 2, 1), f)
    g["wu_t"] = np.ascontiguousarray(inputs["L0_ffn_w_up"].transpose(0, 2, 1), f)
    g["wd_t"] = np.ascontiguousarray(inputs["L0_ffn_w_down"].transpose(0, 2, 1), f)
    g["ww_t"] = np.ascontiguousarray(inputs["L1_attn_W_write"].T, f)
    g["wr_t"] = np.ascontiguousarray(inputs["L1_attn_W_read"].T, f)
    g["decay"] = np.asarray(inputs["L1_attn_decay"], f).reshape(1, 1)
    g["log_alpha"] = np.asarray(inputs["L1_attn_log_alpha"], f).reshape(1, 1)
    g["enc_w"] = np.ascontiguousarray(inputs["L1_ffn_enc_w"], f)
    g["enc_b"] = np.ascontiguousarray(inputs["L1_ffn_enc_b"], f).reshape(1, -1)
    g["s2c_w"] = np.ascontiguousarray(inputs["L1_ffn_s2c_w"], f)
    g["s2c_b"] = np.ascontiguousarray(inputs["L1_ffn_s2c_b"], f).reshape(1, -1)
    g["r_w1"] = np.ascontiguousarray(inputs["L1_ffn_r_w1"], f)
    g["r_b1"] = np.ascontiguousarray(inputs["L1_ffn_r_b1"], f).reshape(1, -1)
    g["r_w2"] = np.ascontiguousarray(inputs["L1_ffn_r_w2"], f)
    g["r_b2"] = np.ascontiguousarray(inputs["L1_ffn_r_b2"], f).reshape(1, -1)
    g["e_w1"] = np.ascontiguousarray(inputs["L1_ffn_e_w1"], f)
    g["e_b1"] = np.ascontiguousarray(inputs["L1_ffn_e_b1"], f)
    g["e_w2"] = np.ascontiguousarray(inputs["L1_ffn_e_w2"], f)
    g["e_b2"] = np.ascontiguousarray(inputs["L1_ffn_e_b2"], f).reshape(1, -1)
    g["c2s_w"] = np.ascontiguousarray(inputs["L1_ffn_c2s_w"], f)
    g["c2s_b"] = np.ascontiguousarray(inputs["L1_ffn_c2s_b"], f).reshape(1, -1)
    g["dec_w"] = np.ascontiguousarray(inputs["L1_ffn_dec_w"], f)
    g["dec_b"] = np.ascontiguousarray(inputs["L1_ffn_dec_b"], f).reshape(1, -1)
    g["ln_g"] = np.ascontiguousarray(inputs["L1_ffn_ln_g"], f).reshape(1, -1)
    g["ln_b"] = np.ascontiguousarray(inputs["L1_ffn_ln_b"], f).reshape(1, -1)
    return g


def kernel(**inputs):
    x = np.ascontiguousarray(np.asarray(inputs["L0_attn_x"], np.float32))
    g = _prep_inputs(inputs)
    nc = build()
    in_maps = []
    for c in range(8):
        m = dict(g)
        m["x_in"] = np.ascontiguousarray(x[c])
        in_maps.append(m)
    res = run_bass_kernel_spmd(nc, in_maps, core_ids=list(range(8)))
    outs = np.stack([res.results[c]["out"] for c in range(8)], axis=0)
    return outs.astype(np.float32)


# version 11
# speedup vs baseline: 1.1605x; 1.0079x over previous
import contextlib
import numpy as np
import concourse.bass as bass
import concourse.mybir as mybir
from concourse.bass_utils import run_bass_kernel_spmd

F32 = mybir.dt.float32
BF16 = mybir.dt.bfloat16
AF = mybir.ActivationFunctionType
ALU = mybir.AluOpType
AX = mybir.AxisListType

NT = 16
L0A_PH = 3
DBG = False
L0A_SUB = 4
TOK = 2048
D = 1024
NDM = 8
FMW = 2064
FMO = 16


class Sem:
    def __init__(self, h):
        self.h = h
        self.v = 0

    def inc(self, ins, n=1):
        ins.then_inc(self.h, n)
        self.v += n
        return self.v


def build(stages=("l0a", "l0f", "l1a", "l1f"), race=True):
    nc = bass.Bass("TRN2", target_bir_lowering=False,
                   detect_race_conditions=race)

    x_in = nc.dram_tensor("x_in", [TOK, D], F32, kind="ExternalInput").ap()
    out = nc.dram_tensor("out", [TOK, D], F32, kind="ExternalOutput").ap()

    def win(name, shape):
        return nc.dram_tensor(name, list(shape), F32, kind="ExternalInput").ap()

    w_qkv = win("w_qkv", (D, 3 * D))
    b_qkv = win("b_qkv", (1, 3 * D))
    w_out = win("w_out", (D, D))
    b_out = win("b_out", (1, D))
    w_router_t = win("w_router_t", (D, 8))
    wg_t = win("wg_t", (8, D, 2816))
    wu_t = win("wu_t", (8, D, 2816))
    wd_t = win("wd_t", (8, 2816, D))
    ww_t = win("ww_t", (D, D))
    wr_t = win("wr_t", (D, D))
    decay = win("decay", (1, 1))
    log_alpha = win("log_alpha", (1, 1))
    enc_w = win("enc_w", (D, 2048))
    enc_b = win("enc_b", (1, 2048))
    s2c_w = win("s2c_w", (2048, 64))
    s2c_b = win("s2c_b", (1, 64))
    r_w1 = win("r_w1", (64, 64))
    r_b1 = win("r_b1", (1, 64))
    r_w2 = win("r_w2", (64, 8))
    r_b2 = win("r_b2", (1, 8))
    e_w1 = win("e_w1", (8, 64, D))
    e_b1 = win("e_b1", (8, D))
    e_w2 = win("e_w2", (8, D, 64))
    e_b2 = win("e_b2", (1, 8 * 64))
    c2s_w = win("c2s_w", (64, 2048))
    c2s_b = win("c2s_b", (1, 2048))
    dec_w = win("dec_w", (2048, D))
    dec_b = win("dec_b", (1, D))
    ln_g = win("ln_g", (1, D))
    ln_b = win("ln_b", (1, D))

    _sk = "ExternalOutput" if DBG else "Internal"
    if DBG:
        dbg_dram = nc.dram_tensor("dbg_dram", [128, 16384], BF16,
                                  kind="ExternalOutput").ap()
        dbg2_dram = nc.dram_tensor("dbg2_dram", [128, 512], F32,
                                   kind="ExternalOutput").ap()
    xbf_dram = nc.dram_tensor("xbf_dram", [TOK, D], BF16, kind=_sk).ap()
    aux_dram = nc.dram_tensor("aux_dram", [TOK, D], BF16, kind=_sk).ap()
    aux2_dram = nc.dram_tensor("aux2_dram", [TOK, D], BF16, kind=_sk).ap()

    with contextlib.ExitStack() as stack:
        ec = stack.enter_context
        x_tm = ec(nc.sbuf_tensor("x_tm", [128, NT * D], F32))
        x_fm = ec(nc.sbuf_tensor("x_fm", [128, NDM * FMW], BF16))
        rows = ec(nc.sbuf_tensor("rows", [128, 9216], BF16))
        mbf = ec(nc.sbuf_tensor("mbf", [128, 2432], BF16))
        mf32 = ec(nc.sbuf_tensor("mf32", [128, 256], F32))
        onesf = ec(nc.sbuf_tensor("onesf", [1, 128], F32))
        pb = ec(nc.psum_tensor("pb", [128, 8, 512], F32))
        sIN, sW, sTP, sPE, sVE, sSC, sGP, sST, sAX = (
            Sem(ec(nc.semaphore(n))) for n in
            ("sIN", "sW", "sTP", "sPE", "sVE", "sSC", "sGP", "sST", "sAX"))
        sD = [Sem(ec(nc.semaphore(f"sD{i}"))) for i in range(6)]
        sG = [Sem(ec(nc.semaphore(f"sG{i}"))) for i in range(6)]
        sU = Sem(ec(nc.semaphore("sU")))

        ident = mbf[:, 0:128]
        m_t_tile = mbf[:, 128:256]
        gam_bc = mbf[:, 256:384]
        lng_bc = mbf[:, 384:1408]
        lnb_bc = mbf[:, 1408:2432]
        ones_row = rows[0:1, 8960:9088]

        ln_sum = mf32[:, 64:80]
        ln_ssq = mf32[:, 80:96]
        ln_mean = mf32[:, 96:112]
        ln_var = mf32[:, 112:128]
        ln_ms = mf32[:, 128:144]
        sm_max = mf32[:, 144:160]
        sm_den = mf32[:, 160:176]
        c_e6 = mf32[:, 176:177]
        c_e5 = mf32[:, 177:178]
        c_t0 = mf32[:, 38:39]
        c_t1 = mf32[:, 39:40]
        c_lg = mf32[:, 40:41]
        c_gw = mf32[:, 42:43]
        c_gc = mf32[:, 43:44]
        c_al = mf32[:, 44:45]

        def xt(t):
            return x_tm[:, t * D:(t + 1) * D]

        def fm(c, t0, n):
            return x_fm[:, c * FMW + FMO + t0: c * FMW + FMO + t0 + n]

        def ln_scalar_pass(s, tiles, tmp1024, lnSC, lnVE, gated=False):
            s.memzero(ln_ssq[:, 0:len(tiles)])
            for j, (i, ap) in enumerate(tiles):
                s.activation(tmp1024, ap, AF.Square, accum_out=ln_ssq[:, j:j + 1])
            sSC.inc(s.drain())
            s.wait_ge(sVE.h, lnVE + 1)
            s.activation(ln_var[:, 0:len(tiles)], ln_var[:, 0:len(tiles)],
                         AF.Ln, bias=c_e5)
            s.activation(ln_var[:, 0:len(tiles)], ln_var[:, 0:len(tiles)],
                         AF.Exp, scale=-0.5)
            sSC.inc(s.drain())

        def ln_vector_pass(v, tiles, lnSC, lnVE, gated=False):
            n = len(tiles)
            for j, (i, ap) in enumerate(tiles):
                v.tensor_reduce(ln_sum[:, j:j + 1], ap, AX.X, ALU.add)
            v.drain()
            v.wait_ge(sSC.h, lnSC + 1)
            v.tensor_scalar(ln_mean[:, 0:n], ln_sum[:, 0:n], 1.0 / 1024.0, None, ALU.mult)
            v.tensor_scalar(ln_ssq[:, 0:n], ln_ssq[:, 0:n], 1.0 / 1024.0, None, ALU.mult)
            v.drain()
            v.tensor_mul(ln_var[:, 0:n], ln_mean[:, 0:n], ln_mean[:, 0:n])
            v.drain()
            v.tensor_sub(ln_var[:, 0:n], ln_ssq[:, 0:n], ln_var[:, 0:n])
            sVE.inc(v.drain())
            v.wait_ge(sSC.h, lnSC + 2)
            v.tensor_mul(ln_ms[:, 0:n], ln_mean[:, 0:n], ln_var[:, 0:n])
            v.drain()
            for j, (i, ap) in enumerate(tiles):
                v.tensor_scalar(ap, ap, ln_var[:, j:j + 1], ln_ms[:, j:j + 1],
                                ALU.mult, ALU.subtract)
                if gated:
                    v.tensor_tensor(ap, ap, lng_bc, ALU.mult)
                    v.tensor_tensor(ap, ap, lnb_bc, ALU.add)
            sVE.inc(v.drain())

        def xfm_rebuild(tag):
            with (
                nc.sbuf_tensor(f"xr_{tag}", [128, NT * D], BF16) as xr,
                nc.Block(f"xfm_{tag}") as blk,
            ):
                @blk.vector
                def _(v):
                    for t in range(NT):
                        ins = v.tensor_copy(xr[:, t * D:(t + 1) * D], xt(t))
                    sVE.inc(ins)

                @blk.sync
                def _(sy):
                    sy.wait_ge(sVE.h, sVE.v)
                    ins = sy.dma_start(
                        out=xbf_dram.rearrange("(t p) d -> p t d", p=128),
                        in_=xr[:].rearrange("p (t d) -> p t d", d=D))
                    sIN.inc(ins, 16)
                    sy.wait_ge(sIN.h, sIN.v)
                    for c in range(NDM):
                        ins = sy.dma_start_transpose(
                            x_fm[:, c * FMW + FMO:c * FMW + FMO + TOK],
                            xbf_dram[:, c * 128:(c + 1) * 128])
                        sTP.inc(ins, 16)
                    sy.wait_ge(sTP.h, sTP.v)

        with nc.Block("pro") as blk:
            bW, bGP = sW.v, sGP.v

            @blk.sync
            def _(sy):
                ins = sy.dma_start(
                    out=x_tm[:].rearrange("p (t d) -> p t d", d=D),
                    in_=x_in.rearrange("(t p) d -> p t d", p=128))
                sIN.inc(ins, 16)
                sy.wait_ge(sIN.h, sIN.v)

            @blk.gpsimd
            def _(g):
                ins = g.iota(mf32[:, 0:128], [[1, 128]], base=0,
                             channel_multiplier=-1,
                             allow_small_or_imprecise_dtypes=True)
                sGP.inc(ins)
                ins = g.dma_start(out=rows[0:1, 0:3072], in_=b_qkv)
                sW.inc(ins, 16)
                ins = g.dma_start(out=rows[0:1, 3072:4096], in_=b_out)
                sW.inc(ins, 16)
                g.wait_ge(sW.h, bW + 32)

            @blk.vector
            def _(v):
                v.memset(ones_row, 1.0)
                v.memset(onesf[:], 1.0)
                v.memset(c_e6, 1e-6)
                v.memset(c_e5, 1e-5)
                for c in range(NDM):
                    v.memset(x_fm[:, c * FMW:c * FMW + FMO], 0.0)
                v.wait_ge(sGP.h, bGP + 1)
                v.tensor_single_scalar(ident, mf32[:, 0:128], 0.0, ALU.is_equal)

        def emit_l0a():
            xfm_rebuild("l0a")
            NG = 4
            GT = NT // NG
            with (
                nc.sbuf_tensor("dist_bf", [128, NT * 256], BF16) as dist_bf,
                nc.sbuf_tensor("ksqm_bf", [128, NT * 256], BF16) as ksqm_bf,
                nc.sbuf_tensor("cos_bf", [128, NT * 256], BF16) as cos_bf,
                nc.sbuf_tensor("qsq_all", [128, 256], F32) as qsq_all,
                nc.sbuf_tensor("mq_bf", [128, 256], BF16) as mq_bf,
                nc.sbuf_tensor("tmp256", [128, 256], F32) as tmp256,
                nc.sbuf_tensor("stmp256", [128, 256], F32) as stmp256,
                nc.sbuf_tensor("qk256", [128, 256], F32) as qk256,
                nc.sbuf_tensor("pr_t", [128, 1024], BF16) as pr_t,
            ):
                with nc.sbuf_tensor("qkv_g", [128, GT * 3072], BF16) as qkv_g, \
                     nc.sbuf_tensor("wq_s", [128, 2 * 4096], BF16) as wq_s:
                    with nc.Block("l0a_g") as blk:
                        bSC, bPE, bW, bAX = sSC.v, sPE.v, sW.v, sAX.v
                        bVE = sVE.v
                        bD0, bD1 = sD[0].v, sD[1].v
                        bDs = (bD0, bD1)

                        @blk.gpsimd
                        def _(gp):
                            for i in range(4 * 6):
                                if i >= 2:
                                    pg, pp = (i - 2) // 6, (i - 2) % 6
                                    gp.wait_ge(sPE.h, bPE + pg * 24 + 4 * (pp + 1))
                                p = i % 6
                                src = w_qkv.rearrange("(c p) n -> p c n", p=128)[
                                    :, :, 512 * p:512 * (p + 1)]
                                dst = wq_s[:, (i % 2) * 4096:(i % 2) * 4096 + 4096]
                                ins = gp.dma_start(
                                    out=dst.rearrange("p (c n) -> p c n", n=512),
                                    in_=src)
                                sD[i % 2].inc(ins, 16)


                        @blk.tensor
                        def _(t_):
                            for u in range(4 * 24):
                                g, ul = u // 24, u % 24
                                p, t = ul // 4, ul % 4
                                i = g * 6 + p
                                t_.wait_ge(sD[i % 2].h, bDs[i % 2] + 16 * (i // 2 + 1))
                                if u >= 8:
                                    t_.wait_ge(sSC.h, bSC + u - 7)
                                bank = pb[:, u % 8, :]
                                wsl = wq_s[:, (i % 2) * 4096:(i % 2) * 4096 + 4096]
                                for dm in range(NDM):
                                    t_.matmul(bank,
                                              fm(dm, (g * GT + t) * 128, 128),
                                              wsl[:, dm * 512:(dm + 1) * 512],
                                              start=(dm == 0), stop=False)
                                ins = t_.matmul(bank, ones_row,
                                                rows[0:1, 512 * p:512 * (p + 1)],
                                                start=False, stop=True)
                                sPE.inc(ins)

                        @blk.scalar
                        def _(s):
                            for u in range(4 * 24):
                                g, ul = u // 24, u % 24
                                p, t = ul // 4, ul % 4
                                s.wait_ge(sPE.h, bPE + u + 1)
                                if g >= 1:
                                    s.wait_ge(sVE.h, bVE + (g - 1) * 4 + t + 1)
                                    s.wait_ge(sAX.h, bAX + 16 * ((g - 1) * 4 + t + 1))
                                ins = s.copy(
                                    qkv_g[:, t * 3072 + 512 * p:t * 3072 + 512 * (p + 1)],
                                    pb[:, u % 8, :])
                                sSC.inc(ins)

                        @blk.sync
                        def _(sy):
                            def stores(g):
                                for t in range(4):
                                    tg = g * 4 + t
                                    sy.wait_ge(sSC.h, bSC + g * 24 + 21 + t)
                                    ins = sy.dma_start(
                                        out=aux_dram[tg * 128:(tg + 1) * 128, :],
                                        in_=qkv_g[:, t * 3072 + 2048:t * 3072 + 3072])
                                    sAX.inc(ins, 16)

                            for g in range(4):
                                if g >= 1:
                                    stores(g - 1)
                            stores(3)

                        @blk.vector
                        def _(v):
                            for tg in range(16):
                                g, t = tg // 4, tg % 4
                                tile = tg
                                v.wait_ge(sSC.h, bSC + g * 24 + 13 + t)
                                qb = qkv_g[:, t * 3072:t * 3072 + 1024]
                                kb = qkv_g[:, t * 3072 + 1024:t * 3072 + 2048]
                                v.tensor_mul(pr_t[:], qb, qb)
                                v.tensor_reduce(
                                    qsq_all[:, tile * 16:tile * 16 + 16],
                                    pr_t[:].rearrange("p (h d) -> p h d", d=64),
                                    AX.X, ALU.add)
                                v.tensor_mul(pr_t[:], kb, kb)
                                v.tensor_reduce(
                                    mf32[:, 0:16],
                                    pr_t[:].rearrange("p (h d) -> p h d", d=64),
                                    AX.X, ALU.add)
                                v.drain()
                                ksq = ksqm_bf[:, tile * 256:(tile + 1) * 256]
                                for r in range(16):
                                    v.tensor_copy(ksq[:, r * 16:r * 16 + 16 - r],
                                                  mf32[:, r:16])
                                    if r > 0:
                                        v.tensor_copy(ksq[:, r * 16 + 16 - r:(r + 1) * 16],
                                                      mf32[:, 0:r])
                                for r in range(16):
                                    seg = (16 - r) * 64
                                    v.tensor_mul(pr_t[:, 0:seg], qb[:, 0:seg],
                                                 kb[:, r * 64:1024])
                                    if r > 0:
                                        v.tensor_mul(pr_t[:, seg:1024],
                                                     qb[:, seg:1024],
                                                     kb[:, 0:r * 64])
                                    v.tensor_reduce(
                                        qk256[:, r * 16:(r + 1) * 16],
                                        pr_t[:].rearrange("p (h d) -> p h d", d=64),
                                        AX.X, ALU.add)
                                qsq_bc = qsq_all[:, tile * 16:tile * 16 + 16] \
                                    .unsqueeze(1).broadcast_to([128, 16, 16])
                                v.tensor_copy(tmp256[:], ksq)
                                v.tensor_tensor(
                                    tmp256[:].rearrange("p (r h) -> p r h", h=16),
                                    tmp256[:].rearrange("p (r h) -> p r h", h=16),
                                    qsq_bc, ALU.add)
                                v.scalar_tensor_tensor(tmp256[:], qk256[:], -2.0,
                                                       tmp256[:], ALU.mult, ALU.add)
                                v.tensor_scalar(tmp256[:], tmp256[:], 0.0, None,
                                                ALU.max)
                                ins = v.tensor_copy(
                                    dist_bf[:, tile * 256:(tile + 1) * 256], tmp256[:])
                                sVE.inc(ins)

                if DBG:
                    with nc.Block("l0a_dbg0") as blk:
                        @blk.sync
                        def _(sy):
                            ins = sy.dma_start(out=dbg_dram[:, 0:4096], in_=dist_bf[:])
                            sAX.inc(ins, 16)
                            ins = sy.dma_start(out=dbg2_dram[:, 256:512], in_=qk256[:])
                            sAX.inc(ins, 16)
                            ins = sy.dma_start(out=dbg_dram[:, 12544:16384],
                                               in_=ksqm_bf[:, 0:3840])
                            sAX.inc(ins, 16)
                            sy.wait_ge(sAX.h, sAX.v)

                if L0A_PH < 2:
                    return
                with nc.sbuf_tensor("vb_t", [128, 2 * 1024], BF16) as vb_t, \
                     nc.sbuf_tensor("ao_t", [128, 2 * 1024], BF16) as ao_t, \
                     nc.Block("l0a_t") as blk:
                    bSC, bVE, bAX, bIN = sSC.v, sVE.v, sAX.v, sIN.v
                    bD2, bD3, bD4, bD5 = sD[2].v, sD[3].v, sD[4].v, sD[5].v
                    bDp = (bD2, bD3)
                    bDa = (bD4, bD5)

                    @blk.scalar
                    def _(s):
                        for t in range(NT):
                            qs = qsq_all[:, t * 16:t * 16 + 16]
                            s.activation(mf32[:, 16:32], qs, AF.Ln)
                            s.activation(mq_bf[:, t * 16:t * 16 + 16],
                                         mf32[:, 16:32], AF.Exp, scale=0.5)
                            ks = ksqm_bf[:, t * 256:(t + 1) * 256]
                            s.activation(stmp256[:], ks, AF.Ln)
                            s.activation(ks, stmp256[:], AF.Exp, scale=0.5)
                            db = dist_bf[:, t * 256:(t + 1) * 256]
                            s.activation(stmp256[:], db, AF.Ln, bias=c_e6)
                            ins = s.activation(cos_bf[:, t * 256:(t + 1) * 256],
                                               stmp256[:], AF.Exp, scale=0.5)
                            sSC.inc(ins)
                        for t in range(NT):
                            cb = cos_bf[:, t * 256:(t + 1) * 256]
                            ins = s.activation(cb, cb, AF.Sin, scale=1.0 / 32.0)
                            sSC.inc(ins)
                        if L0A_SUB < 2:
                            return
                        for t in range(NT):
                            s.wait_ge(sVE.h, bVE + t + 1)
                            cb = cos_bf[:, t * 256:(t + 1) * 256]
                            ins = s.activation(cb, cb, AF.Exp)
                            sSC.inc(ins)

                    @blk.vector
                    def _(v):
                        for t in range(NT):
                            v.wait_ge(sSC.h, bSC + NT + t + 1)
                            cb = cos_bf[:, t * 256:(t + 1) * 256]
                            db = dist_bf[:, t * 256:(t + 1) * 256]
                            mk = ksqm_bf[:, t * 256:(t + 1) * 256]
                            v.tensor_mul(tmp256[:], cb, cb)
                            v.tensor_scalar(tmp256[:], tmp256[:], -2.0, 1.0,
                                            ALU.mult, ALU.add)
                            for _i in range(4):
                                v.tensor_mul(qk256[:], tmp256[:], tmp256[:])
                                v.tensor_scalar(tmp256[:], qk256[:], 2.0, -1.0,
                                                ALU.mult, ALU.add)
                            v.tensor_scalar(tmp256[:], tmp256[:], 0.15, 1.0,
                                            ALU.mult, ALU.add)
                            v.tensor_copy(qk256[:], db)
                            v.tensor_mul(qk256[:], qk256[:], tmp256[:])
                            v.tensor_scalar(qk256[:], qk256[:], 1e-6, None, ALU.add)
                            v.reciprocal(tmp256[:], qk256[:])
                            mq_bc = mq_bf[:, t * 16:t * 16 + 16] \
                                .unsqueeze(1).broadcast_to([128, 16, 16])
                            v.tensor_tensor(
                                tmp256[:].rearrange("p (r h) -> p r h", h=16),
                                tmp256[:].rearrange("p (r h) -> p r h", h=16),
                                mq_bc, ALU.mult)
                            v.tensor_copy(qk256[:], mk)
                            v.tensor_mul(tmp256[:], tmp256[:], qk256[:])
                            fv = tmp256[:].rearrange("p (r h) -> p h r", h=16)
                            v.tensor_reduce(sm_max[:], fv, AX.X, ALU.max)
                            v.drain()
                            mx_bc = sm_max[:].unsqueeze(1).broadcast_to([128, 16, 16])
                            v.tensor_tensor(
                                tmp256[:].rearrange("p (r h) -> p r h", h=16),
                                tmp256[:].rearrange("p (r h) -> p r h", h=16),
                                mx_bc, ALU.subtract)
                            ins = v.tensor_copy(cos_bf[:, t * 256:(t + 1) * 256],
                                                tmp256[:])
                            sVE.inc(ins)
                        if L0A_SUB < 2:
                            return
                        for t in range(NT):
                            v.wait_ge(sSC.h, bSC + 2 * NT + t + 1)
                            v.wait_ge(sD[2 + t % 2].h, bDp[t % 2] + 16 * (t // 2 + 1))
                            cb = cos_bf[:, t * 256:(t + 1) * 256]
                            nv = cb.rearrange("p (r h) -> p h r", h=16)
                            v.tensor_reduce(sm_den[:], nv, AX.X, ALU.add)
                            v.drain()
                            v.reciprocal(sm_den[:], sm_den[:])
                            v.drain()
                            dn_bc = sm_den[:].unsqueeze(1).broadcast_to([128, 16, 16])
                            if t >= 2:
                                v.wait_ge(sD[4 + t % 2].h, bDa[t % 2] + 16 * (t // 2))
                            at = ksqm_bf[:, t * 256:(t + 1) * 256]
                            v.tensor_copy(tmp256[:], cb)
                            v.tensor_tensor(
                                tmp256[:].rearrange("p (r h) -> p r h", h=16),
                                tmp256[:].rearrange("p (r h) -> p r h", h=16),
                                dn_bc, ALU.mult)
                            v.tensor_copy(at, tmp256[:])
                            vb = vb_t[:, (t % 2) * 1024:(t % 2) * 1024 + 1024]
                            ao = ao_t[:, (t % 2) * 1024:(t % 2) * 1024 + 1024]
                            for r in range(16):
                                seg = (16 - r) * 64
                                dst = ao if r == 0 else pr_t
                                a1 = at[:, r * 16:r * 16 + (16 - r)] \
                                    .unsqueeze(2).broadcast_to([128, 16 - r, 64])
                                v.tensor_tensor(
                                    dst[:, 0:seg].rearrange("p (h d) -> p h d", d=64),
                                    vb[:, r * 64:1024].rearrange("p (h d) -> p h d", d=64),
                                    a1, ALU.mult)
                                if r > 0:
                                    a2 = at[:, r * 16 + 16 - r:(r + 1) * 16] \
                                        .unsqueeze(2).broadcast_to([128, r, 64])
                                    v.tensor_tensor(
                                        dst[:, seg:1024].rearrange("p (h d) -> p h d", d=64),
                                        vb[:, 0:r * 64].rearrange("p (h d) -> p h d", d=64),
                                        a2, ALU.mult)
                                    ins = v.tensor_add(ao, ao, pr_t[:])
                            sVE.inc(ins)

                    @blk.sync
                    def _(sy):
                        if L0A_SUB < 2:
                            return
                        sy.wait_ge(sAX.h, bAX)
                        for t in range(NT + 2):
                            if t >= 2:
                                sy.wait_ge(sVE.h, bVE + NT + (t - 1))
                                ins = sy.dma_start(
                                    out=aux2_dram[(t - 2) * 128:(t - 1) * 128, :],
                                    in_=ao_t[:, (t % 2) * 1024:(t % 2) * 1024 + 1024])
                                sD[4 + t % 2].inc(ins, 16)
                            if t < NT:
                                ins = sy.dma_start(
                                    out=vb_t[:, (t % 2) * 1024:(t % 2) * 1024 + 1024],
                                    in_=aux_dram[t * 128:(t + 1) * 128, :])
                                sD[2 + t % 2].inc(ins, 16)
                        sy.wait_ge(sD[4].h, bD4 + 16 * 8)
                        sy.wait_ge(sD[5].h, bD5 + 16 * 8)

                if DBG:
                    with nc.Block("l0a_dbg") as blk:
                        @blk.sync
                        def _(sy):
                            ins = sy.dma_start(out=dbg_dram[:, 4096:8192], in_=ksqm_bf[:])
                            sAX.inc(ins, 16)
                            ins = sy.dma_start(out=dbg_dram[:, 8192:12288], in_=cos_bf[:])
                            sAX.inc(ins, 16)
                            ins = sy.dma_start(out=dbg_dram[:, 12288:12544], in_=mq_bf[:])
                            sAX.inc(ins, 16)
                            ins = sy.dma_start(out=dbg2_dram[:, 0:256], in_=qsq_all[:])
                            sAX.inc(ins, 16)
                            sy.wait_ge(sAX.h, sAX.v)

                if L0A_PH < 3:
                    return
                with nc.sbuf_tensor("afm", [128, 8 * 1024], BF16) as afm, \
                     nc.sbuf_tensor("wo_bf", [128, 8 * 1024], BF16) as wo_bf, \
                     nc.sbuf_tensor("lntmp", [128, 1024], F32) as lntmp, \
                     nc.Block("l0a_e") as blk:
                    bSC, bVE, bPE, bTP, bW = sSC.v, sVE.v, sPE.v, sTP.v, sW.v

                    @blk.sync
                    def _(sy):
                        for h in range(2):
                            if h == 1:
                                sy.wait_ge(sPE.h, bPE + 16)
                            for c in range(NDM):
                                ins = sy.dma_start_transpose(
                                    afm[:, c * 1024:(c + 1) * 1024],
                                    aux2_dram[h * 1024:(h + 1) * 1024,
                                              c * 128:(c + 1) * 128])
                                sTP.inc(ins, 16)

                    @blk.gpsimd
                    def _(gp):
                        ins = gp.dma_start(
                            out=wo_bf[:].rearrange("p (c n) -> p c n", n=1024),
                            in_=w_out.rearrange("(c p) n -> p c n", p=128))
                        sW.inc(ins, 16)

                    @blk.tensor
                    def _(t_):
                        t_.wait_ge(sW.h, bW + 16)
                        for u in range(2 * NT):
                            t, p = u // 2, u % 2
                            h, tl = t // 8, t % 8
                            t_.wait_ge(sTP.h, bTP + 128 * (h + 1))
                            if u >= 8:
                                t_.wait_ge(sVE.h, bVE + u - 7)
                            bank = pb[:, u % 8, :]
                            for c in range(NDM):
                                t_.matmul(bank,
                                          afm[:, c * 1024 + tl * 128:c * 1024 + (tl + 1) * 128],
                                          wo_bf[:, c * 1024 + p * 512:c * 1024 + (p + 1) * 512],
                                          start=(c == 0), stop=False)
                            ins = t_.matmul(bank, ones_row,
                                            rows[0:1, 3072 + 512 * p:3072 + 512 * (p + 1)],
                                            start=False, stop=True)
                            sPE.inc(ins)

                    @blk.vector
                    def _(v):
                        for u in range(2 * NT):
                            t, p = u // 2, u % 2
                            v.wait_ge(sPE.h, bPE + u + 1)
                            sl = x_tm[:, t * D + p * 512:t * D + (p + 1) * 512]
                            ins = v.tensor_add(sl, sl, pb[:, u % 8, :])
                            sVE.inc(ins)
                        if L0A_SUB >= 4:
                            ln_vector_pass(v, [(t, xt(t)) for t in range(NT)],
                                           bSC, bVE + 2 * NT)

                    @blk.scalar
                    def _(s):
                        if L0A_SUB >= 4:
                            s.wait_ge(sVE.h, bVE + 2 * NT)
                            ln_scalar_pass(s, [(t, xt(t)) for t in range(NT)], lntmp[:],
                                           bSC, bVE + 2 * NT)

        def emit_l0f():
            xfm_rebuild("l0f")
            NI = 22
            with (
                nc.sbuf_tensor("h_fm", [128, NI * 1024], BF16) as h_fm,
                nc.sbuf_tensor("wstr", [128, 14336], BF16) as wstr,
                nc.sbuf_tensor("gate_s", [128, 2 * 1024], BF16) as gate_s,
                nc.sbuf_tensor("comb_t", [128, 128], F32) as comb_t,
                nc.sbuf_tensor("wrt_bf", [128, 64], BF16) as wrt_bf,
                nc.sbuf_tensor("sc_t", [128, 16], F32) as sc_t,
                nc.sbuf_tensor("ustag", [128, 2 * 1024], F32) as ustag,
            ):
                with nc.Block("l0f_r") as blk:
                    bW, bPE, bVE, bSC = sW.v, sPE.v, sVE.v, sSC.v

                    @blk.gpsimd
                    def _(gp):
                        ins = gp.dma_start(
                            out=wrt_bf[:].rearrange("p (c e) -> p c e", e=8),
                            in_=w_router_t.rearrange("(c p) e -> p c e", p=128))
                        sW.inc(ins, 16)

                    @blk.tensor
                    def _(t_):
                        t_.wait_ge(sW.h, bW + 16)
                        for t in range(NT):
                            if t >= 8:
                                t_.wait_ge(sVE.h, bVE + t - 7)
                            bank = pb[:, t % 8, 0:8]
                            for dm in range(NDM):
                                ins = t_.matmul(bank, fm(dm, t * 128, 128),
                                                wrt_bf[:, dm * 8:(dm + 1) * 8],
                                                start=(dm == 0), stop=(dm == NDM - 1))
                            sPE.inc(ins)

                    @blk.vector
                    def _(v):
                        for t in range(NT):
                            v.wait_ge(sPE.h, bPE + t + 1)
                            sc = comb_t[:, t * 8:(t + 1) * 8]
                            v.tensor_reduce(sc_t[:, 0:1], pb[:, t % 8, 0:8], AX.X, ALU.max)
                            v.drain()
                            v.tensor_scalar(sc, pb[:, t % 8, 0:8], sc_t[:, 0:1],
                                            None, ALU.subtract)
                            sVE.inc(v.drain())
                            v.wait_ge(sSC.h, bSC + t + 1)
                            v.tensor_reduce(sc_t[:, 1:2], sc, AX.X, ALU.add)
                            v.drain()
                            v.reciprocal(sc_t[:, 1:2], sc_t[:, 1:2])
                            v.drain()
                            v.tensor_scalar(sc, sc, sc_t[:, 1:2], None, ALU.mult)
                            v.drain()
                            v.tensor_reduce(sc_t[:, 2:3], sc, AX.X, ALU.max)
                            v.drain()
                            v.tensor_scalar(sc_t[:, 4:12], sc, sc_t[:, 2:3], None,
                                            ALU.is_ge)
                            v.drain()
                            v.scalar_tensor_tensor(sc_t[:, 4:12], sc_t[:, 4:12], -1e30,
                                                   sc, ALU.mult, ALU.add)
                            v.drain()
                            v.tensor_reduce(sc_t[:, 3:4], sc_t[:, 4:12], AX.X, ALU.max)
                            v.drain()
                            v.tensor_scalar(sc_t[:, 4:12], sc, sc_t[:, 3:4], None,
                                            ALU.is_ge)
                            v.tensor_add(sc_t[:, 2:3], sc_t[:, 2:3], sc_t[:, 3:4])
                            v.drain()
                            v.tensor_scalar(sc_t[:, 2:3], sc_t[:, 2:3], 1e-20, None,
                                            ALU.add)
                            v.drain()
                            v.reciprocal(sc_t[:, 2:3], sc_t[:, 2:3])
                            v.drain()
                            v.tensor_mul(sc, sc, sc_t[:, 4:12])
                            v.drain()
                            v.tensor_scalar(sc, sc, sc_t[:, 2:3], None, ALU.mult)
                            v.drain()

                    @blk.scalar
                    def _(s):
                        for t in range(NT):
                            s.wait_ge(sVE.h, bVE + t + 1)
                            sc = comb_t[:, t * 8:(t + 1) * 8]
                            s.activation(sc, sc, AF.Exp)
                            sSC.inc(s.drain())

                with nc.Block("l0f_e") as blk:
                    bPE, bVE, bSC = sPE.v, sVE.v, sSC.v
                    bD = [s_.v for s_ in sG]
                    bU = [sD[4].v, sD[5].v]
                    bUc = sU.v

                    def iPE(k):
                        return bPE + 66 * k

                    def iSC(k):
                        return bSC + 22 * k

                    def iVE(k):
                        return bVE + 38 * k

                    wg_ord = {0: 0, 1: 0}
                    wu_ord = {0: 0, 1: 0}
                    wd_ord = {}
                    wg_hist = []
                    wu_hist = []
                    wd_hist = []
                    for k in range(16):
                        for j in range(11):
                            s_ = j % 2
                            wg_ord[s_] += 1
                            wg_hist.append((k, j, s_, wg_ord[s_]))
                            wu_ord[s_] += 1
                            wu_hist.append((k, j, s_, wu_ord[s_]))
                        for g in range(22):
                            s_ = (k * 22 + g) % 3
                            wd_ord[s_] = wd_ord.get(s_, 0) + 1
                            wd_hist.append((k, g, s_, wd_ord[s_]))

                    def wg_need(k, i):
                        j = i // 2
                        for (kk, jj, s_, o_) in wg_hist:
                            if kk == k and jj == j:
                                return s_, o_
                    def wd_need(k, g2):
                        for (kk, gg, s_, o_) in wd_hist:
                            if kk == k and gg == g2:
                                return s_, o_

                    @blk.sync
                    def _(sy):
                        for k in range(16):
                            e, half = k // 2, k % 2
                            for i in range(NI):
                                m = NI * k + i
                                if m >= 2:
                                    sy.wait_ge(sU.h, bUc + m - 1)
                                src_u = wu_t[e].rearrange("(c p) i -> p c i", p=128)[
                                    :, :, i * 128:(i + 1) * 128]
                                dst = ustag[:, (m % 2) * 1024:(m % 2) * 1024 + 1024]
                                ins = sy.dma_start(
                                    out=dst.rearrange("p (c i) -> p c i", i=128),
                                    in_=src_u)
                                sD[4 + m % 2].inc(ins, 16)

                    @blk.gpsimd
                    def _(gp):
                        lastg = {}
                        lastd = {}
                        for k in range(16):
                            e, half = k // 2, k % 2
                            for j in range(11):
                                s_ = j % 2
                                if (0, s_) in lastg:
                                    pk, pj = lastg[(0, s_)]
                                    gp.wait_ge(sPE.h, iPE(pk) + 2 * pj + 2)
                                lastg[(0, s_)] = (k, j)
                                src_g = wg_t[e].rearrange("(c p) i -> p c i", p=128)[
                                    :, :, j * 256:(j + 1) * 256]
                                dst = wstr[:, s_ * 2048:s_ * 2048 + 2048]
                                ins = gp.dma_start(
                                    out=dst.rearrange("p (c i) -> p c i", i=256),
                                    in_=src_g)
                                sG[s_].inc(ins, 16)
                            for g2 in range(22):
                                s_ = (k * 22 + g2) % 3
                                if s_ in lastd:
                                    pk, pg = lastd[s_]
                                    pp, pj = pg // 11, pg % 11
                                    gp.wait_ge(sPE.h, iPE(pk) + 22 + pp * NI + 2 * pj + 2)
                                lastd[s_] = (k, g2)
                                pp, pj = g2 // 11, g2 % 11
                                src_d = wd_t[e][pj * 256:(pj + 1) * 256, :] \
                                    .rearrange("(b p) n -> p b n", p=128)
                                dst = wstr[:, 8192 + s_ * 2048:8192 + s_ * 2048 + 2048]
                                ins = gp.dma_start(
                                    out=dst.rearrange("p (b n) -> p b n", n=1024),
                                    in_=src_d)
                                sG[3 + s_].inc(ins, 16)

                    @blk.tensor
                    def _(t_):
                        for k in range(16):
                            e, half = k // 2, k % 2
                            for i in range(NI):
                                s_, o_ = wg_need(k, i)
                                t_.wait_ge(sG[s_].h, bD[s_] + 16 * o_)
                                t_.wait_ge(sU.h, bUc + NI * k + i + 1)
                                if i >= 2:
                                    t_.wait_ge(sSC.h, iSC(k) + i - 1)
                                    t_.wait_ge(sVE.h, iVE(k) + i - 1)
                                elif k >= 1:
                                    t_.wait_ge(sVE.h, iVE(k) - 6 + 2 * i)
                                gb = (i % 2) * 2
                                ub = 4 + (i % 2) * 2
                                sub = i % 2
                                wg_sl = wstr[:, s_ * 2048:s_ * 2048 + 2048]
                                wu_sl = wstr[:, 4096 + (i % 2) * 1024:
                                             4096 + (i % 2) * 1024 + 1024]
                                for b in range(2):
                                    for dm in range(NDM):
                                        t_.matmul(pb[:, gb + b, :],
                                                  wg_sl[:, dm * 256 + sub * 128:dm * 256 + sub * 128 + 128],
                                                  fm(dm, half * 1024 + b * 512, 512),
                                                  start=(dm == 0), stop=(dm == NDM - 1))
                                if i < 2 and k >= 1:
                                    t_.wait_ge(sVE.h, iVE(k) - 2 + 2 * i)
                                for b in range(2):
                                    for dm in range(NDM):
                                        ins = t_.matmul(pb[:, ub + b, :],
                                                        wu_sl[:, dm * 128:
                                                              dm * 128 + 128],
                                                        fm(dm, half * 1024 + b * 512, 512),
                                                        start=(dm == 0), stop=(dm == NDM - 1))
                                sPE.inc(ins)
                            t_.wait_ge(sVE.h, iVE(k) + NI)
                            t_.wait_ge(sSC.h, iSC(k) + NI)
                            for g in range(2 * NI):
                                p_, i_ = g // NI, g % NI
                                s_, o_ = wd_need(k, (g // NI) * 11 + (g % NI) // 2)
                                t_.wait_ge(sG[3 + s_].h, bD[3 + s_] + 16 * o_)
                                sub = i_ % 2
                                wd_sl = wstr[:, 8192 + s_ * 2048:8192 + s_ * 2048 + 2048]
                                for tl in range(8):
                                    if p_ == 1 and i_ == 0:
                                        t_.wait_ge(sVE.h, iVE(k) + NI + tl + 1)
                                    ins = t_.matmul(pb[:, tl, :],
                                                    h_fm[:, i_ * 1024 + tl * 128:i_ * 1024 + (tl + 1) * 128],
                                                    wd_sl[:, sub * 1024 + p_ * 512:sub * 1024 + (p_ + 1) * 512],
                                                    start=(i_ == 0), stop=(i_ == NI - 1),
                                                    skip_group_check=True)
                                sPE.inc(ins)

                    @blk.scalar
                    def _(s):
                        def cast(k, i):
                            m = NI * k + i
                            s.wait_ge(sD[4 + m % 2].h,
                                      bU[m % 2] + 16 * (m // 2 + 1))
                            if i >= 2:
                                s.wait_ge(sPE.h, iPE(k) + i - 1)
                            elif k >= 1:
                                s.wait_ge(sPE.h, iPE(k - 1) + i + 21)
                            ins = s.copy(
                                wstr[:, 4096 + (i % 2) * 1024:
                                     4096 + (i % 2) * 1024 + 1024],
                                ustag[:, (m % 2) * 1024:(m % 2) * 1024 + 1024])
                            sU.inc(ins)

                        for k in range(16):
                            cast(k, 0)
                            cast(k, 1)
                            for i in range(NI):
                                if i + 2 < NI:
                                    cast(k, i + 2)
                                s.wait_ge(sPE.h, iPE(k) + i + 1)
                                if i >= 2:
                                    s.wait_ge(sVE.h, iVE(k) + i - 1)
                                gb = (i % 2) * 2
                                gs = gate_s[:, (i % 2) * 1024:(i % 2) * 1024 + 1024]
                                s.activation(gs[:, 0:512], pb[:, gb, :], AF.Silu)
                                ins = s.activation(gs[:, 512:1024], pb[:, gb + 1, :],
                                                   AF.Silu)
                                sSC.inc(ins)

                    @blk.vector
                    def _(v):
                        for k in range(16):
                            e, half = k // 2, k % 2
                            for i in range(NI):
                                v.wait_ge(sSC.h, iSC(k) + i + 1)
                                ub = 4 + (i % 2) * 2
                                gs = gate_s[:, (i % 2) * 1024:(i % 2) * 1024 + 1024]
                                v.tensor_mul(h_fm[:, i * 1024:i * 1024 + 512],
                                             gs[:, 0:512], pb[:, ub, :])
                                ins = v.tensor_mul(h_fm[:, i * 1024 + 512:(i + 1) * 1024],
                                                   gs[:, 512:1024], pb[:, ub + 1, :])
                                sVE.inc(ins)
                            for g in range(2 * NI):
                                p_, i_ = g // NI, g % NI
                                if i_ == NI - 1:
                                    for tl in range(8):
                                        v.wait_ge(sPE.h, iPE(k) + 22 + p_ * NI + NI)
                                        t = half * 8 + tl
                                        cc = comb_t[:, t * 8 + e:t * 8 + e + 1]
                                        sl = x_tm[:, t * D + p_ * 512:t * D + (p_ + 1) * 512]
                                        ins = v.scalar_tensor_tensor(
                                            sl, pb[:, tl, :], cc, sl, ALU.mult, ALU.add)
                                        sVE.inc(ins)

            with nc.sbuf_tensor("lntmp2", [128, 1024], F32) as lntmp2, \
                 nc.Block("l0f_ln") as blk:
                bSC, bVE = sSC.v, sVE.v

                @blk.vector
                def _(v):
                    ln_vector_pass(v, [(t, xt(t)) for t in range(NT)], bSC, bVE)

                @blk.scalar
                def _(s):
                    ln_scalar_pass(s, [(t, xt(t)) for t in range(NT)], lntmp2[:],
                                   bSC, bVE)

        def emit_l1a():
            xfm_rebuild("l1a")
            NC = 16
            with (
                nc.sbuf_tensor("WTbf", [128, 8 * 1024], BF16) as WTbf,
                nc.sbuf_tensor("stf", [128, 384], F32) as stf,
                nc.sbuf_tensor("gci", [128, 128], BF16) as gci,
                nc.sbuf_tensor("wrt2", [128, 8 * 1024], BF16) as wrt2,
                nc.sbuf_tensor("v_bf", [128, 1024], BF16) as v_bf,
                nc.sbuf_tensor("vt_bf", [128, 1024], BF16) as vt_bf,
                nc.sbuf_tensor("rk_s", [128, 1024], BF16) as rk_s,
                nc.sbuf_tensor("smt", [128, 128], BF16) as smt,
                nc.sbuf_tensor("rd_bf", [128, 1024], BF16) as rd_bf,
                nc.sbuf_tensor("rd_fm", [128, 2048], BF16) as rd_fm,
                nc.sbuf_tensor("sc11", [128, 2], F32) as sc11,
                nc.sbuf_tensor("wl1a", [128, 10240], BF16) as wl1a,
            ):
                st_t = stf[:, 0:128]
                st2 = stf[:, 128:256]
                st3 = stf[:, 256:384]
                wwt_s = wl1a[:, 0:8192]
                wk_tm = wl1a[:, 8192:10240]
                grow_bf = rows[0:1, 9088:9216]
                with nc.Block("l1a_s") as blk:
                    bIN, bW, bSC, bVE, bPE, bGP = (sIN.v, sW.v, sSC.v, sVE.v,
                                                   sPE.v, sGP.v)

                    @blk.sync
                    def _(sy):
                        ins = sy.dma_start(out=sc11[0:1, 0:1], in_=decay)
                        sIN.inc(ins, 16)
                        ins = sy.dma_start(out=sc11[0:1, 1:2], in_=log_alpha)
                        sIN.inc(ins, 16)

                    @blk.gpsimd
                    def _(gp):
                        ins = gp.dma_start(
                            out=wrt2[:].rearrange("p (c n) -> p c n", n=1024),
                            in_=wr_t.rearrange("(c p) n -> p c n", p=128))
                        sW.inc(ins, 16)
                        ins = gp.dma_start(
                            out=wwt_s[:].rearrange("p (c n) -> p c n", n=1024),
                            in_=ww_t.rearrange("(c p) n -> p c n", p=128))
                        sW.inc(ins, 16)
                        gp.iota(mf32[:, 20:21], [[1, 1]], base=0, channel_multiplier=1,
                                allow_small_or_imprecise_dtypes=True)
                        gp.iota(st2[0:1, 0:128], [[1, 128]], base=0,
                                channel_multiplier=0,
                                allow_small_or_imprecise_dtypes=True)
                        ins = gp.iota(st_t[:], [[1, 128]], base=-1,
                                      channel_multiplier=-1,
                                      allow_small_or_imprecise_dtypes=True)
                        sGP.inc(ins)

                    @blk.scalar
                    def _(s):
                        s.wait_ge(sIN.h, bIN + 32)
                        s.activation(sc11[0:1, 0:1], sc11[0:1, 0:1], AF.Sigmoid)
                        s.activation(sc11[0:1, 0:1], sc11[0:1, 0:1], AF.Ln)
                        sSC.inc(s.drain())
                        s.wait_ge(sVE.h, bVE + 1)
                        s.activation(c_gw, c_t1, AF.Exp)
                        s.activation(c_gc, c_t0, AF.Exp)
                        s.activation(c_al, c_al, AF.Exp)
                        s.activation(st3[:], st3[:], AF.Exp)
                        s.activation(st2[0:1, 0:128], st2[0:1, 0:128], AF.Exp)
                        sSC.inc(s.drain())

                    @blk.tensor
                    def _(t_):
                        t_.wait_ge(sSC.h, bSC + 1)
                        ins = t_.matmul(pb[:, 0, 0:2], onesf[:], sc11[0:1, 0:2],
                                        start=True, stop=True)
                        sPE.inc(ins)
                        t_.wait_ge(sVE.h, bVE + 2)
                        ins = t_.matmul(pb[:, 1, 0:128], ones_row, grow_bf,
                                        start=True, stop=True)
                        sPE.inc(ins)

                    @blk.vector
                    def _(v):
                        v.wait_ge(sPE.h, bPE + 1)
                        v.wait_ge(sGP.h, bGP + 1)
                        v.tensor_copy(c_lg, pb[:, 0, 0:1])
                        v.tensor_copy(c_al, pb[:, 0, 1:2])
                        v.drain()
                        v.tensor_mul(c_t0, c_lg, c_lg)
                        v.tensor_scalar(c_t1, mf32[:, 20:21], -1.0, 127.0,
                                        ALU.mult, ALU.add)
                        v.drain()
                        v.tensor_mul(c_t1, c_t1, c_lg)
                        v.tensor_scalar(c_t0, c_lg, 128.0, None, ALU.mult)
                        v.tensor_scalar(st3[:], st_t[:], c_lg, None, ALU.mult)
                        v.drain()
                        v.tensor_single_scalar(st_t[:], st_t[:], 0.0, ALU.is_ge)
                        v.tensor_scalar(st2[0:1, 0:128], st2[0:1, 0:128],
                                        c_lg[0:1, :], None, ALU.mult)
                        sVE.inc(v.drain())
                        v.wait_ge(sSC.h, bSC + 2)
                        v.tensor_mul(st_t[:], st3[:], st_t[:])
                        v.drain()
                        v.tensor_copy(m_t_tile, st_t[:])
                        v.tensor_copy(grow_bf, st2[0:1, 0:128])
                        sVE.inc(v.drain())
                        v.wait_ge(sPE.h, bPE + 2)
                        v.tensor_copy(gam_bc, pb[:, 1, 0:128])
                        v.drain()
                        v.tensor_scalar(gci[:], ident, c_gc, None, ALU.mult)
                        for ec in range(8):
                            v.memset(WTbf[:, ec * 1024:(ec + 1) * 1024], 0.0)
                        sVE.inc(v.drain())

                with nc.Block("l1a_c") as blk:
                    bPE, bVE, bSC = sPE.v, sVE.v, sSC.v
                    bWl = sW.v - 32
                    bD0, bD1, bD2 = sD[0].v, sD[1].v, sD[2].v
                    bDw = (bD0, bD1)
                    PEC, VEC, SCC = 19, 9, 4

                    def cPE(ch):
                        return bPE + PEC * ch

                    def cVE(ch):
                        return bVE + VEC * ch

                    def cSC(ch):
                        return bSC + SCC * ch

                    @blk.sync
                    def _(sy):
                        for ch in range(NC):
                            if ch >= 2:
                                sy.wait_ge(sPE.h, cPE(ch - 2) + 18)
                            dstw = wk_tm[:, (ch % 2) * 1024:(ch % 2) * 1024 + 1024]
                            if ch == 0:
                                ins = sy.dma_start(out=dstw[1:128, :],
                                                   in_=xbf_dram[0:127, :])
                            else:
                                ins = sy.dma_start(out=dstw,
                                                   in_=xbf_dram[ch * 128 - 1:
                                                                (ch + 1) * 128 - 1, :])
                            sD[ch % 2].inc(ins, 16)
                            sy.wait_ge(sVE.h, cVE(ch) + 4)
                            if ch >= 2:
                                sy.wait_ge(sPE.h, cPE(ch - 2) + 19)
                            for dc in range(NDM):
                                ins = sy.dma_start_transpose(
                                    rd_fm[:, (ch % 2) * 1024 + dc * 128:
                                          (ch % 2) * 1024 + (dc + 1) * 128],
                                    rd_bf[:, dc * 128:(dc + 1) * 128])
                                sD[2].inc(ins, 16)

                    @blk.vector
                    def _(v):
                        v.memset(wk_tm[0:1, 0:1024], 0.0)
                        v.drain()
                        for ch in range(NC):
                            v.wait_ge(sPE.h, cPE(ch) + 8)
                            v.tensor_copy(v_bf[:, 0:512], pb[:, 0, :])
                            v.tensor_copy(v_bf[:, 512:1024], pb[:, 1, :])
                            v.tensor_scalar(vt_bf[:, 0:512], pb[:, 0, :], c_gw, None,
                                            ALU.mult)
                            ins = v.tensor_scalar(vt_bf[:, 512:1024], pb[:, 1, :],
                                                  c_gw, None, ALU.mult)
                            sVE.inc(ins)
                            for ec in range(NDM):
                                ins = v.tensor_mul(rk_s[:, ec * 128:(ec + 1) * 128],
                                                   fm(ec, ch * 128, 128), gam_bc)
                            sVE.inc(ins)
                            v.wait_ge(sPE.h, cPE(ch) + 9)
                            ins = v.tensor_mul(smt[:], pb[:, 4, 0:128], m_t_tile)
                            sVE.inc(ins)
                            v.wait_ge(sPE.h, cPE(ch) + 10)
                            if ch >= 1:
                                v.wait_ge(sD[2].h, bD2 + 16 * 8 * ch)
                            v.tensor_copy(rd_bf[:, 0:512], pb[:, 2, :])
                            ins = v.tensor_copy(rd_bf[:, 512:1024], pb[:, 3, :])
                            sVE.inc(ins)
                            for ec in (0, 2, 4, 6):
                                v.wait_ge(sPE.h, cPE(ch) + 11 + ec)
                                wsl = WTbf[:, ec * 1024:(ec + 1) * 1024]
                                v.tensor_copy(wsl[:, 0:512], pb[:, 0, :])
                                ins = v.tensor_copy(wsl[:, 512:1024], pb[:, 1, :])
                                sVE.inc(ins)
                            v.wait_ge(sPE.h, cPE(ch) + 19)
                            xsl = xt(ch)
                            v.tensor_scalar(xsl, xsl, 2.0, None, ALU.mult)
                            v.scalar_tensor_tensor(xsl[:, 0:512], pb[:, 5, :], c_al,
                                                   xsl[:, 0:512], ALU.mult, ALU.add)
                            ins = v.scalar_tensor_tensor(xsl[:, 512:1024], pb[:, 6, :],
                                                         c_al, xsl[:, 512:1024],
                                                         ALU.mult, ALU.add)
                            sVE.inc(ins)

                    @blk.scalar
                    def _(s):
                        for ch in range(NC):
                            for ec in (1, 3, 5, 7):
                                s.wait_ge(sPE.h, cPE(ch) + 11 + ec)
                                wsl = WTbf[:, ec * 1024:(ec + 1) * 1024]
                                s.copy(wsl[:, 0:512], pb[:, 2, :])
                                ins = s.copy(wsl[:, 512:1024], pb[:, 3, :])
                                sSC.inc(ins)

                    @blk.tensor
                    def _(t_):
                        for ch in range(NC):
                            c0 = ch * 128
                            if ch >= 1:
                                t_.wait_ge(sVE.h, cVE(ch) - 1)
                                t_.wait_ge(sSC.h, cSC(ch))
                            if ch == 0:
                                t_.wait_ge(sW.h, bWl + 16 * 2)
                            for dm in range(NDM):
                                for p in range(2):
                                    ins = t_.matmul(
                                        pb[:, p, :], fm(dm, c0, 128),
                                        wwt_s[:, dm * 1024 + p * 512:
                                              dm * 1024 + (p + 1) * 512],
                                        start=(dm == 0), stop=(dm == NDM - 1),
                                        skip_group_check=True)
                                sPE.inc(ins)
                            for ec in range(NDM):
                                ins = t_.matmul(pb[:, 4, 0:128], fm(ec, c0 - 1, 128),
                                                fm(ec, c0, 128), start=(ec == 0),
                                                stop=(ec == NDM - 1))
                            sPE.inc(ins)
                            t_.wait_ge(sVE.h, cVE(ch) + 2)
                            for ec in range(NDM):
                                for p in range(2):
                                    t_.matmul(pb[:, 2 + p, :],
                                              rk_s[:, ec * 128:(ec + 1) * 128],
                                              WTbf[:, ec * 1024 + p * 512:
                                                   ec * 1024 + (p + 1) * 512],
                                              start=(ec == 0), stop=False,
                                              skip_group_check=True)
                            t_.wait_ge(sVE.h, cVE(ch) + 3)
                            for p in range(2):
                                ins = t_.matmul(pb[:, 2 + p, :], smt[:],
                                                v_bf[:, p * 512:(p + 1) * 512],
                                                start=False, stop=True,
                                                skip_group_check=True)
                            sPE.inc(ins)
                            t_.wait_ge(sD[ch % 2].h, bDw[ch % 2] + 16 * (ch // 2 + 1))
                            t_.wait_ge(sVE.h, cVE(ch) + 4)
                            for ec in range(NDM):
                                bk = (ec % 2) * 2
                                if ec >= 2:
                                    if ec % 2 == 0:
                                        t_.wait_ge(sVE.h, cVE(ch) + 4 + ec // 2)
                                    else:
                                        t_.wait_ge(sSC.h, cSC(ch) + (ec - 1) // 2)
                                for p in range(2):
                                    t_.matmul(pb[:, bk + p, :], gci[:],
                                              WTbf[:, ec * 1024 + p * 512:
                                                   ec * 1024 + (p + 1) * 512],
                                              start=True, stop=False,
                                              skip_group_check=True)
                                for p in range(2):
                                    ins = t_.matmul(
                                        pb[:, bk + p, :],
                                        wk_tm[:, (ch % 2) * 1024 + ec * 128:
                                              (ch % 2) * 1024 + (ec + 1) * 128],
                                        vt_bf[:, p * 512:(p + 1) * 512],
                                        start=False, stop=True, skip_group_check=True)
                                sPE.inc(ins)
                            t_.wait_ge(sVE.h, cVE(ch))
                            t_.wait_ge(sD[2].h, bD2 + 16 * 8 * (ch + 1))
                            for dc in range(NDM):
                                for p in range(2):
                                    ins = t_.matmul(pb[:, 5 + p, :],
                                                    rd_fm[:, (ch % 2) * 1024 + dc * 128:
                                                         (ch % 2) * 1024 + (dc + 1) * 128],
                                                    wrt2[:, dc * 1024 + p * 512:
                                                         dc * 1024 + (p + 1) * 512],
                                                    start=(dc == 0),
                                                    stop=(dc == NDM - 1),
                                                    skip_group_check=True)
                            sPE.inc(ins)

                with nc.Block("l1a_ln") as blk:
                    bSC, bVE = sSC.v, sVE.v

                    @blk.vector
                    def _(v):
                        ln_vector_pass(v, [(t, xt(t)) for t in range(NT)], bSC, bVE)

                    @blk.scalar
                    def _(s):
                        ln_scalar_pass(s, [(t, xt(t)) for t in range(NT)],
                                       v_bf[:], bSC, bVE)

        def emit_l1f():
            xfm_rebuild("l1f")
            NQ, QT = 4, 512
            NI = 16
            with (
                nc.sbuf_tensor("spk", [128, NI * QT], BF16) as spk,
                nc.sbuf_tensor("ehf", [128, 8 * QT], BF16) as ehf,
                nc.sbuf_tensor("ew2", [128, 8 * 8 * 64], BF16) as ew2,
                nc.sbuf_tensor("s2cb_w", [128, NI * 64], BF16) as s2cb_w,
                nc.sbuf_tensor("c2s_bf", [128, 2048], BF16) as c2s_bf,
                nc.sbuf_tensor("rw1_bf", [128, 64], BF16) as rw1_bf,
                nc.sbuf_tensor("rw2_bf", [128, 8], BF16) as rw2_bf,
                nc.sbuf_tensor("cont_bf", [128, QT], BF16) as cont_bf,
                nc.sbuf_tensor("rh_bf", [128, QT], BF16) as rh_bf,
                nc.sbuf_tensor("comb2", [128, 32], F32) as comb2,
                nc.sbuf_tensor("moe_tm", [128, 4 * 64], F32) as moe_tm,
                nc.sbuf_tensor("moe_bf", [128, 512], BF16) as moe_bf,
                nc.sbuf_tensor("mfm", [128, 512], BF16) as mfm,
                nc.sbuf_tensor("dec_s", [128, 4 * 1024], F32) as dec_s,
                nc.sbuf_tensor("ones5", [1, 512], BF16) as ones5,
                nc.sbuf_tensor("sc_q", [128, 16], F32) as sc_q,
                nc.sbuf_tensor("disc", [128, 1024], F32) as disc,
                nc.sbuf_tensor("wl1f", [128, 6144], BF16) as wl1f,
            ):
                rts = spk
                r_encb = rows[0:1, 0:2048]
                r_c2sb = rows[0:1, 2048:4096]
                r_decb = rows[0:1, 4096:5120]
                r_s2cb = rows[0:1, 5120:5184]
                r_rb1 = rows[0:1, 5184:5248]
                r_rb2 = rows[0:1, 5248:5256]
                r_eb1 = rows[0:1, 5256:6280]
                r_eb2 = rows[0:1, 6280:6792]
                r_lng = rows[0:1, 6792:7816]
                r_lnb = rows[0:1, 7816:8840]

                with nc.Block("l1f_s") as blk:
                    bW, bPE, bVE = sW.v, sPE.v, sVE.v

                    @blk.gpsimd
                    def _(gp):
                        for dst, srcd in ((r_encb, enc_b), (r_c2sb, c2s_b),
                                          (r_decb, dec_b), (r_s2cb, s2c_b),
                                          (r_rb1, r_b1), (r_rb2, r_b2),
                                          (r_eb2, e_b2), (r_lng, ln_g),
                                          (r_lnb, ln_b)):
                            ins = gp.dma_start(out=dst, in_=srcd)
                            sW.inc(ins, 16)
                        ins = gp.dma_start(
                            out=s2cb_w[:].rearrange("p (c m) -> p c m", m=64),
                            in_=s2c_w.rearrange("(c p) m -> p c m", p=128))
                        sW.inc(ins, 16)
                        ins = gp.dma_start(
                            out=c2s_bf[0:64, :], in_=c2s_w)
                        sW.inc(ins, 16)
                        ins = gp.dma_start(out=rw1_bf[0:64, :], in_=r_w1)
                        sW.inc(ins, 16)
                        ins = gp.dma_start(out=rw2_bf[0:64, :], in_=r_w2)
                        sW.inc(ins, 16)
                        ins = gp.dma_start(
                            out=ew2[:].rearrange("p (e c d) -> p e c d", e=8, d=64),
                            in_=e_w2.rearrange("e (c p) d -> p e c d", p=128))
                        sW.inc(ins, 16)

                    @blk.vector
                    def _(v):
                        v.memset(ones5[:], 1.0)
                        sVE.inc(v.drain())

                    @blk.tensor
                    def _(t_):
                        t_.wait_ge(sW.h, bW + 16 * 14)
                        t_.wait_ge(sVE.h, bVE + 1)
                        t_.matmul(pb[:, 0, 0:512], ones_row, r_lng[0:1, 0:512],
                                  start=True, stop=True)
                        t_.matmul(pb[:, 1, 0:512], ones_row, r_lng[0:1, 512:1024],
                                  start=True, stop=True)
                        t_.matmul(pb[:, 2, 0:512], ones_row, r_lnb[0:1, 0:512],
                                  start=True, stop=True)
                        ins = t_.matmul(pb[:, 3, 0:512], ones_row, r_lnb[0:1, 512:1024],
                                        start=True, stop=True)
                        sPE.inc(ins)

                    @blk.vector
                    def _(v2):
                        v2.wait_ge(sPE.h, bPE + 1)
                        v2.tensor_copy(lng_bc[:, 0:512], pb[:, 0, :])
                        v2.tensor_copy(lng_bc[:, 512:1024], pb[:, 1, :])
                        v2.tensor_copy(lnb_bc[:, 0:512], pb[:, 2, :])
                        v2.tensor_copy(lnb_bc[:, 512:1024], pb[:, 3, :])
                        v2.drain()

                for q in range(NQ):
                    q0 = q * QT
                    with nc.Block(f"l1f_a{q}") as blk:
                        bPE, bVE, bSC = sPE.v, sVE.v, sSC.v
                        bG = [s_.v for s_ in sG]

                        @blk.gpsimd
                        def _(gp):
                            for i in range(NI):
                                if i >= 2:
                                    gp.wait_ge(sPE.h, bPE + i - 1)
                                src = enc_w.rearrange("(c p) n -> p c n", p=128)[
                                    :, :, i * 128:(i + 1) * 128]
                                dst = wl1f[:, (i % 2) * 1024:(i % 2) * 1024 + 1024]
                                ins = gp.dma_start(
                                    out=dst.rearrange("p (c n) -> p c n", n=128),
                                    in_=src)
                                sG[i % 2].inc(ins, 16)

                        @blk.tensor
                        def _(t_):
                            for i in range(NI):
                                t_.wait_ge(sG[i % 2].h, bG[i % 2] + 16 * (i // 2 + 1))
                                if i >= 2:
                                    t_.wait_ge(sSC.h, bSC + i - 1)
                                bank = pb[:, i % 2, :]
                                wsl = wl1f[:, (i % 2) * 1024:(i % 2) * 1024 + 1024]
                                for dm in range(NDM):
                                    t_.matmul(bank, wsl[:, dm * 128:(dm + 1) * 128],
                                              fm(dm, q0, QT),
                                              start=(dm == 0), stop=False)
                                ins = t_.matmul(bank, r_encb[0:1, i * 128:(i + 1) * 128],
                                                ones5[:], start=False, stop=True)
                                sPE.inc(ins)
                            for i in range(NI):
                                t_.wait_ge(sSC.h, bSC + i + 1)
                                t_.matmul(pb[0:64, 2, 0:512], s2cb_w[:, i * 64:(i + 1) * 64],
                                          spk[:, i * QT:(i + 1) * QT],
                                          start=(i == 0), stop=False,
                                          skip_group_check=True)
                            ins = t_.matmul(pb[0:64, 2, 0:512], r_s2cb, ones5[:],
                                            start=False, stop=True,
                                            skip_group_check=True)
                            sPE.inc(ins)
                            t_.wait_ge(sVE.h, bVE + 1)
                            t_.matmul(pb[0:64, 3, 0:512], rw1_bf[0:64, :],
                                      cont_bf[0:64, :], start=True, stop=False)
                            ins = t_.matmul(pb[0:64, 3, 0:512], r_rb1, ones5[:],
                                            start=False, stop=True)
                            sPE.inc(ins)
                            t_.wait_ge(sSC.h, bSC + NI + 1)
                            for tl in range(4):
                                t_.matmul(pb[:, 4 + tl, 0:8], rh_bf[0:64, tl * 128:(tl + 1) * 128],
                                          rw2_bf[0:64, :], start=True, stop=False)
                                ins = t_.matmul(pb[:, 4 + tl, 0:8], ones_row, r_rb2,
                                                start=False, stop=True)
                                sPE.inc(ins)

                        @blk.scalar
                        def _(s):
                            for i in range(NI):
                                s.wait_ge(sPE.h, bPE + i + 1)
                                ins = s.activation(spk[:, i * QT:(i + 1) * QT],
                                                   pb[:, i % 2, :], AF.Sigmoid)
                                sSC.inc(ins)
                            s.wait_ge(sPE.h, bPE + 18)
                            ins = s.activation(rh_bf[0:64, :], pb[0:64, 3, 0:512],
                                               AF.Relu)
                            sSC.inc(s.drain())
                            for tl in range(4):
                                s.wait_ge(sVE.h, bVE + 1 + tl + 1)
                                sc = comb2[:, tl * 8:(tl + 1) * 8]
                                s.activation(sc, sc, AF.Exp)
                                sSC.inc(s.drain())

                        @blk.vector
                        def _(v):
                            v.memset(moe_tm[:], 0.0)
                            v.memset(moe_bf[:], 0.0)
                            v.wait_ge(sPE.h, bPE + 17)
                            ins = v.tensor_copy(cont_bf[0:64, :], pb[0:64, 2, 0:512])
                            sVE.inc(v.drain())
                            for tl in range(4):
                                v.wait_ge(sPE.h, bPE + 18 + tl + 1)
                                sc = comb2[:, tl * 8:(tl + 1) * 8]
                                v.tensor_reduce(sc_q[:, 0:1], pb[:, 4 + tl, 0:8], AX.X,
                                                ALU.max)
                                v.drain()
                                v.tensor_scalar(sc, pb[:, 4 + tl, 0:8], sc_q[:, 0:1],
                                                None, ALU.subtract)
                                sVE.inc(v.drain())
                                v.wait_ge(sSC.h, bSC + 17 + tl + 1)
                                v.tensor_reduce(sc_q[:, 1:2], sc, AX.X, ALU.add)
                                v.drain()
                                v.reciprocal(sc_q[:, 1:2], sc_q[:, 1:2])
                                v.drain()
                                v.tensor_scalar(sc, sc, sc_q[:, 1:2], None, ALU.mult)
                                v.drain()
                                v.tensor_reduce(sc_q[:, 2:3], sc, AX.X, ALU.max)
                                v.drain()
                                v.tensor_scalar(sc_q[:, 4:12], sc, sc_q[:, 2:3], None,
                                                ALU.is_ge)
                                v.drain()
                                v.scalar_tensor_tensor(sc_q[:, 4:12], sc_q[:, 4:12],
                                                       -1e30, sc, ALU.mult, ALU.add)
                                v.drain()
                                v.tensor_reduce(sc_q[:, 3:4], sc_q[:, 4:12], AX.X, ALU.max)
                                v.drain()
                                v.tensor_scalar(sc_q[:, 4:12], sc, sc_q[:, 3:4], None,
                                                ALU.is_ge)
                                v.tensor_add(sc_q[:, 2:3], sc_q[:, 2:3], sc_q[:, 3:4])
                                v.drain()
                                v.reciprocal(sc_q[:, 2:3], sc_q[:, 2:3])
                                v.drain()
                                v.tensor_mul(sc, sc, sc_q[:, 4:12])
                                v.drain()
                                v.tensor_scalar(sc, sc, sc_q[:, 2:3], None, ALU.mult)
                                v.drain()

                    with nc.Block(f"l1f_b{q}") as blk:
                        bPE, bVE, bSC, bW2 = sPE.v, sVE.v, sSC.v, sW.v
                        bG4, bG5 = sG[4].v, sG[5].v
                        bGe = (bG4, bG5)

                        @blk.gpsimd
                        def _(gp):
                            for e in range(8):
                                if e >= 2:
                                    gp.wait_ge(sPE.h, bPE + 12 * (e - 1))
                                src = e_w1[e]
                                dst = wl1f[:, 2048 + (e % 2) * 1024:
                                           2048 + (e % 2) * 1024 + 1024]
                                ins = gp.dma_start(out=dst[0:64, :], in_=src)
                                sG[4 + e % 2].inc(ins, 16)
                                if e >= 1:
                                    gp.wait_ge(sPE.h, bPE + 12 * (e - 1) + 8)
                                ins = gp.dma_start(out=r_eb1, in_=e_b1[e:e + 1, :])
                                sW.inc(ins, 16)

                        @blk.tensor
                        def _(t_):
                            for e in range(8):
                                t_.wait_ge(sG[4 + e % 2].h,
                                           bGe[e % 2] + 16 * (e // 2 + 1))
                                t_.wait_ge(sW.h, bW2 + 16 * (e + 1))
                                w1 = wl1f[:, 2048 + (e % 2) * 1024:
                                          2048 + (e % 2) * 1024 + 1024]
                                for hc in range(8):
                                    if e * 8 + hc >= 2:
                                        t_.wait_ge(sSC.h, bSC + 8 * e + hc - 1)
                                    bank = pb[:, hc % 2, :]
                                    t_.matmul(bank, w1[0:64, hc * 128:(hc + 1) * 128],
                                              cont_bf[0:64, :], start=True, stop=False)
                                    ins = t_.matmul(bank,
                                                    r_eb1[0:1, hc * 128:(hc + 1) * 128],
                                                    ones5[:], start=False, stop=True)
                                    sPE.inc(ins)
                                t_.wait_ge(sSC.h, bSC + 8 * e + 8)
                                for tl in range(4):
                                    if e >= 1:
                                        t_.wait_ge(sVE.h, bVE + 4 * (e - 1) + tl + 1)
                                    bank = pb[:, 2 + tl, 0:64]
                                    for hc in range(8):
                                        t_.matmul(bank,
                                                  ehf[:, hc * QT + tl * 128:
                                                      hc * QT + (tl + 1) * 128],
                                                  ew2[:, (e * 8 + hc) * 64:
                                                      (e * 8 + hc + 1) * 64],
                                                  start=(hc == 0), stop=False,
                                                  skip_group_check=True)
                                    ins = t_.matmul(bank, ones_row,
                                                    r_eb2[0:1, e * 64:(e + 1) * 64],
                                                    start=False, stop=True,
                                                    skip_group_check=True)
                                    sPE.inc(ins)

                        @blk.scalar
                        def _(s):
                            for e in range(8):
                                for hc in range(8):
                                    s.wait_ge(sPE.h, bPE + 12 * e + hc + 1)
                                    ins = s.activation(
                                        ehf[:, hc * QT:(hc + 1) * QT],
                                        pb[:, hc % 2, :], AF.Relu)
                                    sSC.inc(ins)

                        @blk.vector
                        def _(v):
                            for e in range(8):
                                for tl in range(4):
                                    v.wait_ge(sPE.h, bPE + 12 * e + 8 + tl + 1)
                                    cc = comb2[:, tl * 8 + e:tl * 8 + e + 1]
                                    msl = moe_tm[:, tl * 64:(tl + 1) * 64]
                                    v.scalar_tensor_tensor(msl, pb[:, 2 + tl, 0:64],
                                                           cc, msl, ALU.mult, ALU.add)
                                    sVE.inc(v.drain())

                    with nc.Block(f"l1f_c{q}") as blk:
                        bPE, bVE, bSC, bD3 = sPE.v, sVE.v, sSC.v, sD[3].v
                        bG4, bG5 = sG[4].v, sG[5].v
                        bGe = (bG4, bG5)

                        @blk.vector
                        def _(v):
                            for tl in range(4):
                                v.tensor_copy(moe_bf[:, tl * 128:tl * 128 + 64],
                                              moe_tm[:, tl * 64:(tl + 1) * 64])
                            sVE.inc(v.drain())

                        @blk.sync
                        def _(sy):
                            sy.wait_ge(sVE.h, bVE + 1)
                            for tl in range(4):
                                ins = sy.dma_start_transpose(
                                    mfm[:, tl * 128:(tl + 1) * 128],
                                    moe_bf[:, tl * 128:(tl + 1) * 128])
                                sD[3].inc(ins, 16)

                        @blk.gpsimd
                        def _(gp):
                            for i in range(NI):
                                if i >= 2:
                                    gp.wait_ge(sPE.h, bPE + 16 + i - 1)
                                src = dec_w[i * 128:(i + 1) * 128, :]
                                dst = wl1f[:, 4096 + (i % 2) * 1024:
                                           4096 + (i % 2) * 1024 + 1024]
                                ins = gp.dma_start(out=dst, in_=src)
                                sG[4 + i % 2].inc(ins, 16)

                        @blk.tensor
                        def _(t_):
                            t_.wait_ge(sD[3].h, bD3 + 64)
                            for i in range(NI):
                                if i >= 2:
                                    t_.wait_ge(sSC.h, bSC + i - 1)
                                bank = pb[:, i % 2, :]
                                t_.matmul(bank, c2s_bf[0:64, i * 128:(i + 1) * 128],
                                          mfm[0:64, :], start=True, stop=False)
                                ins = t_.matmul(bank,
                                                r_c2sb[0:1, i * 128:(i + 1) * 128],
                                                ones5[:], start=False, stop=True)
                                sPE.inc(ins)
                            t_.wait_ge(sSC.h, bSC + NI)
                            for i in range(NI):
                                gq = q * NI + i
                                t_.wait_ge(sG[4 + i % 2].h,
                                           bGe[i % 2] + 16 * (i // 2 + 1))
                                dsl = wl1f[:, 4096 + (i % 2) * 1024:
                                           4096 + (i % 2) * 1024 + 1024]
                                for tl in range(4):
                                    for p_ in range(2):
                                        ins = t_.matmul(pb[:, tl * 2 + p_, :],
                                                        rts[:, i * QT + tl * 128:
                                                            i * QT + (tl + 1) * 128],
                                                        dsl[:, p_ * 512:(p_ + 1) * 512],
                                                        start=(i == 0), stop=False,
                                                        skip_group_check=True)
                                sPE.inc(ins)
                            for tl in range(4):
                                for p_ in range(2):
                                    ins = t_.matmul(pb[:, tl * 2 + p_, :], ones_row,
                                                    r_decb[0:1, p_ * 512:(p_ + 1) * 512],
                                                    start=False, stop=True,
                                                    skip_group_check=True)
                                sPE.inc(ins)

                        @blk.scalar
                        def _(s):
                            for i in range(NI):
                                s.wait_ge(sPE.h, bPE + i + 1)
                                ins = s.activation(rts[:, i * QT:(i + 1) * QT],
                                                   pb[:, i % 2, :], AF.Sigmoid)
                                sSC.inc(ins)
                            for tl in range(4):
                                s.wait_ge(sPE.h, bPE + 32 + tl + 1)
                                s.activation(dec_s[:, tl * 1024:tl * 1024 + 512],
                                             pb[:, tl * 2, :], AF.Sigmoid)
                                ins = s.activation(dec_s[:, tl * 1024 + 512:
                                                         (tl + 1) * 1024],
                                                   pb[:, tl * 2 + 1, :], AF.Sigmoid)
                                sSC.inc(ins)
                            tiles = [(j, dec_s[:, j * 1024:(j + 1) * 1024])
                                     for j in range(4)]
                            ln_scalar_pass(s, tiles, disc[:], bSC + 20, bVE + 1)

                        @blk.vector
                        def _(v2):
                            tiles = [(j, dec_s[:, j * 1024:(j + 1) * 1024])
                                     for j in range(4)]
                            v2.wait_ge(sSC.h, bSC + 20)
                            ln_vector_pass(v2, tiles, bSC + 20, bVE + 1, gated=True)
                            for j in range(4):
                                t = q * 4 + j
                                v2.tensor_add(xt(t), xt(t),
                                              dec_s[:, j * 1024:(j + 1) * 1024])
                            v2.drain()

                with nc.Block("l1f_ln") as blk:
                    bSC, bVE = sSC.v, sVE.v

                    @blk.vector
                    def _(v):
                        ln_vector_pass(v, [(t, xt(t)) for t in range(NT)], bSC, bVE)

                    @blk.scalar
                    def _(s):
                        ln_scalar_pass(s, [(t, xt(t)) for t in range(NT)],
                                       disc[:], bSC, bVE)

        if "l0a" in stages:
            emit_l0a()
        if "l0f" in stages:
            emit_l0f()
        if "l1a" in stages:
            emit_l1a()
        if "l1f" in stages:
            emit_l1f()

        with nc.Block("epi") as blk:
            @blk.sync
            def _(sy):
                ins = sy.dma_start(
                    out=out.rearrange("(t p) d -> p t d", p=128),
                    in_=x_tm[:].rearrange("p (t d) -> p t d", d=D))
                sST.inc(ins, 16)
                sy.wait_ge(sST.h, sST.v)
    return nc


def _prep_inputs(inputs):
    f = np.float32
    g = {}
    g["w_qkv"] = np.ascontiguousarray(inputs["L0_attn_W_qkv"], f)
    g["b_qkv"] = np.ascontiguousarray(inputs["L0_attn_b_qkv"], f).reshape(1, -1)
    g["w_out"] = np.ascontiguousarray(inputs["L0_attn_W_out"], f)
    g["b_out"] = np.ascontiguousarray(inputs["L0_attn_b_out"], f).reshape(1, -1)
    g["w_router_t"] = np.ascontiguousarray(inputs["L0_ffn_w_router"].T, f)
    g["wg_t"] = np.ascontiguousarray(inputs["L0_ffn_w_gate"].transpose(0, 2, 1), f)
    g["wu_t"] = np.ascontiguousarray(inputs["L0_ffn_w_up"].transpose(0, 2, 1), f)
    g["wd_t"] = np.ascontiguousarray(inputs["L0_ffn_w_down"].transpose(0, 2, 1), f)
    g["ww_t"] = np.ascontiguousarray(inputs["L1_attn_W_write"].T, f)
    g["wr_t"] = np.ascontiguousarray(inputs["L1_attn_W_read"].T, f)
    g["decay"] = np.asarray(inputs["L1_attn_decay"], f).reshape(1, 1)
    g["log_alpha"] = np.asarray(inputs["L1_attn_log_alpha"], f).reshape(1, 1)
    g["enc_w"] = np.ascontiguousarray(inputs["L1_ffn_enc_w"], f)
    g["enc_b"] = np.ascontiguousarray(inputs["L1_ffn_enc_b"], f).reshape(1, -1)
    g["s2c_w"] = np.ascontiguousarray(inputs["L1_ffn_s2c_w"], f)
    g["s2c_b"] = np.ascontiguousarray(inputs["L1_ffn_s2c_b"], f).reshape(1, -1)
    g["r_w1"] = np.ascontiguousarray(inputs["L1_ffn_r_w1"], f)
    g["r_b1"] = np.ascontiguousarray(inputs["L1_ffn_r_b1"], f).reshape(1, -1)
    g["r_w2"] = np.ascontiguousarray(inputs["L1_ffn_r_w2"], f)
    g["r_b2"] = np.ascontiguousarray(inputs["L1_ffn_r_b2"], f).reshape(1, -1)
    g["e_w1"] = np.ascontiguousarray(inputs["L1_ffn_e_w1"], f)
    g["e_b1"] = np.ascontiguousarray(inputs["L1_ffn_e_b1"], f)
    g["e_w2"] = np.ascontiguousarray(inputs["L1_ffn_e_w2"], f)
    g["e_b2"] = np.ascontiguousarray(inputs["L1_ffn_e_b2"], f).reshape(1, -1)
    g["c2s_w"] = np.ascontiguousarray(inputs["L1_ffn_c2s_w"], f)
    g["c2s_b"] = np.ascontiguousarray(inputs["L1_ffn_c2s_b"], f).reshape(1, -1)
    g["dec_w"] = np.ascontiguousarray(inputs["L1_ffn_dec_w"], f)
    g["dec_b"] = np.ascontiguousarray(inputs["L1_ffn_dec_b"], f).reshape(1, -1)
    g["ln_g"] = np.ascontiguousarray(inputs["L1_ffn_ln_g"], f).reshape(1, -1)
    g["ln_b"] = np.ascontiguousarray(inputs["L1_ffn_ln_b"], f).reshape(1, -1)
    return g


def kernel(**inputs):
    x = np.ascontiguousarray(np.asarray(inputs["L0_attn_x"], np.float32))
    g = _prep_inputs(inputs)
    nc = build()
    in_maps = []
    for c in range(8):
        m = dict(g)
        m["x_in"] = np.ascontiguousarray(x[c])
        in_maps.append(m)
    res = run_bass_kernel_spmd(nc, in_maps, core_ids=list(range(8)))
    outs = np.stack([res.results[c]["out"] for c in range(8)], axis=0)
    return outs.astype(np.float32)


# version 12 (final)
# speedup vs baseline: 1.1711x; 1.0091x over previous
import contextlib
import numpy as np
import concourse.bass as bass
import concourse.mybir as mybir
from concourse.bass_utils import run_bass_kernel_spmd

F32 = mybir.dt.float32
BF16 = mybir.dt.bfloat16
AF = mybir.ActivationFunctionType
ALU = mybir.AluOpType
AX = mybir.AxisListType

NT = 16
L0A_PH = 3
DBG = False
L0A_SUB = 4
TOK = 2048
D = 1024
NDM = 8
FMW = 2064
FMO = 16


class Sem:
    def __init__(self, h):
        self.h = h
        self.v = 0

    def inc(self, ins, n=1):
        ins.then_inc(self.h, n)
        self.v += n
        return self.v


def build(stages=("l0a", "l0f", "l1a", "l1f"), race=True):
    nc = bass.Bass("TRN2", target_bir_lowering=False,
                   detect_race_conditions=race)

    x_in = nc.dram_tensor("x_in", [TOK, D], F32, kind="ExternalInput").ap()
    out = nc.dram_tensor("out", [TOK, D], F32, kind="ExternalOutput").ap()

    def win(name, shape):
        return nc.dram_tensor(name, list(shape), F32, kind="ExternalInput").ap()

    w_qkv = win("w_qkv", (D, 3 * D))
    b_qkv = win("b_qkv", (1, 3 * D))
    w_out = win("w_out", (D, D))
    b_out = win("b_out", (1, D))
    w_router_t = win("w_router_t", (D, 8))
    wg_t = win("wg_t", (8, D, 2816))
    wu_t = win("wu_t", (8, D, 2816))
    wd_t = win("wd_t", (8, 2816, D))
    ww_t = win("ww_t", (D, D))
    wr_t = win("wr_t", (D, D))
    decay = win("decay", (1, 1))
    log_alpha = win("log_alpha", (1, 1))
    enc_w = win("enc_w", (D, 2048))
    enc_b = win("enc_b", (1, 2048))
    s2c_w = win("s2c_w", (2048, 64))
    s2c_b = win("s2c_b", (1, 64))
    r_w1 = win("r_w1", (64, 64))
    r_b1 = win("r_b1", (1, 64))
    r_w2 = win("r_w2", (64, 8))
    r_b2 = win("r_b2", (1, 8))
    e_w1 = win("e_w1", (8, 64, D))
    e_b1 = win("e_b1", (8, D))
    e_w2 = win("e_w2", (8, D, 64))
    e_b2 = win("e_b2", (1, 8 * 64))
    c2s_w = win("c2s_w", (64, 2048))
    c2s_b = win("c2s_b", (1, 2048))
    dec_w = win("dec_w", (2048, D))
    dec_b = win("dec_b", (1, D))
    ln_g = win("ln_g", (1, D))
    ln_b = win("ln_b", (1, D))

    _sk = "ExternalOutput" if DBG else "Internal"
    if DBG:
        dbg_dram = nc.dram_tensor("dbg_dram", [128, 16384], BF16,
                                  kind="ExternalOutput").ap()
        dbg2_dram = nc.dram_tensor("dbg2_dram", [128, 512], F32,
                                   kind="ExternalOutput").ap()
    xbf_dram = nc.dram_tensor("xbf_dram", [TOK, D], BF16, kind=_sk).ap()
    aux_dram = nc.dram_tensor("aux_dram", [TOK, D], BF16, kind=_sk).ap()
    aux2_dram = nc.dram_tensor("aux2_dram", [TOK, D], BF16, kind=_sk).ap()

    with contextlib.ExitStack() as stack:
        ec = stack.enter_context
        x_tm = ec(nc.sbuf_tensor("x_tm", [128, NT * D], F32))
        x_fm = ec(nc.sbuf_tensor("x_fm", [128, NDM * FMW], BF16))
        rows = ec(nc.sbuf_tensor("rows", [128, 9216], BF16))
        mbf = ec(nc.sbuf_tensor("mbf", [128, 2432], BF16))
        mf32 = ec(nc.sbuf_tensor("mf32", [128, 256], F32))
        onesf = ec(nc.sbuf_tensor("onesf", [1, 128], F32))
        pb = ec(nc.psum_tensor("pb", [128, 8, 512], F32))
        sIN, sW, sTP, sPE, sVE, sSC, sGP, sST, sAX = (
            Sem(ec(nc.semaphore(n))) for n in
            ("sIN", "sW", "sTP", "sPE", "sVE", "sSC", "sGP", "sST", "sAX"))
        sD = [Sem(ec(nc.semaphore(f"sD{i}"))) for i in range(6)]
        sG = [Sem(ec(nc.semaphore(f"sG{i}"))) for i in range(6)]
        sU = Sem(ec(nc.semaphore("sU")))

        ident = mbf[:, 0:128]
        m_t_tile = mbf[:, 128:256]
        gam_bc = mbf[:, 256:384]
        lng_bc = mbf[:, 384:1408]
        lnb_bc = mbf[:, 1408:2432]
        ones_row = rows[0:1, 8960:9088]

        ln_sum = mf32[:, 64:80]
        ln_ssq = mf32[:, 80:96]
        ln_mean = mf32[:, 96:112]
        ln_var = mf32[:, 112:128]
        ln_ms = mf32[:, 128:144]
        sm_max = mf32[:, 144:160]
        sm_den = mf32[:, 160:176]
        c_e6 = mf32[:, 176:177]
        c_e5 = mf32[:, 177:178]
        c_t0 = mf32[:, 38:39]
        c_t1 = mf32[:, 39:40]
        c_lg = mf32[:, 40:41]
        c_gw = mf32[:, 42:43]
        c_gc = mf32[:, 43:44]
        c_al = mf32[:, 44:45]

        def xt(t):
            return x_tm[:, t * D:(t + 1) * D]

        def fm(c, t0, n):
            return x_fm[:, c * FMW + FMO + t0: c * FMW + FMO + t0 + n]

        def ln_scalar_pass(s, tiles, tmp1024, lnSC, lnVE, gated=False):
            s.memzero(ln_ssq[:, 0:len(tiles)])
            for j, (i, ap) in enumerate(tiles):
                s.activation(tmp1024, ap, AF.Square, accum_out=ln_ssq[:, j:j + 1])
            sSC.inc(s.drain())
            s.wait_ge(sVE.h, lnVE + 1)
            s.activation(ln_var[:, 0:len(tiles)], ln_var[:, 0:len(tiles)],
                         AF.Ln, bias=c_e5)
            s.activation(ln_var[:, 0:len(tiles)], ln_var[:, 0:len(tiles)],
                         AF.Exp, scale=-0.5)
            sSC.inc(s.drain())

        def ln_vector_pass(v, tiles, lnSC, lnVE, gated=False):
            n = len(tiles)
            for j, (i, ap) in enumerate(tiles):
                v.tensor_reduce(ln_sum[:, j:j + 1], ap, AX.X, ALU.add)
            v.drain()
            v.wait_ge(sSC.h, lnSC + 1)
            v.tensor_scalar(ln_mean[:, 0:n], ln_sum[:, 0:n], 1.0 / 1024.0, None, ALU.mult)
            v.tensor_scalar(ln_ssq[:, 0:n], ln_ssq[:, 0:n], 1.0 / 1024.0, None, ALU.mult)
            v.drain()
            v.tensor_mul(ln_var[:, 0:n], ln_mean[:, 0:n], ln_mean[:, 0:n])
            v.drain()
            v.tensor_sub(ln_var[:, 0:n], ln_ssq[:, 0:n], ln_var[:, 0:n])
            sVE.inc(v.drain())
            v.wait_ge(sSC.h, lnSC + 2)
            v.tensor_mul(ln_ms[:, 0:n], ln_mean[:, 0:n], ln_var[:, 0:n])
            v.drain()
            for j, (i, ap) in enumerate(tiles):
                v.tensor_scalar(ap, ap, ln_var[:, j:j + 1], ln_ms[:, j:j + 1],
                                ALU.mult, ALU.subtract)
                if gated:
                    v.tensor_tensor(ap, ap, lng_bc, ALU.mult)
                    v.tensor_tensor(ap, ap, lnb_bc, ALU.add)
            sVE.inc(v.drain())

        def xfm_rebuild(tag):
            with (
                nc.sbuf_tensor(f"xr_{tag}", [128, NT * D], BF16) as xr,
                nc.Block(f"xfm_{tag}") as blk,
            ):
                @blk.vector
                def _(v):
                    for t in range(NT):
                        ins = v.tensor_copy(xr[:, t * D:(t + 1) * D], xt(t))
                    sVE.inc(ins)

                @blk.sync
                def _(sy):
                    sy.wait_ge(sVE.h, sVE.v)
                    ins = sy.dma_start(
                        out=xbf_dram.rearrange("(t p) d -> p t d", p=128),
                        in_=xr[:].rearrange("p (t d) -> p t d", d=D))
                    sIN.inc(ins, 16)
                    sy.wait_ge(sIN.h, sIN.v)
                    for c in range(NDM):
                        ins = sy.dma_start_transpose(
                            x_fm[:, c * FMW + FMO:c * FMW + FMO + TOK],
                            xbf_dram[:, c * 128:(c + 1) * 128])
                        sTP.inc(ins, 16)
                    sy.wait_ge(sTP.h, sTP.v)

        with nc.Block("pro") as blk:
            bW, bGP = sW.v, sGP.v

            @blk.sync
            def _(sy):
                ins = sy.dma_start(
                    out=x_tm[:].rearrange("p (t d) -> p t d", d=D),
                    in_=x_in.rearrange("(t p) d -> p t d", p=128))
                sIN.inc(ins, 16)
                sy.wait_ge(sIN.h, sIN.v)

            @blk.gpsimd
            def _(g):
                ins = g.iota(mf32[:, 0:128], [[1, 128]], base=0,
                             channel_multiplier=-1,
                             allow_small_or_imprecise_dtypes=True)
                sGP.inc(ins)
                ins = g.dma_start(out=rows[0:1, 0:3072], in_=b_qkv)
                sW.inc(ins, 16)
                ins = g.dma_start(out=rows[0:1, 3072:4096], in_=b_out)
                sW.inc(ins, 16)
                g.wait_ge(sW.h, bW + 32)

            @blk.vector
            def _(v):
                v.memset(ones_row, 1.0)
                v.memset(onesf[:], 1.0)
                v.memset(c_e6, 1e-6)
                v.memset(c_e5, 1e-5)
                for c in range(NDM):
                    v.memset(x_fm[:, c * FMW:c * FMW + FMO], 0.0)
                v.wait_ge(sGP.h, bGP + 1)
                v.tensor_single_scalar(ident, mf32[:, 0:128], 0.0, ALU.is_equal)

        def emit_l0a():
            xfm_rebuild("l0a")
            NG = 4
            GT = NT // NG
            with (
                nc.sbuf_tensor("dist_bf", [128, NT * 256], BF16) as dist_bf,
                nc.sbuf_tensor("ksqm_bf", [128, NT * 256], BF16) as ksqm_bf,
                nc.sbuf_tensor("cos_bf", [128, NT * 256], BF16) as cos_bf,
                nc.sbuf_tensor("qsq_all", [128, 256], F32) as qsq_all,
                nc.sbuf_tensor("mq_bf", [128, 256], BF16) as mq_bf,
                nc.sbuf_tensor("tmp256", [128, 256], F32) as tmp256,
                nc.sbuf_tensor("stmp256", [128, 256], F32) as stmp256,
                nc.sbuf_tensor("qk256", [128, 256], F32) as qk256,
                nc.sbuf_tensor("pr_t", [128, 1024], BF16) as pr_t,
            ):
                with nc.sbuf_tensor("qkv_g", [128, GT * 3072], BF16) as qkv_g, \
                     nc.sbuf_tensor("wq_s", [128, 2 * 4096], BF16) as wq_s:
                    with nc.Block("l0a_g") as blk:
                        bSC, bPE, bW, bAX = sSC.v, sPE.v, sW.v, sAX.v
                        bVE = sVE.v
                        bD0, bD1 = sD[0].v, sD[1].v
                        bDs = (bD0, bD1)

                        @blk.gpsimd
                        def _(gp):
                            for i in range(4 * 6):
                                if i >= 2:
                                    pg, pp = (i - 2) // 6, (i - 2) % 6
                                    gp.wait_ge(sPE.h, bPE + pg * 24 + 4 * (pp + 1))
                                p = i % 6
                                src = w_qkv.rearrange("(c p) n -> p c n", p=128)[
                                    :, :, 512 * p:512 * (p + 1)]
                                dst = wq_s[:, (i % 2) * 4096:(i % 2) * 4096 + 4096]
                                ins = gp.dma_start(
                                    out=dst.rearrange("p (c n) -> p c n", n=512),
                                    in_=src)
                                sD[i % 2].inc(ins, 16)


                        @blk.tensor
                        def _(t_):
                            for u in range(4 * 24):
                                g, ul = u // 24, u % 24
                                p, t = ul // 4, ul % 4
                                i = g * 6 + p
                                t_.wait_ge(sD[i % 2].h, bDs[i % 2] + 16 * (i // 2 + 1))
                                if u >= 8:
                                    t_.wait_ge(sSC.h, bSC + u - 7)
                                bank = pb[:, u % 8, :]
                                wsl = wq_s[:, (i % 2) * 4096:(i % 2) * 4096 + 4096]
                                for dm in range(NDM):
                                    t_.matmul(bank,
                                              fm(dm, (g * GT + t) * 128, 128),
                                              wsl[:, dm * 512:(dm + 1) * 512],
                                              start=(dm == 0), stop=False)
                                ins = t_.matmul(bank, ones_row,
                                                rows[0:1, 512 * p:512 * (p + 1)],
                                                start=False, stop=True)
                                sPE.inc(ins)

                        @blk.scalar
                        def _(s):
                            for u in range(4 * 24):
                                g, ul = u // 24, u % 24
                                p, t = ul // 4, ul % 4
                                s.wait_ge(sPE.h, bPE + u + 1)
                                if g >= 1:
                                    s.wait_ge(sVE.h, bVE + (g - 1) * 4 + t + 1)
                                    s.wait_ge(sAX.h, bAX + 16 * ((g - 1) * 4 + t + 1))
                                ins = s.copy(
                                    qkv_g[:, t * 3072 + 512 * p:t * 3072 + 512 * (p + 1)],
                                    pb[:, u % 8, :])
                                sSC.inc(ins)

                        @blk.sync
                        def _(sy):
                            def stores(g):
                                for t in range(4):
                                    tg = g * 4 + t
                                    sy.wait_ge(sSC.h, bSC + g * 24 + 21 + t)
                                    ins = sy.dma_start(
                                        out=aux_dram[tg * 128:(tg + 1) * 128, :],
                                        in_=qkv_g[:, t * 3072 + 2048:t * 3072 + 3072])
                                    sAX.inc(ins, 16)

                            for g in range(4):
                                if g >= 1:
                                    stores(g - 1)
                            stores(3)

                        @blk.vector
                        def _(v):
                            for tg in range(16):
                                g, t = tg // 4, tg % 4
                                tile = tg
                                v.wait_ge(sSC.h, bSC + g * 24 + 13 + t)
                                qb = qkv_g[:, t * 3072:t * 3072 + 1024]
                                kb = qkv_g[:, t * 3072 + 1024:t * 3072 + 2048]
                                v.tensor_mul(pr_t[:], qb, qb)
                                v.tensor_reduce(
                                    qsq_all[:, tile * 16:tile * 16 + 16],
                                    pr_t[:].rearrange("p (h d) -> p h d", d=64),
                                    AX.X, ALU.add)
                                v.tensor_mul(pr_t[:], kb, kb)
                                v.tensor_reduce(
                                    mf32[:, 0:16],
                                    pr_t[:].rearrange("p (h d) -> p h d", d=64),
                                    AX.X, ALU.add)
                                v.drain()
                                ksq = ksqm_bf[:, tile * 256:(tile + 1) * 256]
                                for r in range(16):
                                    v.tensor_copy(ksq[:, r * 16:r * 16 + 16 - r],
                                                  mf32[:, r:16])
                                    if r > 0:
                                        v.tensor_copy(ksq[:, r * 16 + 16 - r:(r + 1) * 16],
                                                      mf32[:, 0:r])
                                for r in range(16):
                                    seg = (16 - r) * 64
                                    v.tensor_mul(pr_t[:, 0:seg], qb[:, 0:seg],
                                                 kb[:, r * 64:1024])
                                    if r > 0:
                                        v.tensor_mul(pr_t[:, seg:1024],
                                                     qb[:, seg:1024],
                                                     kb[:, 0:r * 64])
                                    v.tensor_reduce(
                                        qk256[:, r * 16:(r + 1) * 16],
                                        pr_t[:].rearrange("p (h d) -> p h d", d=64),
                                        AX.X, ALU.add)
                                qsq_bc = qsq_all[:, tile * 16:tile * 16 + 16] \
                                    .unsqueeze(1).broadcast_to([128, 16, 16])
                                v.tensor_copy(tmp256[:], ksq)
                                v.tensor_tensor(
                                    tmp256[:].rearrange("p (r h) -> p r h", h=16),
                                    tmp256[:].rearrange("p (r h) -> p r h", h=16),
                                    qsq_bc, ALU.add)
                                v.scalar_tensor_tensor(tmp256[:], qk256[:], -2.0,
                                                       tmp256[:], ALU.mult, ALU.add)
                                v.tensor_scalar(tmp256[:], tmp256[:], 0.0, None,
                                                ALU.max)
                                ins = v.tensor_copy(
                                    dist_bf[:, tile * 256:(tile + 1) * 256], tmp256[:])
                                sVE.inc(ins)

                if DBG:
                    with nc.Block("l0a_dbg0") as blk:
                        @blk.sync
                        def _(sy):
                            ins = sy.dma_start(out=dbg_dram[:, 0:4096], in_=dist_bf[:])
                            sAX.inc(ins, 16)
                            ins = sy.dma_start(out=dbg2_dram[:, 256:512], in_=qk256[:])
                            sAX.inc(ins, 16)
                            ins = sy.dma_start(out=dbg_dram[:, 12544:16384],
                                               in_=ksqm_bf[:, 0:3840])
                            sAX.inc(ins, 16)
                            sy.wait_ge(sAX.h, sAX.v)

                if L0A_PH < 2:
                    return
                with nc.sbuf_tensor("vb_t", [128, 2 * 1024], BF16) as vb_t, \
                     nc.sbuf_tensor("ao_t", [128, 2 * 1024], BF16) as ao_t, \
                     nc.Block("l0a_t") as blk:
                    bSC, bVE, bAX, bIN = sSC.v, sVE.v, sAX.v, sIN.v
                    bD2, bD3, bD4, bD5 = sD[2].v, sD[3].v, sD[4].v, sD[5].v
                    bDp = (bD2, bD3)
                    bDa = (bD4, bD5)

                    @blk.scalar
                    def _(s):
                        for t in range(NT):
                            qs = qsq_all[:, t * 16:t * 16 + 16]
                            s.activation(mf32[:, 16:32], qs, AF.Ln)
                            s.activation(mq_bf[:, t * 16:t * 16 + 16],
                                         mf32[:, 16:32], AF.Exp, scale=0.5)
                            ks = ksqm_bf[:, t * 256:(t + 1) * 256]
                            s.activation(stmp256[:], ks, AF.Ln)
                            s.activation(ks, stmp256[:], AF.Exp, scale=0.5)
                            db = dist_bf[:, t * 256:(t + 1) * 256]
                            s.activation(stmp256[:], db, AF.Ln, bias=c_e6)
                            ins = s.activation(cos_bf[:, t * 256:(t + 1) * 256],
                                               stmp256[:], AF.Exp, scale=0.5)
                            sSC.inc(ins)
                        for t in range(NT):
                            cb = cos_bf[:, t * 256:(t + 1) * 256]
                            ins = s.activation(cb, cb, AF.Sin, scale=1.0 / 32.0)
                            sSC.inc(ins)
                        if L0A_SUB < 2:
                            return
                        for t in range(NT):
                            s.wait_ge(sVE.h, bVE + t + 1)
                            cb = cos_bf[:, t * 256:(t + 1) * 256]
                            ins = s.activation(cb, cb, AF.Exp)
                            sSC.inc(ins)

                    @blk.vector
                    def _(v):
                        for t in range(NT):
                            v.wait_ge(sSC.h, bSC + NT + t + 1)
                            cb = cos_bf[:, t * 256:(t + 1) * 256]
                            db = dist_bf[:, t * 256:(t + 1) * 256]
                            mk = ksqm_bf[:, t * 256:(t + 1) * 256]
                            v.tensor_mul(tmp256[:], cb, cb)
                            v.tensor_scalar(tmp256[:], tmp256[:], -2.0, 1.0,
                                            ALU.mult, ALU.add)
                            for _i in range(4):
                                v.tensor_mul(qk256[:], tmp256[:], tmp256[:])
                                v.tensor_scalar(tmp256[:], qk256[:], 2.0, -1.0,
                                                ALU.mult, ALU.add)
                            v.tensor_scalar(tmp256[:], tmp256[:], 0.15, 1.0,
                                            ALU.mult, ALU.add)
                            v.tensor_copy(qk256[:], db)
                            v.tensor_mul(qk256[:], qk256[:], tmp256[:])
                            v.tensor_scalar(qk256[:], qk256[:], 1e-6, None, ALU.add)
                            v.reciprocal(tmp256[:], qk256[:])
                            mq_bc = mq_bf[:, t * 16:t * 16 + 16] \
                                .unsqueeze(1).broadcast_to([128, 16, 16])
                            v.tensor_tensor(
                                tmp256[:].rearrange("p (r h) -> p r h", h=16),
                                tmp256[:].rearrange("p (r h) -> p r h", h=16),
                                mq_bc, ALU.mult)
                            v.tensor_copy(qk256[:], mk)
                            v.tensor_mul(tmp256[:], tmp256[:], qk256[:])
                            fv = tmp256[:].rearrange("p (r h) -> p h r", h=16)
                            v.tensor_reduce(sm_max[:], fv, AX.X, ALU.max)
                            v.drain()
                            mx_bc = sm_max[:].unsqueeze(1).broadcast_to([128, 16, 16])
                            v.tensor_tensor(
                                tmp256[:].rearrange("p (r h) -> p r h", h=16),
                                tmp256[:].rearrange("p (r h) -> p r h", h=16),
                                mx_bc, ALU.subtract)
                            ins = v.tensor_copy(cos_bf[:, t * 256:(t + 1) * 256],
                                                tmp256[:])
                            sVE.inc(ins)
                        if L0A_SUB < 2:
                            return
                        for t in range(NT):
                            v.wait_ge(sSC.h, bSC + 2 * NT + t + 1)
                            v.wait_ge(sD[2 + t % 2].h, bDp[t % 2] + 16 * (t // 2 + 1))
                            cb = cos_bf[:, t * 256:(t + 1) * 256]
                            nv = cb.rearrange("p (r h) -> p h r", h=16)
                            v.tensor_reduce(sm_den[:], nv, AX.X, ALU.add)
                            v.drain()
                            v.reciprocal(sm_den[:], sm_den[:])
                            v.drain()
                            dn_bc = sm_den[:].unsqueeze(1).broadcast_to([128, 16, 16])
                            if t >= 2:
                                v.wait_ge(sD[4 + t % 2].h, bDa[t % 2] + 16 * (t // 2))
                            at = ksqm_bf[:, t * 256:(t + 1) * 256]
                            v.tensor_copy(tmp256[:], cb)
                            v.tensor_tensor(
                                tmp256[:].rearrange("p (r h) -> p r h", h=16),
                                tmp256[:].rearrange("p (r h) -> p r h", h=16),
                                dn_bc, ALU.mult)
                            v.tensor_copy(at, tmp256[:])
                            vb = vb_t[:, (t % 2) * 1024:(t % 2) * 1024 + 1024]
                            ao = ao_t[:, (t % 2) * 1024:(t % 2) * 1024 + 1024]
                            for r in range(16):
                                seg = (16 - r) * 64
                                dst = ao if r == 0 else pr_t
                                a1 = at[:, r * 16:r * 16 + (16 - r)] \
                                    .unsqueeze(2).broadcast_to([128, 16 - r, 64])
                                v.tensor_tensor(
                                    dst[:, 0:seg].rearrange("p (h d) -> p h d", d=64),
                                    vb[:, r * 64:1024].rearrange("p (h d) -> p h d", d=64),
                                    a1, ALU.mult)
                                if r > 0:
                                    a2 = at[:, r * 16 + 16 - r:(r + 1) * 16] \
                                        .unsqueeze(2).broadcast_to([128, r, 64])
                                    v.tensor_tensor(
                                        dst[:, seg:1024].rearrange("p (h d) -> p h d", d=64),
                                        vb[:, 0:r * 64].rearrange("p (h d) -> p h d", d=64),
                                        a2, ALU.mult)
                                    ins = v.tensor_add(ao, ao, pr_t[:])
                            sVE.inc(ins)

                    @blk.sync
                    def _(sy):
                        if L0A_SUB < 2:
                            return
                        sy.wait_ge(sAX.h, bAX)
                        for t in range(NT + 2):
                            if t >= 2:
                                sy.wait_ge(sVE.h, bVE + NT + (t - 1))
                                ins = sy.dma_start(
                                    out=aux2_dram[(t - 2) * 128:(t - 1) * 128, :],
                                    in_=ao_t[:, (t % 2) * 1024:(t % 2) * 1024 + 1024])
                                sD[4 + t % 2].inc(ins, 16)
                            if t < NT:
                                ins = sy.dma_start(
                                    out=vb_t[:, (t % 2) * 1024:(t % 2) * 1024 + 1024],
                                    in_=aux_dram[t * 128:(t + 1) * 128, :])
                                sD[2 + t % 2].inc(ins, 16)
                        sy.wait_ge(sD[4].h, bD4 + 16 * 8)
                        sy.wait_ge(sD[5].h, bD5 + 16 * 8)

                if DBG:
                    with nc.Block("l0a_dbg") as blk:
                        @blk.sync
                        def _(sy):
                            ins = sy.dma_start(out=dbg_dram[:, 4096:8192], in_=ksqm_bf[:])
                            sAX.inc(ins, 16)
                            ins = sy.dma_start(out=dbg_dram[:, 8192:12288], in_=cos_bf[:])
                            sAX.inc(ins, 16)
                            ins = sy.dma_start(out=dbg_dram[:, 12288:12544], in_=mq_bf[:])
                            sAX.inc(ins, 16)
                            ins = sy.dma_start(out=dbg2_dram[:, 0:256], in_=qsq_all[:])
                            sAX.inc(ins, 16)
                            sy.wait_ge(sAX.h, sAX.v)

                if L0A_PH < 3:
                    return
                with nc.sbuf_tensor("afm", [128, 8 * 1024], BF16) as afm, \
                     nc.sbuf_tensor("wo_bf", [128, 8 * 1024], BF16) as wo_bf, \
                     nc.sbuf_tensor("lntmp", [128, 1024], F32) as lntmp, \
                     nc.Block("l0a_e") as blk:
                    bSC, bVE, bPE, bTP, bW = sSC.v, sVE.v, sPE.v, sTP.v, sW.v

                    @blk.sync
                    def _(sy):
                        for h in range(2):
                            if h == 1:
                                sy.wait_ge(sPE.h, bPE + 16)
                            for c in range(NDM):
                                ins = sy.dma_start_transpose(
                                    afm[:, c * 1024:(c + 1) * 1024],
                                    aux2_dram[h * 1024:(h + 1) * 1024,
                                              c * 128:(c + 1) * 128])
                                sTP.inc(ins, 16)

                    @blk.gpsimd
                    def _(gp):
                        ins = gp.dma_start(
                            out=wo_bf[:].rearrange("p (c n) -> p c n", n=1024),
                            in_=w_out.rearrange("(c p) n -> p c n", p=128))
                        sW.inc(ins, 16)

                    @blk.tensor
                    def _(t_):
                        t_.wait_ge(sW.h, bW + 16)
                        for u in range(2 * NT):
                            t, p = u // 2, u % 2
                            h, tl = t // 8, t % 8
                            t_.wait_ge(sTP.h, bTP + 128 * (h + 1))
                            if u >= 8:
                                t_.wait_ge(sVE.h, bVE + u - 7)
                            bank = pb[:, u % 8, :]
                            for c in range(NDM):
                                t_.matmul(bank,
                                          afm[:, c * 1024 + tl * 128:c * 1024 + (tl + 1) * 128],
                                          wo_bf[:, c * 1024 + p * 512:c * 1024 + (p + 1) * 512],
                                          start=(c == 0), stop=False)
                            ins = t_.matmul(bank, ones_row,
                                            rows[0:1, 3072 + 512 * p:3072 + 512 * (p + 1)],
                                            start=False, stop=True)
                            sPE.inc(ins)

                    @blk.vector
                    def _(v):
                        for u in range(2 * NT):
                            t, p = u // 2, u % 2
                            v.wait_ge(sPE.h, bPE + u + 1)
                            sl = x_tm[:, t * D + p * 512:t * D + (p + 1) * 512]
                            ins = v.tensor_add(sl, sl, pb[:, u % 8, :])
                            sVE.inc(ins)
                        if L0A_SUB >= 4:
                            ln_vector_pass(v, [(t, xt(t)) for t in range(NT)],
                                           bSC, bVE + 2 * NT)

                    @blk.scalar
                    def _(s):
                        if L0A_SUB >= 4:
                            s.wait_ge(sVE.h, bVE + 2 * NT)
                            ln_scalar_pass(s, [(t, xt(t)) for t in range(NT)], lntmp[:],
                                           bSC, bVE + 2 * NT)

        def emit_l0f():
            xfm_rebuild("l0f")
            NI = 22
            with (
                nc.sbuf_tensor("h_fm", [128, NI * 1024], BF16) as h_fm,
                nc.sbuf_tensor("wstr", [128, 14336], BF16) as wstr,
                nc.sbuf_tensor("gate_s", [128, 2 * 1024], BF16) as gate_s,
                nc.sbuf_tensor("comb_t", [128, 128], F32) as comb_t,
                nc.sbuf_tensor("wrt_bf", [128, 64], BF16) as wrt_bf,
                nc.sbuf_tensor("sc_t", [128, 16], F32) as sc_t,
                nc.sbuf_tensor("ustag", [128, 2 * 1024], F32) as ustag,
            ):
                with nc.Block("l0f_r") as blk:
                    bW, bPE, bVE, bSC = sW.v, sPE.v, sVE.v, sSC.v

                    @blk.gpsimd
                    def _(gp):
                        ins = gp.dma_start(
                            out=wrt_bf[:].rearrange("p (c e) -> p c e", e=8),
                            in_=w_router_t.rearrange("(c p) e -> p c e", p=128))
                        sW.inc(ins, 16)

                    @blk.tensor
                    def _(t_):
                        t_.wait_ge(sW.h, bW + 16)
                        for t in range(NT):
                            if t >= 8:
                                t_.wait_ge(sVE.h, bVE + t - 7)
                            bank = pb[:, t % 8, 0:8]
                            for dm in range(NDM):
                                ins = t_.matmul(bank, fm(dm, t * 128, 128),
                                                wrt_bf[:, dm * 8:(dm + 1) * 8],
                                                start=(dm == 0), stop=(dm == NDM - 1))
                            sPE.inc(ins)

                    @blk.vector
                    def _(v):
                        for t in range(NT):
                            v.wait_ge(sPE.h, bPE + t + 1)
                            sc = comb_t[:, t * 8:(t + 1) * 8]
                            v.tensor_reduce(sc_t[:, 0:1], pb[:, t % 8, 0:8], AX.X, ALU.max)
                            v.drain()
                            v.tensor_scalar(sc, pb[:, t % 8, 0:8], sc_t[:, 0:1],
                                            None, ALU.subtract)
                            sVE.inc(v.drain())
                            v.wait_ge(sSC.h, bSC + t + 1)
                            v.tensor_reduce(sc_t[:, 1:2], sc, AX.X, ALU.add)
                            v.drain()
                            v.reciprocal(sc_t[:, 1:2], sc_t[:, 1:2])
                            v.drain()
                            v.tensor_scalar(sc, sc, sc_t[:, 1:2], None, ALU.mult)
                            v.drain()
                            v.tensor_reduce(sc_t[:, 2:3], sc, AX.X, ALU.max)
                            v.drain()
                            v.tensor_scalar(sc_t[:, 4:12], sc, sc_t[:, 2:3], None,
                                            ALU.is_ge)
                            v.drain()
                            v.scalar_tensor_tensor(sc_t[:, 4:12], sc_t[:, 4:12], -1e30,
                                                   sc, ALU.mult, ALU.add)
                            v.drain()
                            v.tensor_reduce(sc_t[:, 3:4], sc_t[:, 4:12], AX.X, ALU.max)
                            v.drain()
                            v.tensor_scalar(sc_t[:, 4:12], sc, sc_t[:, 3:4], None,
                                            ALU.is_ge)
                            v.tensor_add(sc_t[:, 2:3], sc_t[:, 2:3], sc_t[:, 3:4])
                            v.drain()
                            v.tensor_scalar(sc_t[:, 2:3], sc_t[:, 2:3], 1e-20, None,
                                            ALU.add)
                            v.drain()
                            v.reciprocal(sc_t[:, 2:3], sc_t[:, 2:3])
                            v.drain()
                            v.tensor_mul(sc, sc, sc_t[:, 4:12])
                            v.drain()
                            v.tensor_scalar(sc, sc, sc_t[:, 2:3], None, ALU.mult)
                            v.drain()

                    @blk.scalar
                    def _(s):
                        for t in range(NT):
                            s.wait_ge(sVE.h, bVE + t + 1)
                            sc = comb_t[:, t * 8:(t + 1) * 8]
                            s.activation(sc, sc, AF.Exp)
                            sSC.inc(s.drain())

                with nc.Block("l0f_e") as blk:
                    bPE, bVE, bSC = sPE.v, sVE.v, sSC.v
                    bD = [s_.v for s_ in sG]
                    bU = [sD[4].v, sD[5].v]
                    bUc = sU.v

                    def iPE(k):
                        return bPE + 66 * k

                    def iSC(k):
                        return bSC + 22 * k

                    def iVE(k):
                        return bVE + 38 * k

                    wg_ord = {0: 0, 1: 0}
                    wu_ord = {0: 0, 1: 0}
                    wd_ord = {}
                    wg_hist = []
                    wu_hist = []
                    wd_hist = []
                    for k in range(16):
                        for j in range(11):
                            s_ = j % 2
                            wg_ord[s_] += 1
                            wg_hist.append((k, j, s_, wg_ord[s_]))
                            wu_ord[s_] += 1
                            wu_hist.append((k, j, s_, wu_ord[s_]))
                        for g in range(22):
                            s_ = (k * 22 + g) % 3
                            wd_ord[s_] = wd_ord.get(s_, 0) + 1
                            wd_hist.append((k, g, s_, wd_ord[s_]))

                    def wg_need(k, i):
                        j = i // 2
                        for (kk, jj, s_, o_) in wg_hist:
                            if kk == k and jj == j:
                                return s_, o_
                    def wd_need(k, g2):
                        for (kk, gg, s_, o_) in wd_hist:
                            if kk == k and gg == g2:
                                return s_, o_

                    @blk.sync
                    def _(sy):
                        for k in range(16):
                            e, half = k // 2, k % 2
                            for i in range(NI):
                                m = NI * k + i
                                if m >= 2:
                                    sy.wait_ge(sU.h, bUc + m - 1)
                                src_u = wu_t[e].rearrange("(c p) i -> p c i", p=128)[
                                    :, :, i * 128:(i + 1) * 128]
                                dst = ustag[:, (m % 2) * 1024:(m % 2) * 1024 + 1024]
                                ins = sy.dma_start(
                                    out=dst.rearrange("p (c i) -> p c i", i=128),
                                    in_=src_u)
                                sD[4 + m % 2].inc(ins, 16)

                    @blk.gpsimd
                    def _(gp):
                        lastg = {}
                        lastd = {}
                        for k in range(16):
                            e, half = k // 2, k % 2
                            for j in range(11):
                                s_ = j % 2
                                if (0, s_) in lastg:
                                    pk, pj = lastg[(0, s_)]
                                    gp.wait_ge(sPE.h, iPE(pk) + 2 * pj + 2)
                                lastg[(0, s_)] = (k, j)
                                src_g = wg_t[e].rearrange("(c p) i -> p c i", p=128)[
                                    :, :, j * 256:(j + 1) * 256]
                                dst = wstr[:, s_ * 2048:s_ * 2048 + 2048]
                                ins = gp.dma_start(
                                    out=dst.rearrange("p (c i) -> p c i", i=256),
                                    in_=src_g)
                                sG[s_].inc(ins, 16)
                            for g2 in range(22):
                                s_ = (k * 22 + g2) % 3
                                if s_ in lastd:
                                    pk, pg = lastd[s_]
                                    pp, pj = pg // 11, pg % 11
                                    gp.wait_ge(sPE.h, iPE(pk) + 22 + pp * NI + 2 * pj + 2)
                                lastd[s_] = (k, g2)
                                pp, pj = g2 // 11, g2 % 11
                                src_d = wd_t[e][pj * 256:(pj + 1) * 256, :] \
                                    .rearrange("(b p) n -> p b n", p=128)
                                dst = wstr[:, 8192 + s_ * 2048:8192 + s_ * 2048 + 2048]
                                ins = gp.dma_start(
                                    out=dst.rearrange("p (b n) -> p b n", n=1024),
                                    in_=src_d)
                                sG[3 + s_].inc(ins, 16)

                    @blk.tensor
                    def _(t_):
                        for k in range(16):
                            e, half = k // 2, k % 2
                            for i in range(NI):
                                s_, o_ = wg_need(k, i)
                                t_.wait_ge(sG[s_].h, bD[s_] + 16 * o_)
                                t_.wait_ge(sU.h, bUc + NI * k + i + 1)
                                if i >= 2:
                                    t_.wait_ge(sSC.h, iSC(k) + i - 1)
                                    t_.wait_ge(sVE.h, iVE(k) + i - 1)
                                elif k >= 1:
                                    t_.wait_ge(sVE.h, iVE(k) - 6 + 2 * i)
                                gb = (i % 2) * 2
                                ub = 4 + (i % 2) * 2
                                sub = i % 2
                                wg_sl = wstr[:, s_ * 2048:s_ * 2048 + 2048]
                                wu_sl = wstr[:, 4096 + (i % 2) * 1024:
                                             4096 + (i % 2) * 1024 + 1024]
                                for b in range(2):
                                    for dm in range(NDM):
                                        t_.matmul(pb[:, gb + b, :],
                                                  wg_sl[:, dm * 256 + sub * 128:dm * 256 + sub * 128 + 128],
                                                  fm(dm, half * 1024 + b * 512, 512),
                                                  start=(dm == 0), stop=(dm == NDM - 1))
                                if i < 2 and k >= 1:
                                    t_.wait_ge(sVE.h, iVE(k) - 2 + 2 * i)
                                for b in range(2):
                                    for dm in range(NDM):
                                        ins = t_.matmul(pb[:, ub + b, :],
                                                        wu_sl[:, dm * 128:
                                                              dm * 128 + 128],
                                                        fm(dm, half * 1024 + b * 512, 512),
                                                        start=(dm == 0), stop=(dm == NDM - 1))
                                sPE.inc(ins)
                            t_.wait_ge(sVE.h, iVE(k) + NI)
                            t_.wait_ge(sSC.h, iSC(k) + NI)
                            for g in range(2 * NI):
                                p_, i_ = g // NI, g % NI
                                s_, o_ = wd_need(k, (g // NI) * 11 + (g % NI) // 2)
                                t_.wait_ge(sG[3 + s_].h, bD[3 + s_] + 16 * o_)
                                sub = i_ % 2
                                wd_sl = wstr[:, 8192 + s_ * 2048:8192 + s_ * 2048 + 2048]
                                for tl in range(8):
                                    if p_ == 1 and i_ == 0:
                                        t_.wait_ge(sVE.h, iVE(k) + NI + tl + 1)
                                    ins = t_.matmul(pb[:, tl, :],
                                                    h_fm[:, i_ * 1024 + tl * 128:i_ * 1024 + (tl + 1) * 128],
                                                    wd_sl[:, sub * 1024 + p_ * 512:sub * 1024 + (p_ + 1) * 512],
                                                    start=(i_ == 0), stop=(i_ == NI - 1),
                                                    skip_group_check=True)
                                sPE.inc(ins)

                    @blk.scalar
                    def _(s):
                        def cast(k, i):
                            m = NI * k + i
                            s.wait_ge(sD[4 + m % 2].h,
                                      bU[m % 2] + 16 * (m // 2 + 1))
                            if i >= 2:
                                s.wait_ge(sPE.h, iPE(k) + i - 1)
                            elif k >= 1:
                                s.wait_ge(sPE.h, iPE(k - 1) + i + 21)
                            ins = s.copy(
                                wstr[:, 4096 + (i % 2) * 1024:
                                     4096 + (i % 2) * 1024 + 1024],
                                ustag[:, (m % 2) * 1024:(m % 2) * 1024 + 1024])
                            sU.inc(ins)

                        for k in range(16):
                            cast(k, 0)
                            cast(k, 1)
                            for i in range(NI):
                                if i + 2 < NI:
                                    cast(k, i + 2)
                                s.wait_ge(sPE.h, iPE(k) + i + 1)
                                if i >= 2:
                                    s.wait_ge(sVE.h, iVE(k) + i - 1)
                                gb = (i % 2) * 2
                                gs = gate_s[:, (i % 2) * 1024:(i % 2) * 1024 + 1024]
                                s.activation(gs[:, 0:512], pb[:, gb, :], AF.Silu)
                                ins = s.activation(gs[:, 512:1024], pb[:, gb + 1, :],
                                                   AF.Silu)
                                sSC.inc(ins)

                    @blk.vector
                    def _(v):
                        for k in range(16):
                            e, half = k // 2, k % 2
                            for i in range(NI):
                                v.wait_ge(sSC.h, iSC(k) + i + 1)
                                ub = 4 + (i % 2) * 2
                                gs = gate_s[:, (i % 2) * 1024:(i % 2) * 1024 + 1024]
                                v.tensor_mul(h_fm[:, i * 1024:i * 1024 + 512],
                                             gs[:, 0:512], pb[:, ub, :])
                                ins = v.tensor_mul(h_fm[:, i * 1024 + 512:(i + 1) * 1024],
                                                   gs[:, 512:1024], pb[:, ub + 1, :])
                                sVE.inc(ins)
                            for g in range(2 * NI):
                                p_, i_ = g // NI, g % NI
                                if i_ == NI - 1:
                                    for tl in range(8):
                                        v.wait_ge(sPE.h, iPE(k) + 22 + p_ * NI + NI)
                                        t = half * 8 + tl
                                        cc = comb_t[:, t * 8 + e:t * 8 + e + 1]
                                        sl = x_tm[:, t * D + p_ * 512:t * D + (p_ + 1) * 512]
                                        ins = v.scalar_tensor_tensor(
                                            sl, pb[:, tl, :], cc, sl, ALU.mult, ALU.add)
                                        sVE.inc(ins)

            with nc.sbuf_tensor("lntmp2", [128, 1024], F32) as lntmp2, \
                 nc.Block("l0f_ln") as blk:
                bSC, bVE = sSC.v, sVE.v

                @blk.vector
                def _(v):
                    ln_vector_pass(v, [(t, xt(t)) for t in range(NT)], bSC, bVE)

                @blk.scalar
                def _(s):
                    ln_scalar_pass(s, [(t, xt(t)) for t in range(NT)], lntmp2[:],
                                   bSC, bVE)

        def emit_l1a():
            xfm_rebuild("l1a")
            NC = 16
            with (
                nc.sbuf_tensor("WTbf", [128, 8 * 1024], BF16) as WTbf,
                nc.sbuf_tensor("stf", [128, 384], F32) as stf,
                nc.sbuf_tensor("gci", [128, 128], BF16) as gci,
                nc.sbuf_tensor("wrt2", [128, 8 * 1024], BF16) as wrt2,
                nc.sbuf_tensor("v_bf", [128, 1024], BF16) as v_bf,
                nc.sbuf_tensor("vt_bf", [128, 1024], BF16) as vt_bf,
                nc.sbuf_tensor("rk_s", [128, 1024], BF16) as rk_s,
                nc.sbuf_tensor("smt", [128, 128], BF16) as smt,
                nc.sbuf_tensor("rd_bf", [128, 1024], BF16) as rd_bf,
                nc.sbuf_tensor("rd_fm", [128, 2048], BF16) as rd_fm,
                nc.sbuf_tensor("sc11", [128, 2], F32) as sc11,
                nc.sbuf_tensor("wl1a", [128, 10240], BF16) as wl1a,
            ):
                st_t = stf[:, 0:128]
                st2 = stf[:, 128:256]
                st3 = stf[:, 256:384]
                wwt_s = wl1a[:, 0:8192]
                wk_tm = wl1a[:, 8192:10240]
                grow_bf = rows[0:1, 9088:9216]
                with nc.Block("l1a_s") as blk:
                    bIN, bW, bSC, bVE, bPE, bGP = (sIN.v, sW.v, sSC.v, sVE.v,
                                                   sPE.v, sGP.v)

                    @blk.sync
                    def _(sy):
                        ins = sy.dma_start(out=sc11[0:1, 0:1], in_=decay)
                        sIN.inc(ins, 16)
                        ins = sy.dma_start(out=sc11[0:1, 1:2], in_=log_alpha)
                        sIN.inc(ins, 16)

                    @blk.gpsimd
                    def _(gp):
                        ins = gp.dma_start(
                            out=wrt2[:].rearrange("p (c n) -> p c n", n=1024),
                            in_=wr_t.rearrange("(c p) n -> p c n", p=128))
                        sW.inc(ins, 16)
                        ins = gp.dma_start(
                            out=wwt_s[:].rearrange("p (c n) -> p c n", n=1024),
                            in_=ww_t.rearrange("(c p) n -> p c n", p=128))
                        sW.inc(ins, 16)
                        gp.iota(mf32[:, 20:21], [[1, 1]], base=0, channel_multiplier=1,
                                allow_small_or_imprecise_dtypes=True)
                        gp.iota(st2[0:1, 0:128], [[1, 128]], base=0,
                                channel_multiplier=0,
                                allow_small_or_imprecise_dtypes=True)
                        ins = gp.iota(st_t[:], [[1, 128]], base=-1,
                                      channel_multiplier=-1,
                                      allow_small_or_imprecise_dtypes=True)
                        sGP.inc(ins)

                    @blk.scalar
                    def _(s):
                        s.wait_ge(sIN.h, bIN + 32)
                        s.activation(sc11[0:1, 0:1], sc11[0:1, 0:1], AF.Sigmoid)
                        s.activation(sc11[0:1, 0:1], sc11[0:1, 0:1], AF.Ln)
                        sSC.inc(s.drain())
                        s.wait_ge(sVE.h, bVE + 1)
                        s.activation(c_gw, c_t1, AF.Exp)
                        s.activation(c_gc, c_t0, AF.Exp)
                        s.activation(c_al, c_al, AF.Exp)
                        s.activation(st3[:], st3[:], AF.Exp)
                        s.activation(st2[0:1, 0:128], st2[0:1, 0:128], AF.Exp)
                        sSC.inc(s.drain())

                    @blk.tensor
                    def _(t_):
                        t_.wait_ge(sSC.h, bSC + 1)
                        ins = t_.matmul(pb[:, 0, 0:2], onesf[:], sc11[0:1, 0:2],
                                        start=True, stop=True)
                        sPE.inc(ins)
                        t_.wait_ge(sVE.h, bVE + 2)
                        ins = t_.matmul(pb[:, 1, 0:128], ones_row, grow_bf,
                                        start=True, stop=True)
                        sPE.inc(ins)

                    @blk.vector
                    def _(v):
                        v.wait_ge(sPE.h, bPE + 1)
                        v.wait_ge(sGP.h, bGP + 1)
                        v.tensor_copy(c_lg, pb[:, 0, 0:1])
                        v.tensor_copy(c_al, pb[:, 0, 1:2])
                        v.drain()
                        v.tensor_mul(c_t0, c_lg, c_lg)
                        v.tensor_scalar(c_t1, mf32[:, 20:21], -1.0, 127.0,
                                        ALU.mult, ALU.add)
                        v.drain()
                        v.tensor_mul(c_t1, c_t1, c_lg)
                        v.tensor_scalar(c_t0, c_lg, 128.0, None, ALU.mult)
                        v.tensor_scalar(st3[:], st_t[:], c_lg, None, ALU.mult)
                        v.drain()
                        v.tensor_single_scalar(st_t[:], st_t[:], 0.0, ALU.is_ge)
                        v.tensor_scalar(st2[0:1, 0:128], st2[0:1, 0:128],
                                        c_lg[0:1, :], None, ALU.mult)
                        sVE.inc(v.drain())
                        v.wait_ge(sSC.h, bSC + 2)
                        v.tensor_mul(st_t[:], st3[:], st_t[:])
                        v.drain()
                        v.tensor_copy(m_t_tile, st_t[:])
                        v.tensor_copy(grow_bf, st2[0:1, 0:128])
                        sVE.inc(v.drain())
                        v.wait_ge(sPE.h, bPE + 2)
                        v.tensor_copy(gam_bc, pb[:, 1, 0:128])
                        v.drain()
                        v.tensor_scalar(gci[:], ident, c_gc, None, ALU.mult)
                        for ec in range(8):
                            v.memset(WTbf[:, ec * 1024:(ec + 1) * 1024], 0.0)
                        sVE.inc(v.drain())

                with nc.Block("l1a_c") as blk:
                    bPE, bVE, bSC = sPE.v, sVE.v, sSC.v
                    bWl = sW.v - 32
                    bD0, bD1, bD2 = sD[0].v, sD[1].v, sD[2].v
                    bDw = (bD0, bD1)
                    PEC, VEC, SCC = 19, 9, 4

                    def cPE(ch):
                        return bPE + PEC * ch

                    def cVE(ch):
                        return bVE + VEC * ch

                    def cSC(ch):
                        return bSC + SCC * ch

                    @blk.sync
                    def _(sy):
                        for ch in range(NC):
                            if ch >= 2:
                                sy.wait_ge(sPE.h, cPE(ch - 2) + 18)
                            dstw = wk_tm[:, (ch % 2) * 1024:(ch % 2) * 1024 + 1024]
                            if ch == 0:
                                ins = sy.dma_start(out=dstw[1:128, :],
                                                   in_=xbf_dram[0:127, :])
                            else:
                                ins = sy.dma_start(out=dstw,
                                                   in_=xbf_dram[ch * 128 - 1:
                                                                (ch + 1) * 128 - 1, :])
                            sD[ch % 2].inc(ins, 16)
                            sy.wait_ge(sVE.h, cVE(ch) + 4)
                            if ch >= 2:
                                sy.wait_ge(sPE.h, cPE(ch - 2) + 19)
                            for dc in range(NDM):
                                ins = sy.dma_start_transpose(
                                    rd_fm[:, (ch % 2) * 1024 + dc * 128:
                                          (ch % 2) * 1024 + (dc + 1) * 128],
                                    rd_bf[:, dc * 128:(dc + 1) * 128])
                                sD[2].inc(ins, 16)

                    @blk.vector
                    def _(v):
                        v.memset(wk_tm[0:1, 0:1024], 0.0)
                        v.drain()
                        for ch in range(NC):
                            v.wait_ge(sPE.h, cPE(ch) + 8)
                            v.tensor_copy(v_bf[:, 0:512], pb[:, 0, :])
                            v.tensor_copy(v_bf[:, 512:1024], pb[:, 1, :])
                            v.tensor_scalar(vt_bf[:, 0:512], pb[:, 0, :], c_gw, None,
                                            ALU.mult)
                            ins = v.tensor_scalar(vt_bf[:, 512:1024], pb[:, 1, :],
                                                  c_gw, None, ALU.mult)
                            sVE.inc(ins)
                            for ec in range(NDM):
                                ins = v.tensor_mul(rk_s[:, ec * 128:(ec + 1) * 128],
                                                   fm(ec, ch * 128, 128), gam_bc)
                            sVE.inc(ins)
                            v.wait_ge(sPE.h, cPE(ch) + 9)
                            ins = v.tensor_mul(smt[:], pb[:, 4, 0:128], m_t_tile)
                            sVE.inc(ins)
                            v.wait_ge(sPE.h, cPE(ch) + 10)
                            if ch >= 1:
                                v.wait_ge(sD[2].h, bD2 + 16 * 8 * ch)
                            v.tensor_copy(rd_bf[:, 0:512], pb[:, 2, :])
                            ins = v.tensor_copy(rd_bf[:, 512:1024], pb[:, 3, :])
                            sVE.inc(ins)
                            for ec in (0, 2, 4, 6):
                                v.wait_ge(sPE.h, cPE(ch) + 11 + ec)
                                wsl = WTbf[:, ec * 1024:(ec + 1) * 1024]
                                v.tensor_copy(wsl[:, 0:512], pb[:, 0, :])
                                ins = v.tensor_copy(wsl[:, 512:1024], pb[:, 1, :])
                                sVE.inc(ins)
                            v.wait_ge(sPE.h, cPE(ch) + 19)
                            xsl = xt(ch)
                            v.tensor_scalar(xsl, xsl, 2.0, None, ALU.mult)
                            v.scalar_tensor_tensor(xsl[:, 0:512], pb[:, 5, :], c_al,
                                                   xsl[:, 0:512], ALU.mult, ALU.add)
                            ins = v.scalar_tensor_tensor(xsl[:, 512:1024], pb[:, 6, :],
                                                         c_al, xsl[:, 512:1024],
                                                         ALU.mult, ALU.add)
                            sVE.inc(ins)

                    @blk.scalar
                    def _(s):
                        for ch in range(NC):
                            for ec in (1, 3, 5, 7):
                                s.wait_ge(sPE.h, cPE(ch) + 11 + ec)
                                wsl = WTbf[:, ec * 1024:(ec + 1) * 1024]
                                s.copy(wsl[:, 0:512], pb[:, 2, :])
                                ins = s.copy(wsl[:, 512:1024], pb[:, 3, :])
                                sSC.inc(ins)

                    @blk.tensor
                    def _(t_):
                        for ch in range(NC):
                            c0 = ch * 128
                            if ch >= 1:
                                t_.wait_ge(sVE.h, cVE(ch) - 1)
                                t_.wait_ge(sSC.h, cSC(ch))
                            if ch == 0:
                                t_.wait_ge(sW.h, bWl + 16 * 2)
                            for dm in range(NDM):
                                for p in range(2):
                                    ins = t_.matmul(
                                        pb[:, p, :], fm(dm, c0, 128),
                                        wwt_s[:, dm * 1024 + p * 512:
                                              dm * 1024 + (p + 1) * 512],
                                        start=(dm == 0), stop=(dm == NDM - 1),
                                        skip_group_check=True)
                                sPE.inc(ins)
                            for ec in range(NDM):
                                ins = t_.matmul(pb[:, 4, 0:128], fm(ec, c0 - 1, 128),
                                                fm(ec, c0, 128), start=(ec == 0),
                                                stop=(ec == NDM - 1))
                            sPE.inc(ins)
                            t_.wait_ge(sVE.h, cVE(ch) + 2)
                            for ec in range(NDM):
                                for p in range(2):
                                    t_.matmul(pb[:, 2 + p, :],
                                              rk_s[:, ec * 128:(ec + 1) * 128],
                                              WTbf[:, ec * 1024 + p * 512:
                                                   ec * 1024 + (p + 1) * 512],
                                              start=(ec == 0), stop=False,
                                              skip_group_check=True)
                            t_.wait_ge(sVE.h, cVE(ch) + 3)
                            for p in range(2):
                                ins = t_.matmul(pb[:, 2 + p, :], smt[:],
                                                v_bf[:, p * 512:(p + 1) * 512],
                                                start=False, stop=True,
                                                skip_group_check=True)
                            sPE.inc(ins)
                            t_.wait_ge(sD[ch % 2].h, bDw[ch % 2] + 16 * (ch // 2 + 1))
                            t_.wait_ge(sVE.h, cVE(ch) + 4)
                            for ec in range(NDM):
                                bk = (ec % 2) * 2
                                if ec >= 2:
                                    if ec % 2 == 0:
                                        t_.wait_ge(sVE.h, cVE(ch) + 4 + ec // 2)
                                    else:
                                        t_.wait_ge(sSC.h, cSC(ch) + (ec - 1) // 2)
                                for p in range(2):
                                    t_.matmul(pb[:, bk + p, :], gci[:],
                                              WTbf[:, ec * 1024 + p * 512:
                                                   ec * 1024 + (p + 1) * 512],
                                              start=True, stop=False,
                                              skip_group_check=True)
                                for p in range(2):
                                    ins = t_.matmul(
                                        pb[:, bk + p, :],
                                        wk_tm[:, (ch % 2) * 1024 + ec * 128:
                                              (ch % 2) * 1024 + (ec + 1) * 128],
                                        vt_bf[:, p * 512:(p + 1) * 512],
                                        start=False, stop=True, skip_group_check=True)
                                sPE.inc(ins)
                            t_.wait_ge(sVE.h, cVE(ch))
                            t_.wait_ge(sD[2].h, bD2 + 16 * 8 * (ch + 1))
                            for dc in range(NDM):
                                for p in range(2):
                                    ins = t_.matmul(pb[:, 5 + p, :],
                                                    rd_fm[:, (ch % 2) * 1024 + dc * 128:
                                                         (ch % 2) * 1024 + (dc + 1) * 128],
                                                    wrt2[:, dc * 1024 + p * 512:
                                                         dc * 1024 + (p + 1) * 512],
                                                    start=(dc == 0),
                                                    stop=(dc == NDM - 1),
                                                    skip_group_check=True)
                            sPE.inc(ins)

                with nc.Block("l1a_ln") as blk:
                    bSC, bVE = sSC.v, sVE.v

                    @blk.vector
                    def _(v):
                        ln_vector_pass(v, [(t, xt(t)) for t in range(NT)], bSC, bVE)

                    @blk.scalar
                    def _(s):
                        ln_scalar_pass(s, [(t, xt(t)) for t in range(NT)],
                                       v_bf[:], bSC, bVE)

        def emit_l1f():
            xfm_rebuild("l1f")
            NQ, QT = 4, 512
            NI = 16
            with (
                nc.sbuf_tensor("spk", [128, NI * QT], BF16) as spk,
                nc.sbuf_tensor("ehf", [128, 8 * QT], BF16) as ehf,
                nc.sbuf_tensor("ew2", [128, 8 * 8 * 64], BF16) as ew2,
                nc.sbuf_tensor("s2cb_w", [128, NI * 64], BF16) as s2cb_w,
                nc.sbuf_tensor("c2s_bf", [128, 2048], BF16) as c2s_bf,
                nc.sbuf_tensor("rw1_bf", [128, 64], BF16) as rw1_bf,
                nc.sbuf_tensor("rw2_bf", [128, 8], BF16) as rw2_bf,
                nc.sbuf_tensor("cont_bf", [128, QT], BF16) as cont_bf,
                nc.sbuf_tensor("rh_bf", [128, QT], BF16) as rh_bf,
                nc.sbuf_tensor("comb2", [128, 32], F32) as comb2,
                nc.sbuf_tensor("moe_tm", [128, 4 * 64], F32) as moe_tm,
                nc.sbuf_tensor("moe_bf", [128, 512], BF16) as moe_bf,
                nc.sbuf_tensor("mfm", [128, 512], BF16) as mfm,
                nc.sbuf_tensor("dec_s", [128, 4 * 1024], F32) as dec_s,
                nc.sbuf_tensor("ones5", [1, 512], BF16) as ones5,
                nc.sbuf_tensor("sc_q", [128, 16], F32) as sc_q,
                nc.sbuf_tensor("disc", [128, 1024], F32) as disc,
                nc.sbuf_tensor("wl1f", [128, 6144], BF16) as wl1f,
            ):
                rts = spk
                r_encb = rows[0:1, 0:2048]
                r_c2sb = rows[0:1, 2048:4096]
                r_decb = rows[0:1, 4096:5120]
                r_s2cb = rows[0:1, 5120:5184]
                r_rb1 = rows[0:1, 5184:5248]
                r_rb2 = rows[0:1, 5248:5256]
                r_eb1 = rows[0:1, 5256:6280]
                r_eb2 = rows[0:1, 6280:6792]
                r_lng = rows[0:1, 6792:7816]
                r_lnb = rows[0:1, 7816:8840]

                with nc.Block("l1f_s") as blk:
                    bW, bPE, bVE = sW.v, sPE.v, sVE.v

                    @blk.gpsimd
                    def _(gp):
                        for dst, srcd in ((r_encb, enc_b), (r_c2sb, c2s_b),
                                          (r_decb, dec_b), (r_s2cb, s2c_b),
                                          (r_rb1, r_b1), (r_rb2, r_b2),
                                          (r_eb2, e_b2), (r_lng, ln_g),
                                          (r_lnb, ln_b)):
                            ins = gp.dma_start(out=dst, in_=srcd)
                            sW.inc(ins, 16)
                        ins = gp.dma_start(
                            out=s2cb_w[:].rearrange("p (c m) -> p c m", m=64),
                            in_=s2c_w.rearrange("(c p) m -> p c m", p=128))
                        sW.inc(ins, 16)
                        ins = gp.dma_start(
                            out=c2s_bf[0:64, :], in_=c2s_w)
                        sW.inc(ins, 16)
                        ins = gp.dma_start(out=rw1_bf[0:64, :], in_=r_w1)
                        sW.inc(ins, 16)
                        ins = gp.dma_start(out=rw2_bf[0:64, :], in_=r_w2)
                        sW.inc(ins, 16)
                        ins = gp.dma_start(
                            out=ew2[:].rearrange("p (e c d) -> p e c d", e=8, d=64),
                            in_=e_w2.rearrange("e (c p) d -> p e c d", p=128))
                        sW.inc(ins, 16)

                    @blk.vector
                    def _(v):
                        v.memset(ones5[:], 1.0)
                        sVE.inc(v.drain())

                    @blk.tensor
                    def _(t_):
                        t_.wait_ge(sW.h, bW + 16 * 14)
                        t_.wait_ge(sVE.h, bVE + 1)
                        t_.matmul(pb[:, 0, 0:512], ones_row, r_lng[0:1, 0:512],
                                  start=True, stop=True)
                        t_.matmul(pb[:, 1, 0:512], ones_row, r_lng[0:1, 512:1024],
                                  start=True, stop=True)
                        t_.matmul(pb[:, 2, 0:512], ones_row, r_lnb[0:1, 0:512],
                                  start=True, stop=True)
                        ins = t_.matmul(pb[:, 3, 0:512], ones_row, r_lnb[0:1, 512:1024],
                                        start=True, stop=True)
                        sPE.inc(ins)

                    @blk.vector
                    def _(v2):
                        v2.wait_ge(sPE.h, bPE + 1)
                        v2.tensor_copy(lng_bc[:, 0:512], pb[:, 0, :])
                        v2.tensor_copy(lng_bc[:, 512:1024], pb[:, 1, :])
                        v2.tensor_copy(lnb_bc[:, 0:512], pb[:, 2, :])
                        v2.tensor_copy(lnb_bc[:, 512:1024], pb[:, 3, :])
                        v2.drain()

                for q in range(NQ):
                    q0 = q * QT
                    with nc.Block(f"l1f_q{q}") as blk:
                        bPE, bVE, bSC, bW2 = sPE.v, sVE.v, sSC.v, sW.v
                        bG = [s_.v for s_ in sG]
                        bG4, bG5 = sG[4].v, sG[5].v
                        bGe = (bG4, bG5)
                        bD3 = sD[3].v
                        oPE_B, oPE_C = 22, 118
                        oVE_B, oVE_C = 5, 37
                        oSC_B, oSC_C = 21, 85

                        @blk.gpsimd
                        def _(gp):
                            for i in range(NI):
                                if i >= 2:
                                    gp.wait_ge(sPE.h, bPE + i - 1)
                                src = enc_w.rearrange("(c p) n -> p c n", p=128)[
                                    :, :, i * 128:(i + 1) * 128]
                                dst = wl1f[:, (i % 2) * 1024:(i % 2) * 1024 + 1024]
                                ins = gp.dma_start(
                                    out=dst.rearrange("p (c n) -> p c n", n=128),
                                    in_=src)
                                sG[i % 2].inc(ins, 16)
                            for e in range(8):
                                if e >= 2:
                                    gp.wait_ge(sPE.h, bPE + oPE_B + 12 * (e - 1))
                                src = e_w1[e]
                                dst = wl1f[:, 2048 + (e % 2) * 1024:
                                           2048 + (e % 2) * 1024 + 1024]
                                ins = gp.dma_start(out=dst[0:64, :], in_=src)
                                sG[4 + e % 2].inc(ins, 16)
                                if e >= 1:
                                    gp.wait_ge(sPE.h, bPE + oPE_B + 12 * (e - 1) + 8)
                                ins = gp.dma_start(out=r_eb1, in_=e_b1[e:e + 1, :])
                                sW.inc(ins, 16)
                            for i in range(NI):
                                if i >= 2:
                                    gp.wait_ge(sPE.h, bPE + oPE_C + 16 + i - 1)
                                src = dec_w[i * 128:(i + 1) * 128, :]
                                dst = wl1f[:, 4096 + (i % 2) * 1024:
                                           4096 + (i % 2) * 1024 + 1024]
                                ins = gp.dma_start(out=dst, in_=src)
                                sG[4 + i % 2].inc(ins, 16)

                        @blk.tensor
                        def _(t_):
                            for i in range(NI):
                                t_.wait_ge(sG[i % 2].h, bG[i % 2] + 16 * (i // 2 + 1))
                                if i >= 2:
                                    t_.wait_ge(sSC.h, bSC + i - 1)
                                bank = pb[:, i % 2, :]
                                wsl = wl1f[:, (i % 2) * 1024:(i % 2) * 1024 + 1024]
                                for dm in range(NDM):
                                    t_.matmul(bank, wsl[:, dm * 128:(dm + 1) * 128],
                                              fm(dm, q0, QT),
                                              start=(dm == 0), stop=False)
                                ins = t_.matmul(bank, r_encb[0:1, i * 128:(i + 1) * 128],
                                                ones5[:], start=False, stop=True)
                                sPE.inc(ins)
                            for i in range(NI):
                                t_.wait_ge(sSC.h, bSC + i + 1)
                                t_.matmul(pb[0:64, 2, 0:512], s2cb_w[:, i * 64:(i + 1) * 64],
                                          spk[:, i * QT:(i + 1) * QT],
                                          start=(i == 0), stop=False,
                                          skip_group_check=True)
                            ins = t_.matmul(pb[0:64, 2, 0:512], r_s2cb, ones5[:],
                                            start=False, stop=True,
                                            skip_group_check=True)
                            sPE.inc(ins)
                            t_.wait_ge(sVE.h, bVE + 1)
                            t_.matmul(pb[0:64, 3, 0:512], rw1_bf[0:64, :],
                                      cont_bf[0:64, :], start=True, stop=False)
                            ins = t_.matmul(pb[0:64, 3, 0:512], r_rb1, ones5[:],
                                            start=False, stop=True)
                            sPE.inc(ins)
                            t_.wait_ge(sSC.h, bSC + NI + 1)
                            for tl in range(4):
                                t_.matmul(pb[:, 4 + tl, 0:8], rh_bf[0:64, tl * 128:(tl + 1) * 128],
                                          rw2_bf[0:64, :], start=True, stop=False)
                                ins = t_.matmul(pb[:, 4 + tl, 0:8], ones_row, r_rb2,
                                                start=False, stop=True)
                                sPE.inc(ins)
                            for e in range(8):
                                t_.wait_ge(sG[4 + e % 2].h,
                                           bGe[e % 2] + 16 * (e // 2 + 1))
                                t_.wait_ge(sW.h, bW2 + 16 * (e + 1))
                                if e == 0:
                                    t_.wait_ge(sVE.h, bVE + 1)
                                w1 = wl1f[:, 2048 + (e % 2) * 1024:
                                          2048 + (e % 2) * 1024 + 1024]
                                for hc in range(8):
                                    if e * 8 + hc >= 2:
                                        t_.wait_ge(sSC.h, bSC + oSC_B + 8 * e + hc - 1)
                                    elif e == 0:
                                        t_.wait_ge(sSC.h, bSC + 16)
                                    bank = pb[:, hc % 2, :]
                                    t_.matmul(bank, w1[0:64, hc * 128:(hc + 1) * 128],
                                              cont_bf[0:64, :], start=True, stop=False)
                                    ins = t_.matmul(bank,
                                                    r_eb1[0:1, hc * 128:(hc + 1) * 128],
                                                    ones5[:], start=False, stop=True)
                                    sPE.inc(ins)
                                t_.wait_ge(sSC.h, bSC + oSC_B + 8 * e + 8)
                                for tl in range(4):
                                    if e >= 1:
                                        t_.wait_ge(sVE.h, bVE + oVE_B + 4 * (e - 1) + tl + 1)
                                    else:
                                        if tl == 0:
                                            t_.wait_ge(sVE.h, bVE + 1)
                                        elif tl == 1:
                                            t_.wait_ge(sSC.h, bSC + 17)
                                        else:
                                            t_.wait_ge(sVE.h, bVE + 2 + tl)
                                    bank = pb[:, 2 + tl, 0:64]
                                    for hc in range(8):
                                        t_.matmul(bank,
                                                  ehf[:, hc * QT + tl * 128:
                                                      hc * QT + (tl + 1) * 128],
                                                  ew2[:, (e * 8 + hc) * 64:
                                                      (e * 8 + hc + 1) * 64],
                                                  start=(hc == 0), stop=False,
                                                  skip_group_check=True)
                                    ins = t_.matmul(bank, ones_row,
                                                    r_eb2[0:1, e * 64:(e + 1) * 64],
                                                    start=False, stop=True,
                                                    skip_group_check=True)
                                    sPE.inc(ins)
                            t_.wait_ge(sD[3].h, bD3 + 64)
                            for i in range(NI):
                                if i >= 2:
                                    t_.wait_ge(sSC.h, bSC + oSC_C + i - 1)
                                else:
                                    t_.wait_ge(sSC.h, bSC + oSC_C)
                                bank = pb[:, i % 2, :]
                                t_.matmul(bank, c2s_bf[0:64, i * 128:(i + 1) * 128],
                                          mfm[0:64, :], start=True, stop=False)
                                ins = t_.matmul(bank,
                                                r_c2sb[0:1, i * 128:(i + 1) * 128],
                                                ones5[:], start=False, stop=True)
                                sPE.inc(ins)
                            t_.wait_ge(sSC.h, bSC + oSC_C + NI)
                            t_.wait_ge(sVE.h, bVE + oVE_C)
                            for i in range(NI):
                                t_.wait_ge(sG[4 + i % 2].h,
                                           bGe[i % 2] + 16 * (4 + i // 2 + 1))
                                dsl = wl1f[:, 4096 + (i % 2) * 1024:
                                           4096 + (i % 2) * 1024 + 1024]
                                for tl in range(4):
                                    for p_ in range(2):
                                        ins = t_.matmul(pb[:, tl * 2 + p_, :],
                                                        rts[:, i * QT + tl * 128:
                                                            i * QT + (tl + 1) * 128],
                                                        dsl[:, p_ * 512:(p_ + 1) * 512],
                                                        start=(i == 0), stop=False,
                                                        skip_group_check=True)
                                sPE.inc(ins)
                            for tl in range(4):
                                for p_ in range(2):
                                    ins = t_.matmul(pb[:, tl * 2 + p_, :], ones_row,
                                                    r_decb[0:1, p_ * 512:(p_ + 1) * 512],
                                                    start=False, stop=True,
                                                    skip_group_check=True)
                                sPE.inc(ins)

                        @blk.scalar
                        def _(s):
                            for i in range(NI):
                                s.wait_ge(sPE.h, bPE + i + 1)
                                ins = s.activation(spk[:, i * QT:(i + 1) * QT],
                                                   pb[:, i % 2, :], AF.Sigmoid)
                                sSC.inc(ins)
                            s.wait_ge(sPE.h, bPE + 18)
                            ins = s.activation(rh_bf[0:64, :], pb[0:64, 3, 0:512],
                                               AF.Relu)
                            sSC.inc(s.drain())
                            for tl in range(4):
                                s.wait_ge(sVE.h, bVE + 1 + tl + 1)
                                sc = comb2[:, tl * 8:(tl + 1) * 8]
                                s.activation(sc, sc, AF.Exp)
                                sSC.inc(s.drain())
                            for e in range(8):
                                for hc in range(8):
                                    s.wait_ge(sPE.h, bPE + oPE_B + 12 * e + hc + 1)
                                    ins = s.activation(
                                        ehf[:, hc * QT:(hc + 1) * QT],
                                        pb[:, hc % 2, :], AF.Relu)
                                    sSC.inc(ins)
                            for i in range(NI):
                                s.wait_ge(sPE.h, bPE + oPE_C + i + 1)
                                ins = s.activation(rts[:, i * QT:(i + 1) * QT],
                                                   pb[:, i % 2, :], AF.Sigmoid)
                                sSC.inc(ins)
                            for tl in range(4):
                                s.wait_ge(sPE.h, bPE + oPE_C + 32 + tl + 1)
                                s.activation(dec_s[:, tl * 1024:tl * 1024 + 512],
                                             pb[:, tl * 2, :], AF.Sigmoid)
                                ins = s.activation(dec_s[:, tl * 1024 + 512:
                                                         (tl + 1) * 1024],
                                                   pb[:, tl * 2 + 1, :], AF.Sigmoid)
                                sSC.inc(ins)
                            tiles = [(j, dec_s[:, j * 1024:(j + 1) * 1024])
                                     for j in range(4)]
                            ln_scalar_pass(s, tiles, disc[:], bSC + oSC_C + 20,
                                           bVE + oVE_C + 1)

                        @blk.vector
                        def _(v):
                            v.memset(moe_tm[:], 0.0)
                            v.memset(moe_bf[:], 0.0)
                            v.wait_ge(sPE.h, bPE + 17)
                            ins = v.tensor_copy(cont_bf[0:64, :], pb[0:64, 2, 0:512])
                            sVE.inc(v.drain())
                            for tl in range(4):
                                v.wait_ge(sPE.h, bPE + 18 + tl + 1)
                                sc = comb2[:, tl * 8:(tl + 1) * 8]
                                v.tensor_reduce(sc_q[:, 0:1], pb[:, 4 + tl, 0:8], AX.X,
                                                ALU.max)
                                v.drain()
                                v.tensor_scalar(sc, pb[:, 4 + tl, 0:8], sc_q[:, 0:1],
                                                None, ALU.subtract)
                                sVE.inc(v.drain())
                                v.wait_ge(sSC.h, bSC + 17 + tl + 1)
                                v.tensor_reduce(sc_q[:, 1:2], sc, AX.X, ALU.add)
                                v.drain()
                                v.reciprocal(sc_q[:, 1:2], sc_q[:, 1:2])
                                v.drain()
                                v.tensor_scalar(sc, sc, sc_q[:, 1:2], None, ALU.mult)
                                v.drain()
                                v.tensor_reduce(sc_q[:, 2:3], sc, AX.X, ALU.max)
                                v.drain()
                                v.tensor_scalar(sc_q[:, 4:12], sc, sc_q[:, 2:3], None,
                                                ALU.is_ge)
                                v.drain()
                                v.scalar_tensor_tensor(sc_q[:, 4:12], sc_q[:, 4:12],
                                                       -1e30, sc, ALU.mult, ALU.add)
                                v.drain()
                                v.tensor_reduce(sc_q[:, 3:4], sc_q[:, 4:12], AX.X, ALU.max)
                                v.drain()
                                v.tensor_scalar(sc_q[:, 4:12], sc, sc_q[:, 3:4], None,
                                                ALU.is_ge)
                                v.tensor_add(sc_q[:, 2:3], sc_q[:, 2:3], sc_q[:, 3:4])
                                v.drain()
                                v.reciprocal(sc_q[:, 2:3], sc_q[:, 2:3])
                                v.drain()
                                v.tensor_mul(sc, sc, sc_q[:, 4:12])
                                v.drain()
                                v.tensor_scalar(sc, sc, sc_q[:, 2:3], None, ALU.mult)
                                v.drain()
                            for e in range(8):
                                for tl in range(4):
                                    v.wait_ge(sPE.h, bPE + oPE_B + 12 * e + 8 + tl + 1)
                                    cc = comb2[:, tl * 8 + e:tl * 8 + e + 1]
                                    msl = moe_tm[:, tl * 64:(tl + 1) * 64]
                                    v.scalar_tensor_tensor(msl, pb[:, 2 + tl, 0:64],
                                                           cc, msl, ALU.mult, ALU.add)
                                    sVE.inc(v.drain())
                            for tl in range(4):
                                v.tensor_copy(moe_bf[:, tl * 128:tl * 128 + 64],
                                              moe_tm[:, tl * 64:(tl + 1) * 64])
                            sVE.inc(v.drain())
                            tiles = [(j, dec_s[:, j * 1024:(j + 1) * 1024])
                                     for j in range(4)]
                            v.wait_ge(sSC.h, bSC + oSC_C + 20)
                            ln_vector_pass(v, tiles, bSC + oSC_C + 20,
                                           bVE + oVE_C + 1, gated=True)
                            for j in range(4):
                                t = q * 4 + j
                                v.tensor_add(xt(t), xt(t),
                                             dec_s[:, j * 1024:(j + 1) * 1024])
                            v.drain()

                        @blk.sync
                        def _(sy):
                            sy.wait_ge(sVE.h, bVE + oVE_C + 1)
                            for tl in range(4):
                                ins = sy.dma_start_transpose(
                                    mfm[:, tl * 128:(tl + 1) * 128],
                                    moe_bf[:, tl * 128:(tl + 1) * 128])
                                sD[3].inc(ins, 16)

                with nc.Block("l1f_ln") as blk:
                    bSC, bVE = sSC.v, sVE.v

                    @blk.vector
                    def _(v):
                        ln_vector_pass(v, [(t, xt(t)) for t in range(NT)], bSC, bVE)

                    @blk.scalar
                    def _(s):
                        ln_scalar_pass(s, [(t, xt(t)) for t in range(NT)],
                                       disc[:], bSC, bVE)

        if "l0a" in stages:
            emit_l0a()
        if "l0f" in stages:
            emit_l0f()
        if "l1a" in stages:
            emit_l1a()
        if "l1f" in stages:
            emit_l1f()

        with nc.Block("epi") as blk:
            @blk.sync
            def _(sy):
                ins = sy.dma_start(
                    out=out.rearrange("(t p) d -> p t d", p=128),
                    in_=x_tm[:].rearrange("p (t d) -> p t d", d=D))
                sST.inc(ins, 16)
                sy.wait_ge(sST.h, sST.v)
    return nc


def _prep_inputs(inputs):
    f = np.float32
    g = {}
    g["w_qkv"] = np.ascontiguousarray(inputs["L0_attn_W_qkv"], f)
    g["b_qkv"] = np.ascontiguousarray(inputs["L0_attn_b_qkv"], f).reshape(1, -1)
    g["w_out"] = np.ascontiguousarray(inputs["L0_attn_W_out"], f)
    g["b_out"] = np.ascontiguousarray(inputs["L0_attn_b_out"], f).reshape(1, -1)
    g["w_router_t"] = np.ascontiguousarray(inputs["L0_ffn_w_router"].T, f)
    g["wg_t"] = np.ascontiguousarray(inputs["L0_ffn_w_gate"].transpose(0, 2, 1), f)
    g["wu_t"] = np.ascontiguousarray(inputs["L0_ffn_w_up"].transpose(0, 2, 1), f)
    g["wd_t"] = np.ascontiguousarray(inputs["L0_ffn_w_down"].transpose(0, 2, 1), f)
    g["ww_t"] = np.ascontiguousarray(inputs["L1_attn_W_write"].T, f)
    g["wr_t"] = np.ascontiguousarray(inputs["L1_attn_W_read"].T, f)
    g["decay"] = np.asarray(inputs["L1_attn_decay"], f).reshape(1, 1)
    g["log_alpha"] = np.asarray(inputs["L1_attn_log_alpha"], f).reshape(1, 1)
    g["enc_w"] = np.ascontiguousarray(inputs["L1_ffn_enc_w"], f)
    g["enc_b"] = np.ascontiguousarray(inputs["L1_ffn_enc_b"], f).reshape(1, -1)
    g["s2c_w"] = np.ascontiguousarray(inputs["L1_ffn_s2c_w"], f)
    g["s2c_b"] = np.ascontiguousarray(inputs["L1_ffn_s2c_b"], f).reshape(1, -1)
    g["r_w1"] = np.ascontiguousarray(inputs["L1_ffn_r_w1"], f)
    g["r_b1"] = np.ascontiguousarray(inputs["L1_ffn_r_b1"], f).reshape(1, -1)
    g["r_w2"] = np.ascontiguousarray(inputs["L1_ffn_r_w2"], f)
    g["r_b2"] = np.ascontiguousarray(inputs["L1_ffn_r_b2"], f).reshape(1, -1)
    g["e_w1"] = np.ascontiguousarray(inputs["L1_ffn_e_w1"], f)
    g["e_b1"] = np.ascontiguousarray(inputs["L1_ffn_e_b1"], f)
    g["e_w2"] = np.ascontiguousarray(inputs["L1_ffn_e_w2"], f)
    g["e_b2"] = np.ascontiguousarray(inputs["L1_ffn_e_b2"], f).reshape(1, -1)
    g["c2s_w"] = np.ascontiguousarray(inputs["L1_ffn_c2s_w"], f)
    g["c2s_b"] = np.ascontiguousarray(inputs["L1_ffn_c2s_b"], f).reshape(1, -1)
    g["dec_w"] = np.ascontiguousarray(inputs["L1_ffn_dec_w"], f)
    g["dec_b"] = np.ascontiguousarray(inputs["L1_ffn_dec_b"], f).reshape(1, -1)
    g["ln_g"] = np.ascontiguousarray(inputs["L1_ffn_ln_g"], f).reshape(1, -1)
    g["ln_b"] = np.ascontiguousarray(inputs["L1_ffn_ln_b"], f).reshape(1, -1)
    return g


def kernel(**inputs):
    x = np.ascontiguousarray(np.asarray(inputs["L0_attn_x"], np.float32))
    g = _prep_inputs(inputs)
    nc = build()
    in_maps = []
    for c in range(8):
        m = dict(g)
        m["x_in"] = np.ascontiguousarray(x[c])
        in_maps.append(m)
    res = run_bass_kernel_spmd(nc, in_maps, core_ids=list(range(8)))
    outs = np.stack([res.results[c]["out"] for c in range(8)], axis=0)
    return outs.astype(np.float32)
